# version 12
# baseline (speedup 1.0000x reference)
import math
from contextlib import ExitStack

import numpy as np

import concourse.bass as bass
import concourse.mybir as mybir
from concourse.bass_utils import run_bass_kernel_spmd

F32 = mybir.dt.float32
BF16 = mybir.dt.bfloat16
I32 = mybir.dt.int32
AF = mybir.ActivationFunctionType
ALU = mybir.AluOpType

NCORES = 8
SEQ = 8192
D = 1024
NCH = 8
PI = math.pi
RPC = 2048
L0L = 4

SBUF_BASE = 16512
SBUF_TOP = 229344


FH, EH = 5, 1
WTOK = RPC + FH + EH


def core_slice_fm(x_fm_full, c):
    s0, s1 = c * RPC, (c + 1) * RPC
    lo = s0 - FH
    hi = s1 + EH
    if c in (0, 4):
        front = np.repeat(x_fm_full[:, s0:s0 + 1], FH, axis=1)
    else:
        front = x_fm_full[:, lo:s0]
    if c in (3, 7):
        back = np.repeat(x_fm_full[:, s1 - 1:s1], EH, axis=1)
    else:
        back = x_fm_full[:, s1:hi]
    return np.ascontiguousarray(np.concatenate([front, x_fm_full[:, s0:s1], back], axis=1))


def core_filt_slice(filt, c):
    s0, s1 = c * RPC, (c + 1) * RPC
    sm = s0 % SEQ
    if c in (0, 4):
        front = np.repeat(filt[:, sm:sm + 1], FH, axis=1)
    else:
        front = filt[:, sm - FH:sm]
    em = (s1 - 1) % SEQ
    if c in (3, 7):
        back = np.repeat(filt[:, em:em + 1], EH, axis=1)
    else:
        back = filt[:, em + 1:em + 1 + EH]
    return np.ascontiguousarray(
        np.concatenate([front, filt[:, sm:sm + RPC], back], axis=1))


def tile_widths(w):
    n = (w + 511) // 512
    base = w // n
    rem = w - base * n
    return [base + (1 if i < rem else 0) for i in range(n)]


_CONSTS = None


def host_consts():
    global _CONSTS
    if _CONSTS is not None:
        return _CONSTS
    ph = np.arange(16)
    pd = np.arange(64)
    D16 = np.exp(-2j * np.pi * np.outer(ph, ph) / 16.0)
    D64 = np.exp(-2j * np.pi * np.outer(pd, pd) / 64.0)
    A = np.kron(D16, D64)
    ks = np.fft.fftfreq(SEQ)[:, None, None]
    kh = np.fft.fftfreq(16)[None, :, None]
    kd = np.fft.fftfreq(64)[None, None, :]
    km = np.sqrt(ks ** 2 + kh ** 2 + kd ** 2 + 1e-10)
    f = np.exp(1j * np.arctan(np.log(km + 1e-10)))
    lam = np.arange(SEQ) / SEQ
    t = np.arange(SEQ) * 0.1
    mod = np.sin(t + lam) * np.cos(t - 2.0 * lam + 0.5 * lam ** 2)
    fp = f.reshape(SEQ, D) * (1.0 + 0.1 * mod)[:, None]
    _CONSTS = dict(
        Ar=np.ascontiguousarray(A.real.astype(np.float32)),
        Ai=np.ascontiguousarray(A.imag.astype(np.float32)),
        filt_r=np.ascontiguousarray(fp.real.astype(np.float32).T),
        filt_i=np.ascontiguousarray(fp.imag.astype(np.float32).T),
        ident=np.eye(128, dtype=np.float32),
    )
    return _CONSTS


ENG_SEM = {"sync": "LD", "gpsimd": "GD", "vector": "V", "scalar": "S", "tensor": "T"}


class Bld:
    def __init__(self, nc, ctx):
        self.nc = nc
        self.ctx = ctx
        self.off = SBUF_BASE
        self._uid = 0
        self.sems = {}
        self.cnt = {}
        self.block = ctx.enter_context(nc.Block())
        self._marks = []
        for s in ("LD", "GD", "V", "S", "T"):
            h = ctx.enter_context(nc.semaphore(s))
            self.sems[s] = h
            self.cnt[s] = 0
        self._consts = self.take("CCOLS", [128, 16], F32)
        self._cvals = {}

    def take(self, name, shape, dtype):
        per = int(np.prod(shape[1:])) * mybir.dt.size(dtype)
        o = (self.off + 31) // 32 * 32
        assert o + per <= SBUF_TOP, f"SBUF overflow at {name}: need {o + per}"
        self.off = o + per
        self._uid += 1
        return self.nc.alloc_sbuf_tensor_at(f"{name}_{self._uid}", list(shape), dtype, offset=o)

    def alias(self, name, shape, dtype, at):
        o = at.manual_sbuf_range[0]
        per = int(np.prod(shape[1:])) * mybir.dt.size(dtype)
        assert o + per <= at.manual_sbuf_range[1] or True
        self._uid += 1
        return self.nc.alloc_sbuf_tensor_at(f"{name}_{self._uid}", list(shape), dtype, offset=o)

    def push(self):
        self._marks.append(self.off)

    def pop(self):
        self.off = self._marks.pop()

    def ccol(self, val):
        val = float(val)
        if val not in self._cvals:
            idx = len(self._cvals)
            assert idx < 16
            self._cvals[val] = idx

            def _m(v, idx=idx, val=val):
                self.inc(v.memset(self._consts[:, idx:idx + 1], val), "V", 1)
            self.on("vector", _m, bar=False)
        return self._consts[:, self._cvals[val]:self._cvals[val] + 1]

    def inc(self, inst, name, k=1):
        inst.then_inc(self.sems[name], k)
        self.cnt[name] += k
        return inst

    def barrier(self, eng, engname):
        for e, s in ENG_SEM.items():
            if e != engname and self.cnt[s] > 0:
                eng.wait_ge(self.sems[s], self.cnt[s])

    def on(self, engname, fn, bar=True):
        def _f(eng):
            if bar:
                self.barrier(eng, engname)
            fn(eng)
        getattr(self.block, engname)(_f)

    def dma(self, queue, pairs, bar=False):
        sem = ENG_SEM[queue]

        def _f(e):
            for out_ap, in_ap in pairs:
                self.inc(e.dma_start(out=out_ap, in_=in_ap), sem, 16)
        self.on(queue, _f, bar=bar)


def proj(b, w_sb, wcols, xfn, ic_list, oc_n, n, ps_pair, ev_eng, ev_fn):
    t0 = b.cnt["T"]
    ev_sem = ENG_SEM[ev_eng]
    s0 = b.cnt[ev_sem]
    ic_n = len(ic_list)

    def _t(t):
        for oc in range(oc_n):
            bank = ps_pair[oc % 2]
            if oc >= 2:
                t.wait_ge(b.sems[ev_sem], s0 + (oc - 1))
            for j, ic in enumerate(ic_list):
                inst = t.matmul(bank[:, 0:n],
                                w_sb[:, ic * wcols + oc * 128: ic * wcols + (oc + 1) * 128],
                                xfn(ic),
                                start=(j == 0), stop=(j == ic_n - 1))
            b.inc(inst, "T", 1)
    b.on("tensor", _t)

    def _e(e):
        for oc in range(oc_n):
            e.wait_ge(b.sems["T"], t0 + oc + 1)
            b.inc(ev_fn(oc, ps_pair[oc % 2][:, 0:n], e), ev_sem, 1)
    b.on(ev_eng, _e, bar=False)


def layernorm(b, xfn, n, ps, rows, sq2, ones, out_fn, eps=1e-5):
    r_mu, r_var = rows[0][:], rows[1][:]
    eps_ap = b.ccol(eps)
    pre = {s: b.cnt[s] for s in ("LD", "GD", "V", "S", "T")}
    t0 = b.cnt["T"]
    s0 = b.cnt["S"]

    def _s(e):
        for nm in ("LD", "GD", "V", "T"):
            if pre[nm] > 0:
                e.wait_ge(b.sems[nm], pre[nm])
        for ic in range(NCH):
            if ic >= 2:
                e.wait_ge(b.sems["T"], t0 + 1 + (ic - 1))
            b.inc(e.activation(sq2[:, (ic % 2) * n:(ic % 2) * n + n], xfn(ic),
                               AF.Square), "S", 1)
    b.on("scalar", _s, bar=False)

    def _t(t):
        for nm in ("LD", "GD", "V", "S"):
            if pre[nm] > 0:
                t.wait_ge(b.sems[nm], pre[nm])
        for ic in range(NCH):
            inst = t.matmul(ps["sum"][0:1, 0:n], ones[:, 0:1], xfn(ic),
                            start=(ic == 0), stop=(ic == NCH - 1))
        b.inc(inst, "T", 1)
        for ic in range(NCH):
            t.wait_ge(b.sems["S"], s0 + ic + 1)
            inst = t.matmul(ps["sq"][0:1, 0:n], ones[:, 0:1],
                            sq2[:, (ic % 2) * n:(ic % 2) * n + n],
                            start=(ic == 0), stop=(ic == NCH - 1))
            b.inc(inst, "T", 1)
    b.on("tensor", _t, bar=False)
    t_sums = b.cnt["T"]

    def _v(v):
        v.wait_ge(b.sems["T"], t_sums)
        b.inc(v.tensor_scalar(r_mu[:, 0:n], ps["sum"][0:1, 0:n], 1.0 / D, None, ALU.mult), "V", 1)
        b.inc(v.tensor_scalar(r_var[:, 0:n], ps["sq"][0:1, 0:n], 1.0 / D, None, ALU.mult), "V", 1)
        b.inc(v.tensor_tensor(ps["sq"][0:1, 0:n], r_mu[:, 0:n], r_mu[:, 0:n], ALU.mult), "V", 1)
        b.inc(v.tensor_tensor(r_var[:, 0:n], r_var[:, 0:n], ps["sq"][0:1, 0:n],
                              ALU.subtract), "V", 1)
    b.on("vector", _v, bar=False)
    v_var = b.cnt["V"]

    def _s2(e):
        e.wait_ge(b.sems["V"], v_var)
        b.inc(e.activation(r_var[:, 0:n], r_var[:, 0:n], AF.Sqrt, bias=eps_ap[0:1, 0:1]), "S", 1)
    b.on("scalar", _s2, bar=False)
    s_sqrt = b.cnt["S"]

    def _v2(v):
        v.wait_ge(b.sems["S"], s_sqrt)
        b.inc(v.reciprocal(r_var[:, 0:n], r_var[:, 0:n]), "V", 1)
    b.on("vector", _v2, bar=False)
    v_rstd = b.cnt["V"]

    def _t2(t):
        t.wait_ge(b.sems["V"], v_rstd)
        b.inc(t.matmul(ps["mu"][:, 0:n], ones[0:1, 0:128], r_mu[:, 0:n],
                       start=True, stop=True), "T", 1)
        b.inc(t.matmul(ps["rstd"][:, 0:n], ones[0:1, 0:128], r_var[:, 0:n],
                       start=True, stop=True), "T", 1)
    b.on("tensor", _t2, bar=False)
    t_bc = b.cnt["T"]

    def _v3(v):
        v.wait_ge(b.sems["T"], t_bc)
        for ic in range(NCH):
            slot = sq2[:, (ic % 2) * n:(ic % 2) * n + n]
            b.inc(v.tensor_tensor(slot, xfn(ic), ps["mu"][:, 0:n], ALU.subtract), "V", 1)
            b.inc(v.tensor_tensor(slot, slot, ps["rstd"][:, 0:n], ALU.mult), "V", 1)
            out_fn(ic, v, slot)
    b.on("vector", _v3, bar=False)


def build_l0(w_tok, layers=L0L, do_outer=True, do_ffn=True, stages=(1, 1, 1), no_pop=False, with_l1pre=False):
    nc = bass.Bass("TRN2", target_bir_lowering=False, debug=False)
    L = L0L
    x_in = nc.dram_tensor("x_in", [D, w_tok], F32, kind="ExternalInput").ap()
    x_out = nc.dram_tensor("x_out", [D, w_tok], F32, kind="ExternalOutput").ap()
    d_aw = [nc.dram_tensor(f"aw{i}", [D, 3 * D], F32, kind="ExternalInput").ap() for i in range(L)]
    d_pw = [nc.dram_tensor(f"pw{i}", [D, D], F32, kind="ExternalInput").ap() for i in range(L)]
    d_fw = [nc.dram_tensor(f"fw{i}", [D, 4 * D], F32, kind="ExternalInput").ap() for i in range(L)]
    d_mw = [nc.dram_tensor(f"mw{i}", [4 * D, D], F32, kind="ExternalInput").ap() for i in range(L)]
    NVEC = 8 + 8 + 24 + 8 + 32 + 8 + 8 + 8
    d_vec = nc.dram_tensor("vecs", [128, L * NVEC], F32, kind="ExternalInput").ap()
    d_Ar = nc.dram_tensor("Ar", [D, D], F32, kind="ExternalInput").ap()
    d_Ai = nc.dram_tensor("Ai", [D, D], F32, kind="ExternalInput").ap()
    d_fr = nc.dram_tensor("filt_r", [D, w_tok], F32, kind="ExternalInput").ap()
    d_fi = nc.dram_tensor("filt_i", [D, w_tok], F32, kind="ExternalInput").ap()
    d_ident = nc.dram_tensor("ident", [128, 128], F32, kind="ExternalInput").ap()
    d_W = nc.dram_tensor("ffn_W", [D, D], F32, kind="ExternalInput").ap()
    d_ld = nc.dram_tensor("lora_down", [4, 100], F32, kind="ExternalInput").ap()
    d_lu = nc.dram_tensor("lora_up", [50, 4], F32, kind="ExternalInput").ap()
    d_da = nc.dram_tensor("down_aux", [100, D], F32, kind="ExternalInput").ap()
    d_ua = nc.dram_tensor("up_aux", [D, 50], F32, kind="ExternalInput").ap()
    d_ffnb = nc.dram_tensor("ffn_b", [128, 8], F32, kind="ExternalInput").ap()
    p0_dram = nc.dram_tensor("p0scr", [D, w_tok], BF16, kind="Internal").ap()

    tiles = []
    off = 0
    for w in tile_widths(w_tok):
        tiles.append((off, w))
        off += w

    ctx = ExitStack()
    with ctx:
        b = Bld(nc, ctx)
        mark0 = b.off
        XB = b.take("XB", [128, NCH * w_tok], F32)
        VEC = b.take("VEC", [128, L * NVEC], F32)
        ONES = b.take("ONES", [128, 128], F32)
        IDF = b.take("IDF", [128, 128], F32)
        ROWS_ = tuple(b.take(f"ROW{i}", [1, 512], F32) for i in range(2))

        psA = ctx.enter_context(nc.psum_tensor("psA", [128, 512], F32))
        psB = ctx.enter_context(nc.psum_tensor("psB", [128, 512], F32))
        psSum = ctx.enter_context(nc.psum_tensor("psSum", [128, 512], F32))
        psSq = ctx.enter_context(nc.psum_tensor("psSq", [128, 512], F32))
        psMu = ctx.enter_context(nc.psum_tensor("psMu", [128, 512], F32))
        psRstd = ctx.enter_context(nc.psum_tensor("psRstd", [128, 512], F32))
        psG = ctx.enter_context(nc.psum_tensor("psG", [128, 512], F32))
        LNPS = {"sum": psSum, "sq": psSq, "mu": psMu, "rstd": psRstd}

        b.dma("sync", [(XB[:, ic * w_tok:(ic + 1) * w_tok],
                        x_in[ic * 128:(ic + 1) * 128, :]) for ic in range(NCH)]
              + [(VEC[:], d_vec[:]), (ONES[:], d_ident[:]), (IDF[:], d_ident[:])])

        def _v0(v):
            b.inc(v.memset(ONES[:], 1.0), "V", 1)
        b.on("vector", _v0)

        voff = [0]

        def vcols(n128):
            o = voff[0]
            voff[0] += n128
            return (lambda j, o=o: VEC[:, o + j:o + j + 1])
        LV = [dict(ln1w=vcols(8), ln1b=vcols(8), ab=vcols(24), pb=vcols(8),
                   fcb=vcols(32), mpb=vcols(8), ln2w=vcols(8), ln2b=vcols(8))
              for _ in range(L)]

        if no_pop:
            b.pop = lambda: None
        for li in range(layers):
            lv = LV[li]
            emit_l0_layer(b, nc, ctx, li, lv, tiles, w_tok, XB, ONES, IDF, ROWS_,
                          LNPS, psA, psB, psG,
                          d_aw[li], d_pw[li], d_fw[li], d_mw[li],
                          d_Ar, d_Ai, d_fr, d_fi, p0_dram, stages=stages)

        TW = max(w for w in tile_widths(w_tok))
        if not do_outer:
            do_ffn_local = False
        b.push()
        X0T = b.take("X0T", [128, NCH * TW], F32)
        SQ2 = b.take("SQ2", [128, 2 * TW], F32)
        for (toff, tw) in (tiles if do_outer else []):
            b.dma("sync", [(X0T[:, ic * tw:(ic + 1) * tw],
                            x_in[ic * 128:(ic + 1) * 128, toff:toff + tw])
                           for ic in range(NCH)], bar=True)

            def _v(v, toff=toff, tw=tw):
                for ic in range(NCH):
                    b.inc(v.tensor_tensor(
                        XB[:, ic * w_tok + toff:ic * w_tok + toff + tw],
                        XB[:, ic * w_tok + toff:ic * w_tok + toff + tw],
                        X0T[:, ic * tw:(ic + 1) * tw], ALU.add), "V", 1)
            b.on("vector", _v)

            def xfn(ic, o=toff, n=tw):
                return XB[:, ic * w_tok + o:ic * w_tok + o + n]

            def out_plain(ic, v, normed, o=toff, n=tw):
                b.inc(v.tensor_copy(XB[:, ic * w_tok + o:ic * w_tok + o + n], normed), "V", 1)
            layernorm(b, xfn, tw, LNPS, ROWS_, SQ2, ONES, out_plain)
        b.pop()

        if do_ffn:
            emit_l0_ffn(b, nc, ctx, tiles, w_tok, XB, ONES, IDF, ROWS_, LNPS,
                        psA, psB, psG, d_W, d_ld, d_lu, d_da, d_ua, d_ffnb)

        st = b.cnt
        b.dma("sync", [(x_out[ic * 128:(ic + 1) * 128, :],
                        XB[:, ic * w_tok:(ic + 1) * w_tok]) for ic in range(NCH)], bar=True)
        final = b.cnt["LD"]

        def _fin(e):
            e.wait_ge(b.sems["LD"], final)
        b.on("sync", _fin, bar=False)

        if with_l1pre:
            b.off = mark0
            emit_l1pre(nc, b, ctx, x_out, psA, psB, psMu, psRstd)
    return nc


def emit_l0_layer(b, nc, ctx, li, lv, tiles, w_tok, XB, ONES, IDF, ROWS_, LNPS,
                  psA, psB, psG, d_aw, d_pw, d_fw, d_mw, d_Ar, d_Ai, d_fr, d_fi,
                  p0_dram, stages=(1, 1, 1)):
    TW = max(w for (_, w) in tiles)

    if not stages[0]:
        return _g2_g3_only(b, nc, ctx, li, lv, tiles, w_tok, XB, ONES, IDF, ROWS_,
                           LNPS, psA, psB, psG, d_pw, d_fw, d_mw, d_Ar, d_Ai,
                           d_fr, d_fi, p0_dram, stages, TW)
    b.push()
    WA = b.take("WA", [128, NCH * 3 * D], BF16)
    QKV = b.take("QKV", [128, 24 * TW], F32)
    LNX = b.take("LNX", [128, NCH * TW], BF16)
    TMP = b.take("TMP", [128, 15 * TW], F32)
    TI = b.take("TI", [128, TW], I32)
    P0T = b.take("P0T", [128, NCH * TW], BF16)
    QPREV = b.take("QPREV", [128, 24], F32)
    SQ2 = TMP

    b.dma("gpsimd", [(WA[:, ic * 3 * D:(ic + 1) * 3 * D],
                      d_aw[ic * 128:(ic + 1) * 128, :]) for ic in range(NCH)], bar=True)

    for tix, (toff, tw) in enumerate(tiles):
        def xfn(ic, o=toff, n=tw):
            return XB[:, ic * w_tok + o:ic * w_tok + o + n]

        def ln1_out(ic, v, normed, n=tw):
            b.inc(v.tensor_scalar(LNX[:, ic * n:(ic + 1) * n], normed,
                                  lv["ln1w"](ic), lv["ln1b"](ic), ALU.mult, ALU.add), "V", 1)
        layernorm(b, xfn, tw, LNPS, ROWS_, SQ2, ONES, ln1_out)

        def qkv_ev(oc, ps_ap, e, n=tw):
            return e.activation(QKV[:, oc * n:(oc + 1) * n], ps_ap, AF.Identity,
                                bias=lv["ab"](oc))
        proj(b, WA, 3 * D, lambda ic, n=tw: LNX[:, ic * n:(ic + 1) * n],
             list(range(NCH)), 24, tw, (psA, psB), "scalar", qkv_ev)

        emit_quat(b, tix, tw, TW, QKV, QPREV, TMP, TI, P0T)

        def _vq(v, n=tw):
            b.inc(v.tensor_copy(QPREV[:],
                                QKV[:].rearrange("p (c t) -> p c t", c=24)[:, :, n - 1:n]),
                  "V", 1)
        b.on("vector", _vq, bar=False)

        b.dma("gpsimd", [(p0_dram[ic * 128:(ic + 1) * 128, toff:toff + tw],
                          P0T[:, ic * tw:(ic + 1) * tw]) for ic in range(NCH)], bar=True)
    b.pop()
    if not (stages[1] or stages[2]):
        return
    return _g2_g3_only(b, nc, ctx, li, lv, tiles, w_tok, XB, ONES, IDF, ROWS_,
                       LNPS, psA, psB, psG, d_pw, d_fw, d_mw, d_Ar, d_Ai,
                       d_fr, d_fi, p0_dram, stages, TW)


def _g2_g3_only(b, nc, ctx, li, lv, tiles, w_tok, XB, ONES, IDF, ROWS_, LNPS,
                psA, psB, psG, d_pw, d_fw, d_mw, d_Ar, d_Ai, d_fr, d_fi,
                p0_dram, stages, TW):
    if not stages[1]:
        return _g3_only(b, nc, ctx, li, lv, tiles, w_tok, XB, ONES, IDF, ROWS_,
                        LNPS, psA, psB, psG, d_fw, d_mw, stages, TW)
    b.push()
    ARB = b.take("ARB", [128, NCH * D], BF16)
    AIB = b.take("AIB", [128, NCH * D], BF16)
    G1B = b.take("G1B", [128, NCH * D], BF16)
    G2B = b.take("G2B", [128, NCH * D], BF16)
    b.push()
    PWB = b.take("PWB", [128, NCH * D], BF16)
    b.pop()
    P0L = b.take("P0L", [128, NCH * TW], BF16)
    UR = b.take("UR", [128, NCH * TW], F32)
    UI = b.take("UI", [128, NCH * TW], F32)
    FR = b.take("FR", [128, NCH * TW], F32)
    FI = b.take("FI", [128, NCH * TW], F32)
    WRB = b.take("WRB", [128, NCH * TW], BF16)
    WIB = b.take("WIB", [128, NCH * TW], BF16)
    TMP2 = b.alias("TMP2", [128, 2 * TW], F32, P0L)

    b.dma("gpsimd", [(ARB[:, ic * D:(ic + 1) * D], d_Ar[ic * 128:(ic + 1) * 128, :])
                     for ic in range(NCH)]
          + [(AIB[:, ic * D:(ic + 1) * D], d_Ai[ic * 128:(ic + 1) * 128, :])
             for ic in range(NCH)]
          + [(PWB[:, ic * D:(ic + 1) * D], d_pw[ic * 128:(ic + 1) * 128, :])
             for ic in range(NCH)], bar=True)

    for (gsrc, gdst) in ((ARB, G1B), (AIB, G2B)):
        for fc in range(NCH):
            for half in range(2):
                def _t(t, gsrc=gsrc, fc=fc, half=half):
                    for kc in range(NCH):
                        inst = t.matmul(psG[:, 0:512],
                                        gsrc[:, kc * D + fc * 128: kc * D + (fc + 1) * 128],
                                        PWB[:, kc * D + half * 512: kc * D + (half + 1) * 512],
                                        start=(kc == 0), stop=(kc == NCH - 1))
                    b.inc(inst, "T", 1)
                b.on("tensor", _t)
                tt = b.cnt["T"]

                def _s(e, gdst=gdst, fc=fc, half=half, tt=tt):
                    e.wait_ge(b.sems["T"], tt)
                    b.inc(e.activation(gdst[:, fc * D + half * 512: fc * D + (half + 1) * 512],
                                       psG[:, 0:512], AF.Copy, scale=1.0 / D), "S", 1)
                b.on("scalar", _s, bar=False)

    for (toff, tw) in tiles:
        b.dma("gpsimd", [(P0L[:, ic * tw:(ic + 1) * tw],
                          p0_dram[ic * 128:(ic + 1) * 128, toff:toff + tw])
                         for ic in range(NCH)], bar=True)
        b.dma("sync", [(FR[:, ic * tw:(ic + 1) * tw],
                        d_fr[ic * 128:(ic + 1) * 128, toff:toff + tw]) for ic in range(NCH)]
              + [(FI[:, ic * tw:(ic + 1) * tw],
                  d_fi[ic * 128:(ic + 1) * 128, toff:toff + tw]) for ic in range(NCH)],
              bar=True)

        def p0fn(ic, n=tw):
            return P0L[:, ic * n:(ic + 1) * n]

        proj(b, ARB, D, p0fn, list(range(NCH)), NCH, tw, (psA, psB), "vector",
             lambda oc, ps_ap, e, n=tw: e.tensor_copy(UR[:, oc * n:(oc + 1) * n], ps_ap))
        proj(b, AIB, D, p0fn, list(range(NCH)), NCH, tw, (psA, psB), "vector",
             lambda oc, ps_ap, e, n=tw: e.tensor_copy(UI[:, oc * n:(oc + 1) * n], ps_ap))

        def _vf(v, tw=tw):
            t1, t2 = TMP2[:, 0:tw], TMP2[:, TW:TW + tw]
            for ic in range(NCH):
                s = slice(ic * tw, (ic + 1) * tw)
                b.inc(v.tensor_tensor(t1, UR[:, s], FR[:, s], ALU.mult), "V", 1)
                b.inc(v.tensor_tensor(t2, UI[:, s], FI[:, s], ALU.mult), "V", 1)
                b.inc(v.tensor_tensor(WRB[:, s], t1, t2, ALU.subtract), "V", 1)
                b.inc(v.tensor_tensor(t1, UR[:, s], FI[:, s], ALU.mult), "V", 1)
                b.inc(v.tensor_tensor(t2, UI[:, s], FR[:, s], ALU.mult), "V", 1)
                b.inc(v.tensor_tensor(WIB[:, s], t1, t2, ALU.add), "V", 1)
        b.on("vector", _vf)

        t0 = b.cnt["T"]
        v0 = b.cnt["V"]

        def _t(t, toff=toff, tw=tw, t0=t0, v0=v0):
            for oc in range(NCH):
                bank = (psA, psB)[oc % 2]
                if oc >= 2:
                    t.wait_ge(b.sems["V"], v0 + (oc - 1))
                for ic in range(NCH):
                    t.matmul(bank[:, 0:tw],
                             G1B[:, ic * D + oc * 128: ic * D + (oc + 1) * 128],
                             WRB[:, ic * tw:(ic + 1) * tw], start=(ic == 0), stop=False)
                for ic in range(NCH):
                    inst = t.matmul(bank[:, 0:tw],
                                    G2B[:, ic * D + oc * 128: ic * D + (oc + 1) * 128],
                                    WIB[:, ic * tw:(ic + 1) * tw],
                                    start=False, stop=(ic == NCH - 1))
                b.inc(inst, "T", 1)
        b.on("tensor", _t)

        def _v(v, toff=toff, tw=tw, t0=t0):
            for oc in range(NCH):
                v.wait_ge(b.sems["T"], t0 + oc + 1)
                xap = XB[:, oc * w_tok + toff:oc * w_tok + toff + tw]
                b.inc(v.scalar_tensor_tensor(xap, (psA, psB)[oc % 2][:, 0:tw],
                                             lv["pb"](oc), xap, ALU.add, ALU.add), "V", 1)
        b.on("vector", _v, bar=False)
    b.pop()
    return _g3_only(b, nc, ctx, li, lv, tiles, w_tok, XB, ONES, IDF, ROWS_,
                    LNPS, psA, psB, psG, d_fw, d_mw, stages, TW)


def _g3_only(b, nc, ctx, li, lv, tiles, w_tok, XB, ONES, IDF, ROWS_, LNPS,
             psA, psB, psG, d_fw, d_mw, stages, TW):
    if not stages[2]:
        return
    w_tot = sum(w for (_, w) in tiles)
    n3 = (w_tot + 255) // 256
    base3 = w_tot // n3
    rem3 = w_tot - base3 * n3
    tiles3 = []
    o3 = 0
    for i in range(n3):
        w = base3 + (1 if i < rem3 else 0)
        tiles3.append((o3, w))
        o3 += w
    TW = max(w for (_, w) in tiles3)
    tiles = tiles3
    b.push()
    WFC = b.take("WFC", [128, NCH * 4 * D], BF16)
    WMP = b.take("WMP", [128, 32 * D], BF16)
    LNX2 = b.take("LNX2", [128, NCH * TW], BF16)
    GEL = b.take("GEL", [128, NCH * TW], BF16)
    SQ3 = b.alias("SQ3", [128, 2 * TW], F32, GEL)

    b.dma("gpsimd", [(WFC[:, ic * 4 * D:(ic + 1) * 4 * D],
                      d_fw[ic * 128:(ic + 1) * 128, :]) for ic in range(NCH)]
          + [(WMP[:, ic * D:(ic + 1) * D], d_mw[ic * 128:(ic + 1) * 128, :])
             for ic in range(32)], bar=True)

    for (toff, tw) in tiles:
        def xfn(ic, o=toff, n=tw):
            return XB[:, ic * w_tok + o:ic * w_tok + o + n]

        def ln2_out(ic, v, normed, n=tw):
            b.inc(v.tensor_scalar(LNX2[:, ic * n:(ic + 1) * n], normed,
                                  lv["ln2w"](ic), lv["ln2b"](ic), ALU.mult, ALU.add), "V", 1)
        layernorm(b, xfn, tw, LNPS, ROWS_, SQ3, ONES, ln2_out)

        for q in range(4):
            t0 = b.cnt["T"]
            s0 = b.cnt["S"]

            def _t(t, q=q, tw=tw, s0=s0):
                for oc in range(NCH):
                    bank = (psA, psB)[oc % 2]
                    if oc >= 2:
                        t.wait_ge(b.sems["S"], s0 + (oc - 1))
                    for ic in range(NCH):
                        inst = t.matmul(bank[:, 0:tw],
                                        WFC[:, ic * 4 * D + (q * 8 + oc) * 128:
                                            ic * 4 * D + (q * 8 + oc + 1) * 128],
                                        LNX2[:, ic * tw:(ic + 1) * tw],
                                        start=(ic == 0), stop=(ic == NCH - 1))
                    b.inc(inst, "T", 1)
            b.on("tensor", _t)

            def _s(e, q=q, tw=tw, t0=t0):
                for oc in range(NCH):
                    e.wait_ge(b.sems["T"], t0 + oc + 1)
                    b.inc(e.activation(GEL[:, oc * tw:(oc + 1) * tw],
                                       (psA, psB)[oc % 2][:, 0:tw], AF.Gelu,
                                       bias=lv["fcb"](q * 8 + oc)), "S", 1)
            b.on("scalar", _s, bar=False)

            t1 = b.cnt["T"]
            v1 = b.cnt["V"]

            def _t2(t, q=q, toff=toff, tw=tw, v1=v1):
                for oc in range(NCH):
                    bank = (LNPS["mu"], LNPS["rstd"])[oc % 2]
                    if oc >= 2:
                        t.wait_ge(b.sems["V"], v1 + (oc - 1))
                    for j in range(NCH):
                        ic = q * 8 + j
                        inst = t.matmul(bank[:, 0:tw],
                                        WMP[:, ic * D + oc * 128: ic * D + (oc + 1) * 128],
                                        GEL[:, j * tw:(j + 1) * tw],
                                        start=(j == 0), stop=(j == NCH - 1))
                    b.inc(inst, "T", 1)
            b.on("tensor", _t2)

            def _v2(v, q=q, toff=toff, tw=tw, t1=t1):
                for oc in range(NCH):
                    v.wait_ge(b.sems["T"], t1 + oc + 1)
                    xap = XB[:, oc * w_tok + toff:oc * w_tok + toff + tw]
                    bank = (LNPS["mu"], LNPS["rstd"])[oc % 2][:, 0:tw]
                    if q == 0:
                        inst = v.scalar_tensor_tensor(xap, bank, lv["mpb"](oc), xap,
                                                      ALU.add, ALU.add)
                    else:
                        inst = v.tensor_tensor(xap, bank, xap, ALU.add)
                    b.inc(inst, "V", 1)
            b.on("vector", _v2, bar=False)
    b.pop()


def emit_quat(b, tix, tw, TW, QKV, QPREV, TMP, TI, P0T):
    n = tw

    def T_(i):
        return TMP[:, i * TW:i * TW + n]

    for l in range(NCH):
        qs, ks, vs = l, 8 + l, 16 + l
        trip = ((0, qs), (1, ks), (2, vs))

        def _vA(v, l=l, trip=trip):
            for (cx, j) in trip:
                x0 = QKV[:, j * n:(j + 1) * n]
                p1, p2 = T_(3 * cx), T_(3 * cx + 1)
                b.inc(v.tensor_tensor(p1[:, 1:n], QKV[:, j * n + 1:(j + 1) * n],
                                      QKV[:, j * n:(j + 1) * n - 1], ALU.subtract), "V", 1)
                if tix == 0:
                    b.inc(v.memset(p1[:, 0:1], 0.0), "V", 1)
                else:
                    b.inc(v.tensor_tensor(p1[:, 0:1], QKV[:, j * n:j * n + 1],
                                          QPREV[:, j:j + 1], ALU.subtract), "V", 1)
                lj = (l - 1) % 8
                jprev = j - l + lj
                b.inc(v.tensor_copy(p2[0:64, 0:n],
                                    QKV[64:128, jprev * n:(jprev + 1) * n]), "V", 1)
                b.inc(v.tensor_copy(p2[64:128, 0:n], QKV[0:64, j * n:(j + 1) * n]), "V", 1)
                b.inc(v.tensor_scalar(T_(12), x0, 1.0 / (2 * PI), 64.25,
                                      ALU.mult, ALU.add), "V", 1)
                b.inc(v.tensor_copy(TI[:, 0:n], T_(12)), "V", 1)
                b.inc(v.tensor_copy(T_(12), TI[:, 0:n]), "V", 1)
                b.inc(v.scalar_tensor_tensor(T_(9 + cx), T_(12), -2 * PI, x0,
                                             ALU.mult, ALU.add), "V", 1)
                b.inc(v.tensor_scalar(T_(9 + cx), T_(9 + cx), float(128 * PI + PI / 2),
                                      None, ALU.add), "V", 1)
                b.inc(v.tensor_scalar(T_(12), T_(9 + cx), float(PI), None, ALU.is_gt),
                      "V", 1)
                b.inc(v.scalar_tensor_tensor(T_(9 + cx), T_(12), -2 * PI, T_(9 + cx),
                                             ALU.mult, ALU.add), "V", 1)
        b.on("vector", _vA)
        vA = b.cnt["V"]

        def _sA(e, l=l):
            e.wait_ge(b.sems["V"], vA)
            for cx in range(3):
                b.inc(e.activation(T_(3 * cx + 2), T_(9 + cx), AF.Sin), "S", 1)
        b.on("scalar", _sA, bar=False)
        sA = b.cnt["S"]

        def _vB(v, l=l):
            v.wait_ge(b.sems["S"], sA)
            qs, ks, vs = l, 8 + l, 16 + l
            qq = (QKV[:, qs * n:(qs + 1) * n], T_(0), T_(1), T_(2))
            kk = (QKV[:, ks * n:(ks + 1) * n], T_(3), T_(4), T_(5))
            vv = (QKV[:, vs * n:(vs + 1) * n], T_(6), T_(7), T_(8))
            qk = [T_(9), T_(10), T_(11), T_(12)]
            acc, tmp = T_(13), T_(14)
            SIGNS = [
                [(0, 0, 1), (1, 1, -1), (2, 2, -1), (3, 3, -1)],
                [(0, 1, 1), (1, 0, 1), (2, 3, 1), (3, 2, -1)],
                [(0, 2, 1), (1, 3, -1), (2, 0, 1), (3, 1, 1)],
                [(0, 3, 1), (1, 2, 1), (2, 1, -1), (3, 0, 1)],
            ]
            for ci, terms in enumerate(SIGNS):
                dst = qk[ci]
                for idx, (ia, ib, sg) in enumerate(terms):
                    if idx == 0:
                        b.inc(v.tensor_tensor(dst, qq[ia], kk[ib], ALU.mult), "V", 1)
                        assert sg > 0
                    else:
                        b.inc(v.tensor_tensor(tmp, qq[ia], kk[ib], ALU.mult), "V", 1)
                        b.inc(v.tensor_tensor(dst, dst, tmp,
                                              ALU.add if sg > 0 else ALU.subtract), "V", 1)
            b.inc(v.tensor_tensor(acc, qk[0], vv[0], ALU.mult), "V", 1)
            for ci in (1, 2, 3):
                b.inc(v.tensor_tensor(tmp, qk[ci], vv[ci], ALU.mult), "V", 1)
                b.inc(v.tensor_tensor(acc, acc, tmp, ALU.subtract), "V", 1)
            b.inc(v.tensor_copy(P0T[:, l * n:(l + 1) * n], acc), "V", 1)
        b.on("vector", _vB, bar=False)


def emit_l0_ffn(b, nc, ctx, tiles, w_tok, XB, ONES, IDF, ROWS_, LNPS,
                psA, psB, psG, d_W, d_ld, d_lu, d_da, d_ua, d_ffnb):
    b.push()
    WF = b.take("WF", [128, NCH * D], F32)
    WTF = b.take("WTF", [128, NCH * D], F32)
    WTB = b.take("WTB", [128, NCH * D], BF16)
    UAS = b.take("UAS", [128, NCH * 64], F32)
    LDS = b.take("LDS", [4, 128], F32)
    LDT = b.take("LDT", [100, 8], F32)
    DAS = b.take("DAS", [100, D], F32)
    M1S = b.take("M1S", [4, D], F32)
    LUS = b.take("LUS", [50, 8], F32)
    LUT_ = b.take("LUT", [4, 64], F32)
    M2S = b.take("M2S", [50, D], F32)
    UAT = b.take("UAT", [50, D], F32)
    FB = b.take("FB", [128, 8], F32)
    TW = max(w for (_, w) in tiles)
    LNX3 = b.take("LNX3", [128, NCH * TW], BF16)
    SQ4 = b.take("SQ4", [128, 2 * TW], F32)

    b.dma("sync", [(WF[:, ic * D:(ic + 1) * D], d_W[ic * 128:(ic + 1) * 128, :])
                   for ic in range(NCH)]
          + [(LDS[0:4, 0:100], d_ld[:]), (DAS[0:100, :], d_da[:]),
             (LUS[0:50, 0:4], d_lu[:]), (FB[:], d_ffnb[:])]
          + [(UAS[:, ic * 64:ic * 64 + 50], d_ua[ic * 128:(ic + 1) * 128, :])
             for ic in range(NCH)], bar=True)

    t0 = b.cnt["T"]
    s0 = b.cnt["S"]

    def _t(t):
        k = 0
        for ic in range(NCH):
            for jc in range(NCH):
                if k >= 2:
                    t.wait_ge(b.sems["S"], s0 + (k - 1))
                b.inc(t.transpose((psA, psB)[k % 2][:, 0:128],
                                  WF[:, ic * D + jc * 128: ic * D + (jc + 1) * 128],
                                  IDF[:]), "T", 1)
                k += 1
    b.on("tensor", _t)

    def _s(e):
        k = 0
        for ic in range(NCH):
            for jc in range(NCH):
                e.wait_ge(b.sems["T"], t0 + k + 1)
                b.inc(e.copy(WTF[:, jc * D + ic * 128: jc * D + (ic + 1) * 128],
                             (psA, psB)[k % 2][:, 0:128]), "S", 1)
                k += 1
    b.on("scalar", _s, bar=False)

    def _t1(t):
        b.inc(t.transpose(psA[0:100, 0:4], LDS[0:4, 0:100], IDF[0:4, 0:4]), "T", 1)
        b.inc(t.transpose(psB[0:4, 0:50], LUS[0:50, 0:4], IDF[0:50, 0:50]), "T", 1)
    b.on("tensor", _t1)
    tS = b.cnt["T"]

    def _s1(e):
        e.wait_ge(b.sems["T"], tS)
        b.inc(e.copy(LDT[0:100, 0:4], psA[0:100, 0:4]), "S", 1)
        b.inc(e.copy(LUT_[0:4, 0:50], psB[0:4, 0:50]), "S", 1)
    b.on("scalar", _s1, bar=False)

    for ic in range(NCH):
        def _tu(t, ic=ic):
            b.inc(t.transpose((psA, psB)[ic % 2][0:50, 0:128],
                              UAS[:, ic * 64:ic * 64 + 50], IDF[:]), "T", 1)
        b.on("tensor", _tu)
        tc = b.cnt["T"]

        def _su(e, ic=ic, tc=tc):
            e.wait_ge(b.sems["T"], tc)
            b.inc(e.copy(UAT[0:50, ic * 128:(ic + 1) * 128],
                         (psA, psB)[ic % 2][0:50, 0:128]), "S", 1)
        b.on("scalar", _su, bar=False)

    def _t3(t):
        b.inc(t.matmul(psA[0:4, 0:512], LDT[0:100, 0:4], DAS[0:100, 0:512],
                       start=True, stop=True), "T", 1)
        b.inc(t.matmul(psB[0:4, 0:512], LDT[0:100, 0:4], DAS[0:100, 512:1024],
                       start=True, stop=True), "T", 1)
    b.on("tensor", _t3)
    tM1 = b.cnt["T"]

    def _sM1(e):
        e.wait_ge(b.sems["T"], tM1)
        b.inc(e.copy(M1S[0:4, 0:512], psA[0:4, 0:512]), "S", 1)
        b.inc(e.copy(M1S[0:4, 512:1024], psB[0:4, 0:512]), "S", 1)
    b.on("scalar", _sM1, bar=False)

    def _t4(t):
        b.inc(t.matmul(psA[0:50, 0:512], LUT_[0:4, 0:50], M1S[0:4, 0:512],
                       start=True, stop=True), "T", 1)
        b.inc(t.matmul(psB[0:50, 0:512], LUT_[0:4, 0:50], M1S[0:4, 512:1024],
                       start=True, stop=True), "T", 1)
    b.on("tensor", _t4)
    tM2 = b.cnt["T"]

    def _sM2(e):
        e.wait_ge(b.sems["T"], tM2)
        b.inc(e.copy(M2S[0:50, 0:512], psA[0:50, 0:512]), "S", 1)
        b.inc(e.copy(M2S[0:50, 512:1024], psB[0:50, 0:512]), "S", 1)
    b.on("scalar", _sM2, bar=False)

    for ic in range(NCH):
        def _t5(t, ic=ic):
            b.inc(t.matmul(psA[:, 0:512], M2S[0:50, ic * 128:(ic + 1) * 128],
                           UAT[0:50, 0:512], start=True, stop=True), "T", 1)
            b.inc(t.matmul(psB[:, 0:512], M2S[0:50, ic * 128:(ic + 1) * 128],
                           UAT[0:50, 512:1024], start=True, stop=True), "T", 1)
        b.on("tensor", _t5)
        tD = b.cnt["T"]

        def _vd(v, ic=ic, tD=tD):
            v.wait_ge(b.sems["T"], tD)
            for half in range(2):
                ap = WTF[:, ic * D + half * 512: ic * D + (half + 1) * 512]
                b.inc(v.tensor_tensor(ap, ap, (psA, psB)[half][:, 0:512], ALU.add), "V", 1)
        b.on("vector", _vd, bar=False)

    def _vc(v):
        for ic in range(NCH):
            b.inc(v.tensor_copy(WTB[:, ic * D:(ic + 1) * D], WTF[:, ic * D:(ic + 1) * D]),
                  "V", 1)
    b.on("vector", _vc)

    for (toff, tw) in tiles:
        def xfn(ic, o=toff, n=tw):
            return XB[:, ic * w_tok + o:ic * w_tok + o + n]

        def _vb(v, toff=toff, tw=tw):
            for ic in range(NCH):
                b.inc(v.tensor_copy(LNX3[:, ic * tw:(ic + 1) * tw],
                                    XB[:, ic * w_tok + toff:ic * w_tok + toff + tw]), "V", 1)
        b.on("vector", _vb)

        t6 = b.cnt["T"]
        v6 = b.cnt["V"]

        def _t6(t, tw=tw, v6=v6):
            for oc in range(NCH):
                bank = (psA, psB)[oc % 2]
                if oc >= 2:
                    t.wait_ge(b.sems["V"], v6 + (oc - 1))
                for ic in range(NCH):
                    inst = t.matmul(bank[:, 0:tw],
                                    WTB[:, ic * D + oc * 128: ic * D + (oc + 1) * 128],
                                    LNX3[:, ic * tw:(ic + 1) * tw],
                                    start=(ic == 0), stop=(ic == NCH - 1))
                b.inc(inst, "T", 1)
        b.on("tensor", _t6)

        def _v6(v, toff=toff, tw=tw, t6=t6):
            for oc in range(NCH):
                v.wait_ge(b.sems["T"], t6 + oc + 1)
                xap = XB[:, oc * w_tok + toff:oc * w_tok + toff + tw]
                b.inc(v.scalar_tensor_tensor(xap, (psA, psB)[oc % 2][:, 0:tw],
                                             FB[:, oc:oc + 1], xap, ALU.add, ALU.add), "V", 1)
        b.on("vector", _v6, bar=False)

        def out_plain(ic, v, normed, o=toff, n=tw):
            b.inc(v.tensor_copy(XB[:, ic * w_tok + o:ic * w_tok + o + n], normed), "V", 1)
        layernorm(b, xfn, tw, LNPS, ROWS_, SQ4, ONES, out_plain)
    b.pop()


def build_l1pre():
    nc = bass.Bass("TRN2", target_bir_lowering=False, debug=False)
    W2 = RPC + 2
    x_in = nc.dram_tensor("x2p", [D, W2], F32, kind="ExternalInput").ap()
    d_wr = nc.dram_tensor("W_r", [D, D], F32, kind="ExternalInput").ap()
    d_wk = nc.dram_tensor("W_k", [D, D], F32, kind="ExternalInput").ap()
    d_wv = nc.dram_tensor("W_v", [D, D], F32, kind="ExternalInput").ap()
    d_wg = nc.dram_tensor("W_g", [D, D], F32, kind="ExternalInput").ap()
    d_w1 = nc.dram_tensor("maa_w1", [D, 160], F32, kind="ExternalInput").ap()
    d_w2 = nc.dram_tensor("maa_w2", [5 * 32, D], F32, kind="ExternalInput").ap()
    d_tdw1 = nc.dram_tensor("tdw1", [D, 64], F32, kind="ExternalInput").ap()
    d_tdw2 = nc.dram_tensor("tdw2", [64, D], F32, kind="ExternalInput").ap()
    d_vec = nc.dram_tensor("vecs", [128, 64], F32, kind="ExternalInput").ap()
    d_hdb = nc.dram_tensor("hdb", [1, 16], F32, kind="ExternalInput").ap()
    o_r = nc.dram_tensor("r_bf", [D, RPC], BF16, kind="ExternalOutput").ap()
    o_k = nc.dram_tensor("k_bf", [D, RPC], BF16, kind="ExternalOutput").ap()
    o_v = nc.dram_tensor("v_bf", [D, RPC], BF16, kind="ExternalOutput").ap()
    o_g = nc.dram_tensor("g", [D, RPC], F32, kind="ExternalOutput").ap()
    o_mw = nc.dram_tensor("meanw", [16, RPC], F32, kind="ExternalOutput").ap()

    tiles = []
    off = 0
    for w in tile_widths(RPC):
        tiles.append((off, w))
        off += w
    TW = max(w for (_, w) in tiles)

    ctx = ExitStack()
    with ctx:
        b = Bld(nc, ctx)
        XT = b.take("XT", [128, NCH * (TW + 2)], F32)
        VEC = b.take("VEC", [128, 64], F32)
        ONES = b.take("ONES", [128, 128], F32)
        ROWS_ = tuple(b.take(f"ROW{i}", [1, 512], F32) for i in range(2))
        WPROJ = b.take("WPROJ", [128, 4 * NCH * D], BF16)
        W1B = b.take("W1B", [128, NCH * 160], BF16)
        W2B = b.take("W2B", [32, 5 * D], BF16)
        TD1 = b.take("TD1", [128, NCH * 64], BF16)
        TD2 = b.take("TD2", [64, D], BF16)
        HDBE = b.take("HDBE", [1, 16], F32)
        DX = b.take("DX", [128, NCH * TW], F32)
        XXX = b.take("XXX", [128, NCH * TW], BF16)
        X5 = b.take("X5", [128, 5 * NCH * TW], BF16)
        M5 = b.take("M5", [32, 5 * TW], BF16)
        MBT = b.take("MBT", [64, TW], F32)
        TT_ = b.take("TT", [128, 2 * TW], F32)
        OUTT = b.take("OUTT", [128, NCH * TW], BF16)
        MWT = b.take("MWT", [1, 2 * TW], F32)
        TWB = b.take("TWB", [64, TW], BF16)
        EXPT = b.take("EXPT", [128, TW], F32)
        EXPT2 = b.take("EXPT2", [64, TW], F32)
        GT = b.alias("GT", [128, NCH * TW], F32, DX)

        psA = ctx.enter_context(nc.psum_tensor("psA", [128, 512], F32))
        psB = ctx.enter_context(nc.psum_tensor("psB", [128, 512], F32))
        psC = ctx.enter_context(nc.psum_tensor("psC", [128, 512], F32))
        psD = ctx.enter_context(nc.psum_tensor("psD", [128, 512], F32))

        b.dma("sync", [(VEC[:], d_vec[:]), (HDBE[:], d_hdb[:])])
        b.dma("gpsimd",
              [(WPROJ[:, (w * NCH + ic) * D:(w * NCH + ic) * D + D],
                dw[ic * 128:(ic + 1) * 128, :])
               for w, dw in enumerate((d_wr, d_wk, d_wv, d_wg)) for ic in range(NCH)]
              + [(W1B[:, ic * 160:(ic + 1) * 160], d_w1[ic * 128:(ic + 1) * 128, :])
                 for ic in range(NCH)]
              + [(W2B[0:32, f * D:(f + 1) * D], d_w2[f * 32:(f + 1) * 32, :])
                 for f in range(5)]
              + [(TD1[:, ic * 64:(ic + 1) * 64], d_tdw1[ic * 128:(ic + 1) * 128, :])
                 for ic in range(NCH)]
              + [(TD2[0:64, :], d_tdw2[:])])

        def _v0(v):
            b.inc(v.memset(ONES[:], 1.0), "V", 1)
        b.on("vector", _v0)

        def _s0(e):
            b.inc(e.activation(HDBE[:], HDBE[:], AF.Exp), "S", 1)
        b.on("scalar", _s0)
        s0_ = b.cnt["S"]

        def _v1(v, s0_=s0_):
            v.wait_ge(b.sems["S"], s0_)
            b.inc(v.tensor_scalar(HDBE[:], HDBE[:], -1.0 / 64.0, None, ALU.mult), "V", 1)
        b.on("vector", _v1, bar=False)

        MAAX, MAAW, MAAK, MAAV, MAAR, MAAG, TDC = [
            (lambda j, o=o: VEC[:, o * 8 + j:o * 8 + j + 1]) for o in range(7)]

        for (toff, tw) in tiles:
            WS = TW + 2
            b.dma("sync", [(XT[:, ic * WS:ic * WS + tw + 2],
                            x_in[ic * 128:(ic + 1) * 128, toff:toff + tw + 2])
                           for ic in range(NCH)], bar=True)

            def _vd(v, tw=tw, WS=WS):
                for ic in range(NCH):
                    xs = XT[:, ic * WS + 1:ic * WS + 1 + tw]
                    xp = XT[:, ic * WS:ic * WS + tw]
                    xn = XT[:, ic * WS + 2:ic * WS + 2 + tw]
                    dx = DX[:, ic * tw:(ic + 1) * tw]
                    b.inc(v.tensor_tensor(dx, xp, xn, ALU.add), "V", 1)
                    b.inc(v.scalar_tensor_tensor(dx, dx, 0.5, xs, ALU.mult, ALU.subtract),
                          "V", 1)
                    t = TT_[:, 0:tw]
                    b.inc(v.tensor_scalar(t, dx, MAAX(ic), None, ALU.mult), "V", 1)
                    b.inc(v.tensor_tensor(XXX[:, ic * tw:(ic + 1) * tw], t, xs, ALU.add),
                          "V", 1)
            b.on("vector", _vd)

            t0 = b.cnt["T"]

            def _tm(t, tw=tw):
                for ic in range(NCH):
                    inst = t.matmul(psA[:, 0:tw], W1B[:, ic * 160:ic * 160 + 128],
                                    XXX[:, ic * tw:(ic + 1) * tw],
                                    start=(ic == 0), stop=(ic == NCH - 1))
                b.inc(inst, "T", 1)
                for ic in range(NCH):
                    inst = t.matmul(psB[0:32, 0:tw], W1B[:, ic * 160 + 128:ic * 160 + 160],
                                    XXX[:, ic * tw:(ic + 1) * tw],
                                    start=(ic == 0), stop=(ic == NCH - 1))
                b.inc(inst, "T", 1)
            b.on("tensor", _tm)
            tm = b.cnt["T"]

            def _sm(e, tw=tw, tm=tm):
                e.wait_ge(b.sems["T"], tm)
                b.inc(e.activation(M5[0:32, 0 * TW:0 * TW + tw], psA[0:32, 0:tw], AF.Tanh), "S", 1)
                b.inc(e.activation(M5[0:32, 1 * TW:1 * TW + tw], psA[32:64, 0:tw], AF.Tanh), "S", 1)
                b.inc(e.activation(MBT[0:64, 0:tw], psA[64:128, 0:tw], AF.Tanh), "S", 1)
                b.inc(e.activation(M5[0:32, 4 * TW:4 * TW + tw], psB[0:32, 0:tw], AF.Tanh), "S", 1)
            b.on("scalar", _sm, bar=False)
            sm_ = b.cnt["S"]

            def _vmx(v, tw=tw, sm_=sm_):
                v.wait_ge(b.sems["S"], sm_)
                b.inc(v.tensor_copy(M5[0:32, 2 * TW:2 * TW + tw], MBT[0:32, 0:tw]), "V", 1)
                b.inc(v.tensor_copy(M5[0:32, 3 * TW:3 * TW + tw], MBT[32:64, 0:tw]), "V", 1)
            b.on("vector", _vmx, bar=False)

            maas = [MAAW, MAAK, MAAV, MAAR, MAAG]
            for f in range(5):
                t1 = b.cnt["T"]
                v1 = b.cnt["V"]

                def _tf(t, f=f, tw=tw, v1=v1):
                    for oc in range(NCH):
                        bank = (psC, psD)[oc % 2]
                        if oc >= 2:
                            t.wait_ge(b.sems["V"], v1 + 3 * (oc - 1))
                        b.inc(t.matmul(bank[:, 0:tw],
                                       W2B[0:32, f * D + oc * 128: f * D + (oc + 1) * 128],
                                       M5[0:32, f * TW:f * TW + tw],
                                       start=True, stop=True), "T", 1)
                b.on("tensor", _tf)

                def _vf(v, f=f, tw=tw, t1=t1):
                    for oc in range(NCH):
                        v.wait_ge(b.sems["T"], t1 + oc + 1)
                        bank = (psC, psD)[oc % 2][:, 0:tw]
                        t_ = TT_[:, 0:tw]
                        b.inc(v.tensor_scalar(t_, bank, maas[f](oc), None, ALU.add),
                              "V", 1)
                        b.inc(v.tensor_tensor(t_, t_, DX[:, oc * tw:(oc + 1) * tw],
                                              ALU.mult), "V", 1)
                        b.inc(v.tensor_tensor(
                            X5[:, (f * NCH + oc) * tw:(f * NCH + oc) * tw + tw],
                            t_, XT[:, oc * (TW + 2) + 1:oc * (TW + 2) + 1 + tw],
                            ALU.add), "V", 1)
                b.on("vector", _vf, bar=False)

            for (name, f, wslot, oap, act) in (
                    ("r", 3, 0, o_r, "scale8"), ("k", 1, 1, o_k, None),
                    ("v", 2, 2, o_v, None), ("g", 4, 3, o_g, "silu")):
                def xf(ic, f=f, tw=tw):
                    return X5[:, (f * NCH + ic) * tw:(f * NCH + ic) * tw + tw]

                if act == "silu":
                    def ev(oc, ps_ap, e, tw=tw):
                        return e.activation(GT[:, oc * tw:(oc + 1) * tw], ps_ap, AF.Silu)
                elif act == "scale8":
                    def ev(oc, ps_ap, e, tw=tw):
                        return e.activation(OUTT[:, oc * tw:(oc + 1) * tw], ps_ap,
                                            AF.Copy, scale=0.125)
                else:
                    def ev(oc, ps_ap, e, tw=tw):
                        return e.copy(OUTT[:, oc * tw:(oc + 1) * tw], ps_ap)
                proj(b, WPROJ[:, wslot * NCH * D:(wslot + 1) * NCH * D] if False else WPROJ,
                     0, xf, list(range(NCH)), 0, 0, (psA, psB), "scalar", ev) if False                     else None
                t2 = b.cnt["T"]
                s2 = b.cnt["S"]
                gd2 = b.cnt["GD"]

                def _tp(t, f=f, wslot=wslot, tw=tw, s2=s2):
                    for oc in range(NCH):
                        bank = (psA, psB)[oc % 2]
                        if oc >= 2:
                            t.wait_ge(b.sems["S"], s2 + (oc - 1))
                        for ic in range(NCH):
                            inst = t.matmul(
                                bank[:, 0:tw],
                                WPROJ[:, (wslot * NCH + ic) * D + oc * 128:
                                      (wslot * NCH + ic) * D + (oc + 1) * 128],
                                X5[:, (f * NCH + ic) * tw:(f * NCH + ic) * tw + tw],
                                start=(ic == 0), stop=(ic == NCH - 1))
                        b.inc(inst, "T", 1)
                b.on("tensor", _tp)

                def _sp(e, ev=ev, tw=tw, t2=t2, gd2=gd2):
                    if gd2 > 0:
                        e.wait_ge(b.sems["GD"], gd2)
                    for oc in range(NCH):
                        e.wait_ge(b.sems["T"], t2 + oc + 1)
                        b.inc(ev(oc, (psA, psB)[oc % 2][:, 0:tw], e), "S", 1)
                b.on("scalar", _sp, bar=False)
                if name == "g":
                    b.dma("gpsimd", [(o_g[ic * 128:(ic + 1) * 128, toff:toff + tw],
                                      GT[:, ic * tw:(ic + 1) * tw]) for ic in range(NCH)],
                          bar=True)
                else:
                    b.dma("gpsimd", [(oap[ic * 128:(ic + 1) * 128, toff:toff + tw],
                                      OUTT[:, ic * tw:(ic + 1) * tw]) for ic in range(NCH)],
                          bar=True)

            t3 = b.cnt["T"]

            def _tw(t, tw=tw):
                for ic in range(NCH):
                    inst = t.matmul(psC[0:64, 0:tw], TD1[:, ic * 64:(ic + 1) * 64],
                                    X5[:, (0 * NCH + ic) * tw:(0 * NCH + ic) * tw + tw],
                                    start=(ic == 0), stop=(ic == NCH - 1))
                b.inc(inst, "T", 1)
            b.on("tensor", _tw)
            tw3 = b.cnt["T"]

            def _sw(e, tw=tw, tw3=tw3):
                e.wait_ge(b.sems["T"], tw3)
                b.inc(e.activation(TWB[0:64, 0:tw], psC[0:64, 0:tw], AF.Tanh), "S", 1)
            b.on("scalar", _sw, bar=False)
            sw3 = b.cnt["S"]

            for oc in range(NCH):
                def _tw2(t, oc=oc, tw=tw, sw3=sw3):
                    t.wait_ge(b.sems["S"], sw3)
                    b.inc(t.matmul(psD[:, 0:tw], TD2[0:64, oc * 128:(oc + 1) * 128],
                                   TWB[0:64, 0:tw], start=True, stop=True), "T", 1)
                b.on("tensor", _tw2)
                tc = b.cnt["T"]

                def _se(e, oc=oc, tw=tw, tc=tc):
                    e.wait_ge(b.sems["T"], tc)
                    b.inc(e.activation(EXPT[:, 0:tw], psD[:, 0:tw], AF.Exp,
                                       bias=TDC(oc)), "S", 1)
                b.on("scalar", _se, bar=False)
                se_ = b.cnt["S"]
                def _ve(v, oc=oc, tw=tw, se_=se_):
                    v.wait_ge(b.sems["S"], se_)
                    b.inc(v.tensor_copy(EXPT2[0:64, 0:tw], EXPT[64:128, 0:tw]), "V", 1)
                b.on("vector", _ve, bar=False)
                ve_ = b.cnt["V"]

                def _tm2(t, oc=oc, tw=tw, se_=se_, ve_=ve_):
                    t.wait_ge(b.sems["S"], se_)
                    t.wait_ge(b.sems["V"], ve_)
                    b.inc(t.matmul(psC[0:1, 0:tw], ONES[0:64, 1:2], EXPT[0:64, 0:tw],
                                   start=True, stop=True), "T", 1)
                    b.inc(t.matmul(psB[0:1, 0:tw], ONES[0:64, 2:3], EXPT2[0:64, 0:tw],
                                   start=True, stop=True), "T", 1)
                b.on("tensor", _tm2, bar=False)
                tm2 = b.cnt["T"]

                def _vm(v, oc=oc, tw=tw, tm2=tm2):
                    v.wait_ge(b.sems["T"], tm2)
                    b.inc(v.tensor_scalar(MWT[0:1, 0:tw], psC[0:1, 0:tw],
                                          HDBE[0:1, 2 * oc:2 * oc + 1], None, ALU.mult),
                          "V", 1)
                    b.inc(v.tensor_scalar(MWT[0:1, TW:TW + tw], psB[0:1, 0:tw],
                                          HDBE[0:1, 2 * oc + 1:2 * oc + 2], None, ALU.mult),
                          "V", 1)
                b.on("vector", _vm, bar=True)
                b.dma("gpsimd", [(o_mw[2 * oc:2 * oc + 1, toff:toff + tw],
                                  MWT[0:1, 0:tw]),
                                 (o_mw[2 * oc + 1:2 * oc + 2, toff:toff + tw],
                                  MWT[0:1, TW:TW + tw])], bar=True)
    return nc


def build_l1attn(soff, dbg=False):
    nc = bass.Bass("TRN2", target_bir_lowering=False, debug=False)
    if dbg:
        d_dbgA = nc.dram_tensor("dbgA", [128, 1024], F32, kind="ExternalOutput").ap()
        d_dbgB = nc.dram_tensor("dbgB", [64, 4096], F32, kind="ExternalOutput").ap()
        d_dbgC = nc.dram_tensor("dbgC", [1, SEQ], F32, kind="ExternalOutput").ap()
    d_r = nc.dram_tensor("r_bf", [D, RPC], BF16, kind="ExternalInput").ap()
    d_k = nc.dram_tensor("kAll", [D, SEQ], BF16, kind="ExternalInput").ap()
    d_v = nc.dram_tensor("vAll", [D, SEQ], BF16, kind="ExternalInput").ap()
    d_mw = nc.dram_tensor("mwAll", [16, SEQ], F32, kind="ExternalInput").ap()
    d_slow = nc.dram_tensor("slow_row", [1, 16], F32, kind="ExternalInput").ap()
    d_alpha = nc.dram_tensor("alpha2", [64, 16], F32, kind="ExternalInput").ap()
    d_ident = nc.dram_tensor("ident", [128, 128], F32, kind="ExternalInput").ap()
    o_y = nc.dram_tensor("y_attn", [D, RPC], F32, kind="ExternalOutput").ap()

    NSTRIP, SW, NJ = 4, 512, SEQ // 128

    ctx = ExitStack()
    with ctx:
        b = Bld(nc, ctx)
        MWR = b.take("MWR", [1, SEQ], F32)
        CH = b.take("CH", [1, SEQ], F32)
        ZROW = b.take("ZROW", [1, SEQ], F32)
        CROWS = b.take("CROWS", [1, 512], F32)
        CMAT = b.take("CMAT", [128, 64], F32)
        CMATS = b.take("CMATS", [128, 64], F32)
        KH = b.take("KH", [64, SEQ], BF16)
        VH = b.take("VH", [64, SEQ], BF16)
        VT = b.take("VT", [128, 64 * 64], BF16)
        RH = b.take("RH", [64, RPC], BF16)
        IDF = b.take("IDF", [128, 128], F32)
        IDB = b.take("IDB", [128, 128], BF16)
        ONES = b.take("ONES", [128, 128], F32)
        SIGR = b.take("SIGR", [1, 16], F32)
        SIGB = b.take("SIGB", [128, 16], F32)
        ALPH = b.take("ALPH", [64, 16], F32)
        SCL = b.take("SCL", [128, 4 * 64], F32)
        CQB = b.take("CQB", [128, 2], F32)
        RVB = b.take("RVB", [64, 4 * SW], F32)
        RROW4 = b.take("RROW4", [1, 4 * SW], F32)
        DROW = b.take("DROW", [1, 2 * SW], F32)
        CTBF = b.take("CTBF", [128, SW], F32)
        CTBS = b.take("CTBS", [128, SW], F32)
        PTILE = b.take("PTILE", [128, 2 * SW], BF16)
        DBGPT = b.take("DBGPT", [128, 2 * SW], F32)
        MTILE = b.take("MTILE", [128, 2 * 64], F32)
        MKT = b.take("MKT", [128, 2 * SW], F32)
        T64 = b.take("T64", [64, 2 * SW], F32)
        YT = b.take("YT", [64, SW], F32)
        YACC = b.take("YACC", [64, 4 * SW], F32)

        psS1 = ctx.enter_context(nc.psum_tensor("psS1", [128, 512], F32))
        psS2 = ctx.enter_context(nc.psum_tensor("psS2", [128, 512], F32))
        psYA = ctx.enter_context(nc.psum_tensor("psYA", [128, 512], F32))
        psYB = ctx.enter_context(nc.psum_tensor("psYB", [128, 512], F32))
        psYC = ctx.enter_context(nc.psum_tensor("psYC", [128, 512], F32))
        psYD = ctx.enter_context(nc.psum_tensor("psYD", [128, 512], F32))
        psG1 = ctx.enter_context(nc.psum_tensor("psG1", [128, 512], F32))
        psTB = ctx.enter_context(nc.psum_tensor("psTB", [128, 512], BF16))

        b.dma("sync", [(IDF[:], d_ident[:]), (SIGR[:], d_slow[:]),
                       (ALPH[0:64, :], d_alpha[:])])

        def _v0(v):
            b.inc(v.memset(ONES[:], 1.0), "V", 1)
            b.inc(v.memset(ZROW[:], 0.0), "V", 1)
            b.inc(v.tensor_copy(IDB[:], IDF[:]), "V", 1)
        b.on("vector", _v0)

        def _s0(e):
            b.inc(e.activation(SIGR[:], SIGR[:], AF.Sigmoid), "S", 1)
            b.inc(e.activation(ALPH[0:64, :], ALPH[0:64, :], AF.Sigmoid), "S", 1)
        b.on("scalar", _s0)

        def _t0(t):
            b.inc(t.matmul(psG1[:, 0:16], ONES[0:1, 0:128], SIGR[:],
                           start=True, stop=True), "T", 1)
            for ps in (psYA, psYB, psYC, psYD):
                b.inc(t.matmul(ps[0:64, 0:512], ONES[0:1, 0:64], ZROW[0:1, 0:512],
                               start=True, stop=True), "T", 1)
        b.on("tensor", _t0)
        tb0 = b.cnt["T"]

        def _v1(v, tb0=tb0):
            v.wait_ge(b.sems["T"], tb0)
            b.inc(v.tensor_copy(SIGB[:], psG1[:, 0:16]), "V", 1)
        b.on("vector", _v1, bar=False)

        for h in range(16):
            b.dma("sync", [(MWR[:], d_mw[h:h + 1, :])], bar=True)
            b.dma("gpsimd", [(KH[0:64, :], d_k[h * 64:(h + 1) * 64, :]),
                             (VH[0:64, :], d_v[h * 64:(h + 1) * 64, :]),
                             (RH[0:64, :], d_r[h * 64:(h + 1) * 64, :])], bar=True)

            def _vc(v, h=h):
                b.inc(v.tensor_tensor_scan(CH[:], MWR[:], ZROW[:], 0.0,
                                           ALU.add, ALU.add), "V", 1)
                v.wait_ge(b.sems["V"], b.cnt["V"])
            b.on("vector", _vc)

            def _tc(t):
                for j in range(64):
                    b.inc(t.transpose(psG1[:, j:j + 1], CH[0:1, j * 128:(j + 1) * 128],
                                      IDF[0:1, 0:1]), "T", 1)
            b.on("tensor", _tc)
            tc = b.cnt["T"]

            def _vc2(v, h=h, tc=tc):
                v.wait_ge(b.sems["T"], tc)
                b.inc(v.tensor_copy(CMAT[:, 0:64], psG1[:, 0:64]), "V", 1)
                v.wait_ge(b.sems["V"], b.cnt["V"])
                b.inc(v.tensor_scalar(CMATS[:, 0:64], CMAT[:, 0:64],
                                      SIGB[:, h:h + 1], None, ALU.mult), "V", 1)
                v.wait_ge(b.sems["V"], b.cnt["V"])
            b.on("vector", _vc2, bar=False)

            for g in range(8):
                def _tv(t, g=g):
                    for jj in range(8):
                        j = g * 8 + jj
                        b.inc(t.transpose(psTB[:, jj * 64:(jj + 1) * 64],
                                          VH[0:64, j * 128:(j + 1) * 128],
                                          IDB[0:64, 0:64]), "T", 1)
                b.on("tensor", _tv)
                tv = b.cnt["T"]

                def _vv(v, g=g, tv=tv):
                    v.wait_ge(b.sems["T"], tv)
                    b.inc(v.tensor_copy(VT[:, g * 512:(g + 1) * 512], psTB[:, 0:512]),
                          "V", 1)
                b.on("vector", _vv, bar=False)

            for si in range(NSTRIP):
                q0 = soff + si * SW
                qm = q0 + SW // 2
                jd = q0 // 128

                def _tq(t, qm=qm):
                    b.inc(t.matmul(psG1[:, 0:1], ONES[0:1, 0:128], CH[0:1, qm:qm + 1],
                                   start=True, stop=True), "T", 1)
                b.on("tensor", _tq)
                tq = b.cnt["T"]

                def _vq(v, h=h, tq=tq, q0=q0, qm=qm):
                    v.wait_ge(b.sems["T"], tq)
                    b.inc(v.tensor_copy(CQB[:, 0:1], psG1[:, 0:1]), "V", 1)
                    v.wait_ge(b.sems["V"], b.cnt["V"])
                    b.inc(v.tensor_scalar(CQB[:, 1:2], CQB[:, 0:1], SIGB[:, h:h + 1],
                                          None, ALU.mult), "V", 1)
                    b.inc(v.tensor_scalar(CROWS[0:1, 0:SW], CH[0:1, q0:q0 + SW],
                                          SIGR[0:1, h:h + 1], None, ALU.mult), "V", 1)
                    for mi, cm in ((0, CMAT), (1, CMATS)):
                        t_ = MTILE[:, mi * 64:mi * 64 + 64]
                        b.inc(v.tensor_scalar(t_, cm[:, 0:64], CQB[:, mi:mi + 1],
                                              None, ALU.subtract), "V", 1)
                        v.wait_ge(b.sems["V"], b.cnt["V"])
                        b.inc(v.tensor_scalar(t_, t_, 80.0, -80.0, ALU.min, ALU.max),
                              "V", 1)
                        v.wait_ge(b.sems["V"], b.cnt["V"])
                    d0 = DROW[0:1, 0:SW]
                    b.inc(v.tensor_scalar(d0, CH[0:1, q0:q0 + SW],
                                          CH[0:1, qm:qm + 1], None, ALU.subtract),
                          "V", 1)
                    v.wait_ge(b.sems["V"], b.cnt["V"])
                    d1 = DROW[0:1, SW:2 * SW]
                    b.inc(v.tensor_scalar(d1, d0, SIGR[0:1, h:h + 1], None, ALU.mult),
                          "V", 1)
                    v.wait_ge(b.sems["V"], b.cnt["V"])
                    b.inc(v.tensor_scalar(d0, d0, 80.0, -80.0, ALU.min, ALU.max), "V", 1)
                    b.inc(v.tensor_scalar(d1, d1, 80.0, -80.0, ALU.min, ALU.max), "V", 1)
                    v.wait_ge(b.sems["V"], b.cnt["V"])
                b.on("vector", _vq, bar=True)
                vq = b.cnt["V"]

                def _sq(e, vq=vq):
                    e.wait_ge(b.sems["V"], vq)
                    for mi in range(2):
                        src = MTILE[:, mi * 64:mi * 64 + 64]
                        b.inc(e.activation(SCL[:, (2 * mi) * 64:(2 * mi) * 64 + 64],
                                           src, AF.Exp, scale=-1.0), "S", 1)
                        b.inc(e.activation(SCL[:, (2 * mi + 1) * 64:(2 * mi + 1) * 64 + 64],
                                           src, AF.Exp), "S", 1)
                        d_ = DROW[0:1, mi * SW:(mi + 1) * SW]
                        b.inc(e.activation(RROW4[0:1, (2 * mi) * SW:(2 * mi) * SW + SW],
                                           d_, AF.Exp), "S", 1)
                        b.inc(e.activation(
                            RROW4[0:1, (2 * mi + 1) * SW:(2 * mi + 1) * SW + SW],
                            d_, AF.Exp, scale=-1.0), "S", 1)
                b.on("scalar", _sq, bar=False)
                sq_ = b.cnt["S"]

                for bi in range(6):
                    def _tbx(t, bi=bi, q0=q0, sq_=sq_):
                        if bi == 0:
                            t.wait_ge(b.sems["S"], sq_)
                        if bi < 4:
                            b.inc(t.matmul(psG1[0:64, 0:SW], ONES[0:1, 0:64],
                                           RROW4[0:1, bi * SW:(bi + 1) * SW],
                                           start=True, stop=True), "T", 1)
                        else:
                            rhs = (CH[0:1, q0:q0 + SW] if bi == 4
                                   else CROWS[0:1, 0:SW])
                            b.inc(t.matmul(psG1[:, 0:SW], ONES[0:1, 0:128], rhs,
                                           start=True, stop=True), "T", 1)
                    b.on("tensor", _tbx)
                    tbx = b.cnt["T"]

                    def _vbx(v, bi=bi, tbx=tbx):
                        v.wait_ge(b.sems["T"], tbx)
                        if bi < 4:
                            b.inc(v.tensor_copy(RVB[0:64, bi * SW:(bi + 1) * SW],
                                                psG1[0:64, 0:SW]), "V", 1)
                        else:
                            b.inc(v.tensor_copy((CTBF, CTBS)[bi - 4][:, 0:SW],
                                                psG1[:, 0:SW]), "V", 1)
                    b.on("vector", _vbx, bar=False)

                t0c = b.cnt["T"]
                v0c = b.cnt["V"]
                s0c = b.cnt["S"]
                diag = lambda j: jd <= j < jd + 4
                bdiag = jd // 4
                NB = NJ // 4
                t_smm, t_av, v_evict, v_badd = {}, {}, {}, {}
                v_diag1, v_diag15, s_diag1, s_diag2 = {}, {}, {}, {}
                vc, tc_, sc = v0c, t0c, s0c
                for j in range(NJ):
                    if diag(j):
                        vc += 2
                        v_diag1[j] = vc
                        vc += 2
                        v_diag15[j] = vc
                        vc += 2
                        sc += 2
                        s_diag1[j] = sc
                        sc += 2
                        s_diag2[j] = sc
                    else:
                        vc += 2
                    v_evict[j] = vc
                    if j % 4 == 3:
                        vc += 4
                        v_badd[j // 4] = vc
                tc_ += 1
                t_smm[0] = tc_
                for j in range(1, NJ):
                    tc_ += 1
                    t_smm[j] = tc_
                    tc_ += 1
                    t_av[j - 1] = tc_
                tc_ += 1
                t_av[NJ - 1] = tc_

                def _tj(t, si=si, jd=jd, bdiag=bdiag):
                    def emit_av(j):
                        t.wait_ge(b.sems["V"], v_evict[j])
                        bi = j // 4
                        if j % 4 == 0 and bi >= 2:
                            t.wait_ge(b.sems["V"], v_badd[bi - 2])
                        lo = j < jd + 4
                        for mi in range(2):
                            ps = ((psYA, psYC) if bi % 2 == 0 else (psYB, psYD))[mi]
                            inst = t.matmul(ps[0:64, 0:SW], VT[:, j * 64:(j + 1) * 64],
                                            PTILE[:, mi * SW:(mi + 1) * SW],
                                            start=False, stop=False,
                                            skip_group_check=True)
                        b.inc(inst, "T", 1)
                        assert b.cnt["T"] == t_av[j]
                    for j in range(NJ):
                        bank = (psS1, psS2)[j % 2]
                        if j >= 1:
                            t.wait_ge(b.sems["V"], v_evict[j - 1])
                        b.inc(t.matmul(bank[:, 0:SW], KH[0:64, j * 128:(j + 1) * 128],
                                       RH[0:64, si * SW:si * SW + SW],
                                       start=True, stop=True), "T", 1)
                        assert b.cnt["T"] == t_smm[j]
                        if j >= 1:
                            emit_av(j - 1)
                    emit_av(NJ - 1)
                b.on("tensor", _tj, bar=False)

                def _vj(v, jd=jd, bdiag=bdiag):
                    for j in range(NJ):
                        bank = (psS1, psS2)[j % 2]
                        v.wait_ge(b.sems["T"], t_smm[j])
                        if j >= 1:
                            v.wait_ge(b.sems["T"], t_av[j - 1])
                        if not diag(j):
                            ci = 0 if j < jd else 1
                            for mi in range(2):
                                b.inc(v.tensor_scalar(
                                    PTILE[:, mi * SW:(mi + 1) * SW], bank[:, 0:SW],
                                    SCL[:, (2 * mi + ci) * 64 + j:
                                        (2 * mi + ci) * 64 + j + 1],
                                    None, ALU.mult), "V", 1)
                        else:
                            for mi, (ctb, cm) in enumerate(((CTBF, CMAT), (CTBS, CMATS))):
                                b.inc(v.tensor_scalar(
                                    MKT[:, mi * SW:(mi + 1) * SW], ctb[:, 0:SW],
                                    cm[:, j:j + 1], None, ALU.subtract), "V", 1)
                            v.wait_ge(b.sems["S"], s_diag1[j])
                            for mi, ctb in enumerate((CTBF, CTBS)):
                                b.inc(v.tensor_tensor(MKT[:, mi * SW:(mi + 1) * SW],
                                                      MKT[:, mi * SW:(mi + 1) * SW],
                                                      ctb[:, 0:SW], ALU.add), "V", 1)
                            v.wait_ge(b.sems["S"], s_diag2[j])
                            for mi in range(2):
                                b.inc(v.tensor_tensor(
                                    PTILE[:, mi * SW:(mi + 1) * SW], bank[:, 0:SW],
                                    MKT[:, mi * SW:(mi + 1) * SW], ALU.mult), "V", 1)

                        if j % 4 == 3:
                            bi = j // 4
                            v.wait_ge(b.sems["T"], t_av[j])
                            lo = bi <= bdiag
                            for mi in range(2):
                                ps = ((psYA, psYC) if bi % 2 == 0 else (psYB, psYD))[mi]
                                seg = (0 if lo else 2) + mi
                                dst = YACC[0:64, seg * SW:(seg + 1) * SW]
                                first = (bi == 0) if lo else (bi == bdiag + 1)
                                if first:
                                    inst = v.tensor_copy(dst, ps[0:64, 0:SW])
                                else:
                                    inst = v.tensor_tensor(dst, dst, ps[0:64, 0:SW],
                                                           ALU.add)
                                b.inc(inst, "V", 1)
                            for mi in range(2):
                                ps = ((psYA, psYC) if bi % 2 == 0 else (psYB, psYD))[mi]
                                b.inc(v.memset(ps[0:64, 0:SW], 0.0), "V", 1)
                            assert b.cnt["V"] == v_badd[bi]
                            if dbg and h == 0 and si == 0 and j == 11:
                                v.tensor_copy(DBGPT[0:64, 0:SW], psYA[0:64, 0:SW])
                            if dbg and h == 0 and si == 0 and j == 15:
                                v.tensor_copy(DBGPT[0:64, SW:2 * SW],
                                              YACC[0:64, 2 * SW:3 * SW])
                b.on("vector", _vj, bar=False)

                def _sj(e, jd=jd):
                    for j in range(jd, jd + 4):
                        e.wait_ge(b.sems["V"], v_diag1[j])
                        for mi in range(2):
                            b.inc(e.activation(MKT[:, mi * SW:(mi + 1) * SW],
                                               MKT[:, mi * SW:(mi + 1) * SW], AF.Abs),
                                  "S", 1)
                        e.wait_ge(b.sems["V"], v_diag15[j])
                        for mi in range(2):
                            b.inc(e.activation(MKT[:, mi * SW:(mi + 1) * SW],
                                               MKT[:, mi * SW:(mi + 1) * SW], AF.Exp,
                                               scale=-1.0, bias=CQB[:, mi:mi + 1]),
                                  "S", 1)
                b.on("scalar", _sj, bar=False)

                up_empty = bdiag == NB - 1

                def _vy(v, h=h, si=si, up_empty=up_empty):
                    v.wait_ge(b.sems["V"], b.cnt["V"])
                    for mi in range(2):
                        dst = T64[0:64, mi * SW:(mi + 1) * SW]
                        b.inc(v.tensor_tensor(dst, YACC[0:64, mi * SW:(mi + 1) * SW],
                                              RVB[0:64, (2 * mi) * SW:(2 * mi) * SW + SW],
                                              ALU.mult), "V", 1)
                        if not up_empty:
                            b.inc(v.tensor_tensor(YT[0:64, 0:SW],
                                                  YACC[0:64, (2 + mi) * SW:(3 + mi) * SW],
                                                  RVB[0:64, (2 * mi + 1) * SW:
                                                      (2 * mi + 1) * SW + SW],
                                                  ALU.mult), "V", 1)
                            v.wait_ge(b.sems["V"], b.cnt["V"])
                            b.inc(v.tensor_tensor(dst, dst, YT[0:64, 0:SW], ALU.add),
                                  "V", 1)
                            v.wait_ge(b.sems["V"], b.cnt["V"])
                    v.wait_ge(b.sems["V"], b.cnt["V"])
                    b.inc(v.tensor_tensor(YT[0:64, 0:SW], T64[0:64, 0:SW],
                                          T64[0:64, SW:2 * SW], ALU.subtract), "V", 1)
                    v.wait_ge(b.sems["V"], b.cnt["V"])
                    b.inc(v.scalar_tensor_tensor(YT[0:64, 0:SW], YT[0:64, 0:SW],
                                                 ALPH[0:64, h:h + 1],
                                                 T64[0:64, SW:2 * SW],
                                                 ALU.mult, ALU.add), "V", 1)
                    v.wait_ge(b.sems["V"], b.cnt["V"])
                b.on("vector", _vy, bar=True)
                b.dma("gpsimd", [(o_y[h * 64:(h + 1) * 64, si * SW:(si + 1) * SW],
                                  YT[0:64, 0:SW])], bar=True)
                if dbg and h == 0 and si == 0:
                    b.dma("gpsimd", [(d_dbgB[0:64, 2048:2048 + 2048],
                                      YACC[0:64, 0:2048])], bar=True)
                if dbg and h == 0 and si == 0:
                    def _vdbg(v):
                        for mi, ps in enumerate((psYA, psYB, psYC, psYD)):
                            b.inc(v.tensor_copy(
                                RVB[0:64, mi * SW:(mi + 1) * SW] if False else
                                T64[0:64, 0:SW], ps[0:64, 0:SW]), "V", 1)
                            b.inc(v.tensor_copy(
                                MKT[0:64, 0:SW] if False else
                                RVB[0:64, mi * SW:(mi + 1) * SW], T64[0:64, 0:SW]),
                                "V", 1)
                    b.on("vector", _vdbg)
                    b.dma("gpsimd", [
                        (d_dbgA[:, 0:64], CMAT[:, 0:64]),
                        (d_dbgA[:, 64:128], CMATS[:, 0:64]),
                        (d_dbgA[:, 128:384], SCL[:, 0:256]),
                        (d_dbgA[:, 384:386], CQB[:, 0:2]),
                        (d_dbgA[:, 386:898], CTBF[:, 0:SW]),
                        (d_dbgB[0:64, 0:2048], RVB[0:64, 0:2048]),
                        (d_dbgB[0:64, 2048:2048 + 512],
                         PTILE[:, 0:SW].bitcast(F32) if False else RVB[0:64, 0:512]),
                        (d_dbgC[:], CH[:]),
                        (d_dbgA[:, 898:1022], DBGPT[:, 0:124]) if False else
                        (d_dbgB[0:64, 0:1024], DBGPT[0:64, 0:1024]),
                        ], bar=True)

        fin = b.cnt["GD"]

        def _fin(e, fin=fin):
            e.wait_ge(b.sems["GD"], fin)
        b.on("gpsimd", _fin, bar=False)
    return nc


def build_l1post():
    nc = bass.Bass("TRN2", target_bir_lowering=False, debug=False)
    d_y = nc.dram_tensor("y_attn", [D, RPC], F32, kind="ExternalInput").ap()
    d_g = nc.dram_tensor("g", [D, RPC], F32, kind="ExternalInput").ap()
    d_x2 = nc.dram_tensor("x2", [D, RPC], F32, kind="ExternalInput").ap()
    d_wo = nc.dram_tensor("wo", [D, D], F32, kind="ExternalInput").ap()
    d_gnv = nc.dram_tensor("gnv", [128, 16], F32, kind="ExternalInput").ap()
    d_w1 = nc.dram_tensor("w1", [D, 4 * D], F32, kind="ExternalInput").ap()
    d_w2 = nc.dram_tensor("w2", [4 * D, D], F32, kind="ExternalInput").ap()
    d_fv = nc.dram_tensor("fv", [128, 56], F32, kind="ExternalInput").ap()
    d_ident = nc.dram_tensor("ident", [128, 128], F32, kind="ExternalInput").ap()
    x_out = nc.dram_tensor("x_out", [D, RPC], F32, kind="ExternalOutput").ap()

    NT = 8
    W = RPC // NT
    GN_EPS = 1e-5 * 64.0

    ctx = ExitStack()
    with ctx:
        b = Bld(nc, ctx)
        WO = b.take("WO", [128, NCH * D], BF16)
        W1B = b.take("W1B", [128, NCH * 4 * D], BF16)
        W2B = b.take("W2B", [128, 32 * D], BF16)
        Y = b.take("Y", [128, NCH * W], F32)
        G = b.take("G", [128, NCH * W], F32)
        X2 = b.take("X2", [128, NCH * W], F32)
        X3 = b.take("X3", [128, NCH * W], F32)
        YN = b.take("YN", [128, NCH * W], BF16)
        H1 = b.take("H1", [128, 32 * W], BF16)
        YL2 = b.take("YL2", [64, W], F32)
        SQU = b.take("SQU", [128, 2 * W], F32)
        MROW = b.take("MROW", [1, 256], F32)
        ONES = b.take("ONES", [128, 128], F32)
        VEC = b.take("VEC", [128, 16], F32)
        FV = b.take("FV", [128, 56], F32)
        ROWS_ = tuple(b.take(f"ROW{i}", [1, 512], F32) for i in range(2))
        SQ2 = b.take("SQ2", [128, 2 * W], F32)
        MUR = ROWS_[0]
        RSR = ROWS_[1]
        TRW = SQ2

        psA = ctx.enter_context(nc.psum_tensor("psA", [128, 512], F32))
        psB = ctx.enter_context(nc.psum_tensor("psB", [128, 512], F32))
        psC = ctx.enter_context(nc.psum_tensor("psC", [128, 512], F32))
        psD = ctx.enter_context(nc.psum_tensor("psD", [128, 512], F32))
        psSum = ctx.enter_context(nc.psum_tensor("psSum", [128, 512], F32))
        psSq = ctx.enter_context(nc.psum_tensor("psSq", [128, 512], F32))
        psMu = ctx.enter_context(nc.psum_tensor("psMu", [128, 512], F32))
        psRstd = ctx.enter_context(nc.psum_tensor("psRstd", [128, 512], F32))
        LNPS = {"sum": psSum, "sq": psSq, "mu": psMu, "rstd": psRstd}

        b.dma("sync", [(VEC[:], d_gnv[:]), (FV[:], d_fv[:])])
        b.dma("gpsimd", [(WO[:, ic * D:(ic + 1) * D], d_wo[ic * 128:(ic + 1) * 128, :])
                         for ic in range(NCH)]
              + [(W1B[:, ic * 4 * D:(ic + 1) * 4 * D],
                  d_w1[ic * 128:(ic + 1) * 128, :]) for ic in range(NCH)]
              + [(W2B[:, ic * D:(ic + 1) * D], d_w2[ic * 128:(ic + 1) * 128, :])
                 for ic in range(32)])

        def _v0(v):
            b.inc(v.memset(ONES[:], 1.0), "V", 1)
            b.inc(v.memset(MROW[0:1, 0:64], 1.0), "V", 1)
            b.inc(v.memset(MROW[0:1, 64:128], 0.0), "V", 1)
            b.inc(v.memset(MROW[0:1, 128:192], 0.0), "V", 1)
            b.inc(v.memset(MROW[0:1, 192:256], 1.0), "V", 1)
        b.on("vector", _v0)

        def GNG(j):
            return VEC[:, j:j + 1]

        def GNB(j):
            return VEC[:, 8 + j:8 + j + 1]

        def FB1(j):
            return FV[:, j:j + 1]

        def FB2(j):
            return FV[:, 32 + j:32 + j + 1]

        def FG(j):
            return FV[:, 40 + j:40 + j + 1]

        def FBt(j):
            return FV[:, 48 + j:48 + j + 1]

        eps_gn = b.ccol(GN_EPS)

        for ti in range(NT):
            toff = ti * W
            b.dma("sync", [(Y[:, ic * W:(ic + 1) * W],
                            d_y[ic * 128:(ic + 1) * 128, toff:toff + W])
                           for ic in range(NCH)]
                  + [(G[:, ic * W:(ic + 1) * W],
                      d_g[ic * 128:(ic + 1) * 128, toff:toff + W]) for ic in range(NCH)]
                  + [(X2[:, ic * W:(ic + 1) * W],
                      d_x2[ic * 128:(ic + 1) * 128, toff:toff + W]) for ic in range(NCH)],
                  bar=True)

            for c in range(NCH):
                yc_ = Y[:, c * W:(c + 1) * W]

                def _vg(v, c=c, yc_=yc_):
                    b.inc(v.tensor_copy(YL2[0:64, 0:W], yc_[64:128, 0:W]), "V", 1)
                b.on("vector", _vg)
                vg = b.cnt["V"]

                def _sg(e, c=c, yc_=yc_, vg=vg):
                    e.wait_ge(b.sems["V"], vg)
                    b.inc(e.activation(SQU[0:64, 0:W], yc_[0:64, 0:W], AF.Square), "S", 1)
                    b.inc(e.activation(SQU[0:64, W:2 * W], YL2[0:64, 0:W], AF.Square),
                          "S", 1)
                b.on("scalar", _sg, bar=False)
                sg = b.cnt["S"]

                def _tg(t, c=c, yc_=yc_, sg=sg):
                    t.wait_ge(b.sems["S"], sg)
                    b.inc(t.matmul(psSum[0:1, 0:W], ONES[0:64, 0:1], yc_[0:64, 0:W],
                                   start=True, stop=True), "T", 1)
                    b.inc(t.matmul(psSum[0:1, W:2 * W], ONES[0:64, 0:1], YL2[0:64, 0:W],
                                   start=True, stop=True), "T", 1)
                    b.inc(t.matmul(psSq[0:1, 0:W], ONES[0:64, 0:1], SQU[0:64, 0:W],
                                   start=True, stop=True), "T", 1)
                    b.inc(t.matmul(psSq[0:1, W:2 * W], ONES[0:64, 0:1],
                                   SQU[0:64, W:2 * W], start=True, stop=True), "T", 1)
                b.on("tensor", _tg, bar=False)
                tg = b.cnt["T"]

                def _vs(v, tg=tg):
                    v.wait_ge(b.sems["T"], tg)
                    b.inc(v.tensor_scalar(MUR[0:1, 0:2 * W], psSum[0:1, 0:2 * W],
                                          1.0 / 64, None, ALU.mult), "V", 1)
                    b.inc(v.tensor_scalar(RSR[0:1, 0:2 * W], psSq[0:1, 0:2 * W],
                                          1.0 / 64, None, ALU.mult), "V", 1)
                    b.inc(v.tensor_tensor(TRW[0:1, 0:2 * W], MUR[0:1, 0:2 * W],
                                          MUR[0:1, 0:2 * W], ALU.mult), "V", 1)
                    b.inc(v.tensor_tensor(RSR[0:1, 0:2 * W], RSR[0:1, 0:2 * W],
                                          TRW[0:1, 0:2 * W], ALU.subtract), "V", 1)
                b.on("vector", _vs, bar=False)
                vs = b.cnt["V"]

                def _ss(e, vs=vs):
                    e.wait_ge(b.sems["V"], vs)
                    b.inc(e.activation(RSR[0:1, 0:2 * W], RSR[0:1, 0:2 * W], AF.Sqrt,
                                       bias=eps_gn[0:1, 0:1]), "S", 1)
                b.on("scalar", _ss, bar=False)
                ss = b.cnt["S"]

                def _vr(v, ss=ss):
                    v.wait_ge(b.sems["S"], ss)
                    b.inc(v.reciprocal(RSR[0:1, 0:2 * W], RSR[0:1, 0:2 * W]), "V", 1)
                b.on("vector", _vr, bar=False)
                vr = b.cnt["V"]

                def _tb(t, vr=vr):
                    t.wait_ge(b.sems["V"], vr)
                    t.matmul(psMu[:, 0:W], MROW[0:1, 0:128], MUR[0:1, 0:W],
                             start=True, stop=False)
                    b.inc(t.matmul(psMu[:, 0:W], MROW[0:1, 128:256], MUR[0:1, W:2 * W],
                                   start=False, stop=True), "T", 1)
                    t.matmul(psRstd[:, 0:W], MROW[0:1, 0:128], RSR[0:1, 0:W],
                             start=True, stop=False)
                    b.inc(t.matmul(psRstd[:, 0:W], MROW[0:1, 128:256], RSR[0:1, W:2 * W],
                                   start=False, stop=True), "T", 1)
                b.on("tensor", _tb, bar=False)
                tbx = b.cnt["T"]

                def _vn(v, c=c, yc_=yc_, tbx=tbx):
                    v.wait_ge(b.sems["T"], tbx)
                    b.inc(v.tensor_tensor(yc_, yc_, psMu[:, 0:W], ALU.subtract), "V", 1)
                    b.inc(v.tensor_tensor(yc_, yc_, psRstd[:, 0:W], ALU.mult), "V", 1)
                    b.inc(v.tensor_scalar(yc_, yc_, GNG(c), GNB(c), ALU.mult, ALU.add),
                          "V", 1)
                    b.inc(v.tensor_tensor(YN[:, c * W:(c + 1) * W], yc_,
                                          G[:, c * W:(c + 1) * W], ALU.mult), "V", 1)
                b.on("vector", _vn, bar=False)

            t0 = b.cnt["T"]
            v0 = b.cnt["V"]

            def _tw(t, v0=v0):
                for oc in range(NCH):
                    bank = (psA, psB)[oc % 2]
                    if oc >= 2:
                        t.wait_ge(b.sems["V"], v0 + (oc - 1))
                    for ic in range(NCH):
                        inst = t.matmul(bank[:, 0:W],
                                        WO[:, ic * D + oc * 128: ic * D + (oc + 1) * 128],
                                        YN[:, ic * W:(ic + 1) * W],
                                        start=(ic == 0), stop=(ic == NCH - 1))
                    b.inc(inst, "T", 1)
            b.on("tensor", _tw)

            def _vw(v, t0=t0):
                for oc in range(NCH):
                    v.wait_ge(b.sems["T"], t0 + oc + 1)
                    xap = X2[:, oc * W:(oc + 1) * W]
                    b.inc(v.tensor_tensor(xap, (psA, psB)[oc % 2][:, 0:W], xap, ALU.add),
                          "V", 1)
            b.on("vector", _vw, bar=False)

            def x2fn(ic):
                return X2[:, ic * W:(ic + 1) * W]

            def to_x3(ic, v, normed):
                b.inc(v.tensor_copy(X3[:, ic * W:(ic + 1) * W], normed), "V", 1)
            layernorm(b, x2fn, W, LNPS, ROWS_, SQ2, ONES, to_x3)

            def _vb(v):
                for ic in range(NCH):
                    b.inc(v.tensor_copy(YN[:, ic * W:(ic + 1) * W],
                                        X3[:, ic * W:(ic + 1) * W]), "V", 1)
            b.on("vector", _vb)

            t1 = b.cnt["T"]
            s1 = b.cnt["S"]

            def _tf(t, s1=s1):
                for oc in range(32):
                    bank = (psA, psB)[oc % 2]
                    if oc >= 2:
                        t.wait_ge(b.sems["S"], s1 + (oc - 1))
                    for ic in range(NCH):
                        inst = t.matmul(bank[:, 0:W],
                                        W1B[:, ic * 4 * D + oc * 128:
                                            ic * 4 * D + (oc + 1) * 128],
                                        YN[:, ic * W:(ic + 1) * W],
                                        start=(ic == 0), stop=(ic == NCH - 1))
                    b.inc(inst, "T", 1)
            b.on("tensor", _tf)

            def _sf(e, t1=t1):
                for oc in range(32):
                    e.wait_ge(b.sems["T"], t1 + oc + 1)
                    b.inc(e.activation(H1[:, oc * W:(oc + 1) * W],
                                       (psA, psB)[oc % 2][:, 0:W], AF.Relu,
                                       bias=FB1(oc)), "S", 1)
            b.on("scalar", _sf, bar=False)

            t2 = b.cnt["T"]
            v2 = b.cnt["V"]

            def _tf2(t, v2=v2):
                for oc in range(NCH):
                    bank = (psC, psD)[oc % 2]
                    if oc >= 2:
                        t.wait_ge(b.sems["V"], v2 + (oc - 1))
                    for ic in range(32):
                        inst = t.matmul(bank[:, 0:W],
                                        W2B[:, ic * D + oc * 128: ic * D + (oc + 1) * 128],
                                        H1[:, ic * W:(ic + 1) * W],
                                        start=(ic == 0), stop=(ic == 31))
                    b.inc(inst, "T", 1)
            b.on("tensor", _tf2)

            def _vf2(v, t2=t2):
                for oc in range(NCH):
                    v.wait_ge(b.sems["T"], t2 + oc + 1)
                    xap = X2[:, oc * W:(oc + 1) * W]
                    b.inc(v.scalar_tensor_tensor(xap, (psC, psD)[oc % 2][:, 0:W],
                                                 FB2(oc), X3[:, oc * W:(oc + 1) * W],
                                                 ALU.add, ALU.add), "V", 1)
            b.on("vector", _vf2, bar=False)

            def x4fn(ic):
                return X2[:, ic * W:(ic + 1) * W]

            def to_v(ic, v, normed):
                b.inc(v.tensor_scalar(Y[:, ic * W:(ic + 1) * W], normed,
                                      FG(ic), FBt(ic), ALU.mult, ALU.add), "V", 1)
            layernorm(b, x4fn, W, LNPS, ROWS_, SQ2, ONES, to_v)

            def _vu(v):
                for ic in range(NCH):
                    b.inc(v.tensor_tensor(Y[:, ic * W:(ic + 1) * W],
                                          Y[:, ic * W:(ic + 1) * W],
                                          X3[:, ic * W:(ic + 1) * W], ALU.add), "V", 1)
            b.on("vector", _vu)

            def yfn(ic):
                return Y[:, ic * W:(ic + 1) * W]

            def to_out(ic, v, normed):
                b.inc(v.tensor_copy(X3[:, ic * W:(ic + 1) * W], normed), "V", 1)
            layernorm(b, yfn, W, LNPS, ROWS_, SQ2, ONES, to_out)

            b.dma("gpsimd", [(x_out[ic * 128:(ic + 1) * 128, toff:toff + W],
                              X3[:, ic * W:(ic + 1) * W]) for ic in range(NCH)],
                  bar=True)

        fin = b.cnt["GD"]

        def _fin(e, fin=fin):
            e.wait_ge(b.sems["GD"], fin)
        b.on("gpsimd", _fin, bar=False)
    return nc


def emit_l1pre(nc, b, ctx, x2src, psA, psB, psC, psD):
    FHM1 = FH - 1
    d_wr = nc.dram_tensor("W_r", [D, D], F32, kind="ExternalInput").ap()
    d_wk = nc.dram_tensor("W_k", [D, D], F32, kind="ExternalInput").ap()
    d_wv = nc.dram_tensor("W_v", [D, D], F32, kind="ExternalInput").ap()
    d_wg = nc.dram_tensor("W_g", [D, D], F32, kind="ExternalInput").ap()
    d_w1 = nc.dram_tensor("maa_w1", [D, 160], F32, kind="ExternalInput").ap()
    d_w2 = nc.dram_tensor("maa_w2", [5 * 32, D], F32, kind="ExternalInput").ap()
    d_tdw1 = nc.dram_tensor("tdw1", [D, 64], F32, kind="ExternalInput").ap()
    d_tdw2 = nc.dram_tensor("tdw2", [64, D], F32, kind="ExternalInput").ap()
    d_vec = nc.dram_tensor("pvecs", [128, 64], F32, kind="ExternalInput").ap()
    d_hdb = nc.dram_tensor("hdb", [1, 16], F32, kind="ExternalInput").ap()
    o_r = nc.dram_tensor("r_bf", [D, RPC], BF16, kind="ExternalOutput").ap()
    o_k = nc.dram_tensor("k_bf", [D, RPC], BF16, kind="ExternalOutput").ap()
    o_v = nc.dram_tensor("v_bf", [D, RPC], BF16, kind="ExternalOutput").ap()
    o_g = nc.dram_tensor("g", [D, RPC], F32, kind="ExternalOutput").ap()
    o_mw = nc.dram_tensor("meanw", [16, RPC], F32, kind="ExternalOutput").ap()

    tiles = []
    off = 0
    for w in tile_widths(RPC):
        tiles.append((off, w))
        off += w
    TW = max(w for (_, w) in tiles)

    d_em = nc.dram_tensor("em", [128, 2], F32, kind="ExternalInput").ap()
    XT = b.take("XT", [128, NCH * (TW + 2)], F32)
    EMS = b.take("EMS", [128, 2], F32)
    VEC = b.take("VEC", [128, 64], F32)
    ONES = b.take("ONES", [128, 128], F32)
    ROWS_ = tuple(b.take(f"ROW{i}", [1, 512], F32) for i in range(2))
    WPROJ = b.take("WPROJ", [128, 4 * NCH * D], BF16)
    W1B = b.take("W1B", [128, NCH * 160], BF16)
    W2B = b.take("W2B", [32, 5 * D], BF16)
    TD1 = b.take("TD1", [128, NCH * 64], BF16)
    TD2 = b.take("TD2", [64, D], BF16)
    HDBE = b.take("HDBE", [1, 16], F32)
    DX = b.take("DX", [128, NCH * TW], F32)
    XXX = b.take("XXX", [128, NCH * TW], BF16)
    X5 = b.take("X5", [128, 5 * NCH * TW], BF16)
    M5 = b.take("M5", [32, 5 * TW], BF16)
    MBT = b.take("MBT", [64, TW], F32)
    TT_ = b.take("TT", [128, 2 * TW], F32)
    OUTT = b.take("OUTT", [128, NCH * TW], BF16)
    MWT = b.take("MWT", [1, 2 * TW], F32)
    TWB = b.take("TWB", [64, TW], BF16)
    EXPT = b.take("EXPT", [128, TW], F32)
    EXPT2 = b.take("EXPT2", [64, TW], F32)
    GT = b.alias("GT", [128, NCH * TW], F32, DX)


    b.dma("sync", [(VEC[:], d_vec[:]), (HDBE[:], d_hdb[:]), (EMS[:], d_em[:])],
          bar=True)
    b.dma("gpsimd",
          [(WPROJ[:, (w * NCH + ic) * D:(w * NCH + ic) * D + D],
            dw[ic * 128:(ic + 1) * 128, :])
           for w, dw in enumerate((d_wr, d_wk, d_wv, d_wg)) for ic in range(NCH)]
          + [(W1B[:, ic * 160:(ic + 1) * 160], d_w1[ic * 128:(ic + 1) * 128, :])
             for ic in range(NCH)]
          + [(W2B[0:32, f * D:(f + 1) * D], d_w2[f * 32:(f + 1) * 32, :])
             for f in range(5)]
          + [(TD1[:, ic * 64:(ic + 1) * 64], d_tdw1[ic * 128:(ic + 1) * 128, :])
             for ic in range(NCH)]
          + [(TD2[0:64, :], d_tdw2[:])], bar=True)

    def _v0(v):
        b.inc(v.memset(ONES[:], 1.0), "V", 1)
    b.on("vector", _v0)

    def _s0(e):
        b.inc(e.activation(HDBE[:], HDBE[:], AF.Exp), "S", 1)
    b.on("scalar", _s0)
    s0_ = b.cnt["S"]

    def _v1(v, s0_=s0_):
        v.wait_ge(b.sems["S"], s0_)
        b.inc(v.tensor_scalar(HDBE[:], HDBE[:], -1.0 / 64.0, None, ALU.mult), "V", 1)
    b.on("vector", _v1, bar=False)

    MAAX, MAAW, MAAK, MAAV, MAAR, MAAG, TDC = [
        (lambda j, o=o: VEC[:, o * 8 + j:o * 8 + j + 1]) for o in range(7)]

    for (toff, tw) in tiles:
        WS = TW + 2
        b.dma("sync", [(XT[:, ic * WS:ic * WS + tw + 2],
                        x2src[ic * 128:(ic + 1) * 128, FHM1 + toff:FHM1 + toff + tw + 2])
                       for ic in range(NCH)], bar=True)

        def _vd(v, tw=tw, WS=WS, toff=toff):
            if toff == 0:
                for ic in range(NCH):
                    b.inc(v.tensor_scalar(XT[:, ic * WS:ic * WS + 1],
                                          XT[:, ic * WS:ic * WS + 1],
                                          EMS[:, 0:1], None, ALU.mult), "V", 1)
            if toff + tw == RPC:
                for ic in range(NCH):
                    b.inc(v.tensor_scalar(XT[:, ic * WS + tw + 1:ic * WS + tw + 2],
                                          XT[:, ic * WS + tw + 1:ic * WS + tw + 2],
                                          EMS[:, 1:2], None, ALU.mult), "V", 1)
            for ic in range(NCH):
                xs = XT[:, ic * WS + 1:ic * WS + 1 + tw]
                xp = XT[:, ic * WS:ic * WS + tw]
                xn = XT[:, ic * WS + 2:ic * WS + 2 + tw]
                dx = DX[:, ic * tw:(ic + 1) * tw]
                b.inc(v.tensor_tensor(dx, xp, xn, ALU.add), "V", 1)
                b.inc(v.scalar_tensor_tensor(dx, dx, 0.5, xs, ALU.mult, ALU.subtract),
                      "V", 1)
                t = TT_[:, 0:tw]
                b.inc(v.tensor_scalar(t, dx, MAAX(ic), None, ALU.mult), "V", 1)
                b.inc(v.tensor_tensor(XXX[:, ic * tw:(ic + 1) * tw], t, xs, ALU.add),
                      "V", 1)
        b.on("vector", _vd)

        t0 = b.cnt["T"]

        def _tm(t, tw=tw):
            for ic in range(NCH):
                inst = t.matmul(psA[:, 0:tw], W1B[:, ic * 160:ic * 160 + 128],
                                XXX[:, ic * tw:(ic + 1) * tw],
                                start=(ic == 0), stop=(ic == NCH - 1))
            b.inc(inst, "T", 1)
            for ic in range(NCH):
                inst = t.matmul(psB[0:32, 0:tw], W1B[:, ic * 160 + 128:ic * 160 + 160],
                                XXX[:, ic * tw:(ic + 1) * tw],
                                start=(ic == 0), stop=(ic == NCH - 1))
            b.inc(inst, "T", 1)
        b.on("tensor", _tm)
        tm = b.cnt["T"]

        def _sm(e, tw=tw, tm=tm):
            e.wait_ge(b.sems["T"], tm)
            b.inc(e.activation(M5[0:32, 0 * TW:0 * TW + tw], psA[0:32, 0:tw], AF.Tanh), "S", 1)
            b.inc(e.activation(M5[0:32, 1 * TW:1 * TW + tw], psA[32:64, 0:tw], AF.Tanh), "S", 1)
            b.inc(e.activation(MBT[0:64, 0:tw], psA[64:128, 0:tw], AF.Tanh), "S", 1)
            b.inc(e.activation(M5[0:32, 4 * TW:4 * TW + tw], psB[0:32, 0:tw], AF.Tanh), "S", 1)
        b.on("scalar", _sm, bar=False)
        sm_ = b.cnt["S"]

        def _vmx(v, tw=tw, sm_=sm_):
            v.wait_ge(b.sems["S"], sm_)
            b.inc(v.tensor_copy(M5[0:32, 2 * TW:2 * TW + tw], MBT[0:32, 0:tw]), "V", 1)
            b.inc(v.tensor_copy(M5[0:32, 3 * TW:3 * TW + tw], MBT[32:64, 0:tw]), "V", 1)
        b.on("vector", _vmx, bar=False)

        maas = [MAAW, MAAK, MAAV, MAAR, MAAG]
        for f in range(5):
            t1 = b.cnt["T"]
            v1 = b.cnt["V"]

            def _tf(t, f=f, tw=tw, v1=v1):
                for oc in range(NCH):
                    bank = (psC, psD)[oc % 2]
                    if oc >= 2:
                        t.wait_ge(b.sems["V"], v1 + 3 * (oc - 1))
                    b.inc(t.matmul(bank[:, 0:tw],
                                   W2B[0:32, f * D + oc * 128: f * D + (oc + 1) * 128],
                                   M5[0:32, f * TW:f * TW + tw],
                                   start=True, stop=True), "T", 1)
            b.on("tensor", _tf)

            def _vf(v, f=f, tw=tw, t1=t1):
                for oc in range(NCH):
                    v.wait_ge(b.sems["T"], t1 + oc + 1)
                    bank = (psC, psD)[oc % 2][:, 0:tw]
                    t_ = TT_[:, 0:tw]
                    b.inc(v.tensor_scalar(t_, bank, maas[f](oc), None, ALU.add),
                          "V", 1)
                    b.inc(v.tensor_tensor(t_, t_, DX[:, oc * tw:(oc + 1) * tw],
                                          ALU.mult), "V", 1)
                    b.inc(v.tensor_tensor(
                        X5[:, (f * NCH + oc) * tw:(f * NCH + oc) * tw + tw],
                        t_, XT[:, oc * (TW + 2) + 1:oc * (TW + 2) + 1 + tw],
                        ALU.add), "V", 1)
            b.on("vector", _vf, bar=False)

        for (name, f, wslot, oap, act) in (
                ("r", 3, 0, o_r, "scale8"), ("k", 1, 1, o_k, None),
                ("v", 2, 2, o_v, None), ("g", 4, 3, o_g, "silu")):
            def xf(ic, f=f, tw=tw):
                return X5[:, (f * NCH + ic) * tw:(f * NCH + ic) * tw + tw]

            if act == "silu":
                def ev(oc, ps_ap, e, tw=tw):
                    return e.activation(GT[:, oc * tw:(oc + 1) * tw], ps_ap, AF.Silu)
            elif act == "scale8":
                def ev(oc, ps_ap, e, tw=tw):
                    return e.activation(OUTT[:, oc * tw:(oc + 1) * tw], ps_ap,
                                        AF.Copy, scale=0.125)
            else:
                def ev(oc, ps_ap, e, tw=tw):
                    return e.copy(OUTT[:, oc * tw:(oc + 1) * tw], ps_ap)
            proj(b, WPROJ[:, wslot * NCH * D:(wslot + 1) * NCH * D] if False else WPROJ,
                 0, xf, list(range(NCH)), 0, 0, (psA, psB), "scalar", ev) if False                     else None
            t2 = b.cnt["T"]
            s2 = b.cnt["S"]
            gd2 = b.cnt["GD"]

            def _tp(t, f=f, wslot=wslot, tw=tw, s2=s2):
                for oc in range(NCH):
                    bank = (psA, psB)[oc % 2]
                    if oc >= 2:
                        t.wait_ge(b.sems["S"], s2 + (oc - 1))
                    for ic in range(NCH):
                        inst = t.matmul(
                            bank[:, 0:tw],
                            WPROJ[:, (wslot * NCH + ic) * D + oc * 128:
                                  (wslot * NCH + ic) * D + (oc + 1) * 128],
                            X5[:, (f * NCH + ic) * tw:(f * NCH + ic) * tw + tw],
                            start=(ic == 0), stop=(ic == NCH - 1))
                    b.inc(inst, "T", 1)
            b.on("tensor", _tp)

            def _sp(e, ev=ev, tw=tw, t2=t2, gd2=gd2):
                if gd2 > 0:
                    e.wait_ge(b.sems["GD"], gd2)
                for oc in range(NCH):
                    e.wait_ge(b.sems["T"], t2 + oc + 1)
                    b.inc(ev(oc, (psA, psB)[oc % 2][:, 0:tw], e), "S", 1)
            b.on("scalar", _sp, bar=False)
            if name == "g":
                b.dma("gpsimd", [(o_g[ic * 128:(ic + 1) * 128, toff:toff + tw],
                                  GT[:, ic * tw:(ic + 1) * tw]) for ic in range(NCH)],
                      bar=True)
            else:
                b.dma("gpsimd", [(oap[ic * 128:(ic + 1) * 128, toff:toff + tw],
                                  OUTT[:, ic * tw:(ic + 1) * tw]) for ic in range(NCH)],
                      bar=True)

        t3 = b.cnt["T"]

        def _tw(t, tw=tw):
            for ic in range(NCH):
                inst = t.matmul(psC[0:64, 0:tw], TD1[:, ic * 64:(ic + 1) * 64],
                                X5[:, (0 * NCH + ic) * tw:(0 * NCH + ic) * tw + tw],
                                start=(ic == 0), stop=(ic == NCH - 1))
            b.inc(inst, "T", 1)
        b.on("tensor", _tw)
        tw3 = b.cnt["T"]

        def _sw(e, tw=tw, tw3=tw3):
            e.wait_ge(b.sems["T"], tw3)
            b.inc(e.activation(TWB[0:64, 0:tw], psC[0:64, 0:tw], AF.Tanh), "S", 1)
        b.on("scalar", _sw, bar=False)
        sw3 = b.cnt["S"]

        for oc in range(NCH):
            def _tw2(t, oc=oc, tw=tw, sw3=sw3):
                t.wait_ge(b.sems["S"], sw3)
                b.inc(t.matmul(psD[:, 0:tw], TD2[0:64, oc * 128:(oc + 1) * 128],
                               TWB[0:64, 0:tw], start=True, stop=True), "T", 1)
            b.on("tensor", _tw2)
            tc = b.cnt["T"]

            def _se(e, oc=oc, tw=tw, tc=tc):
                e.wait_ge(b.sems["T"], tc)
                b.inc(e.activation(EXPT[:, 0:tw], psD[:, 0:tw], AF.Exp,
                                   bias=TDC(oc)), "S", 1)
            b.on("scalar", _se, bar=False)
            se_ = b.cnt["S"]
            def _ve(v, oc=oc, tw=tw, se_=se_):
                v.wait_ge(b.sems["S"], se_)
                b.inc(v.tensor_copy(EXPT2[0:64, 0:tw], EXPT[64:128, 0:tw]), "V", 1)
            b.on("vector", _ve, bar=False)
            ve_ = b.cnt["V"]

            def _tm2(t, oc=oc, tw=tw, se_=se_, ve_=ve_):
                t.wait_ge(b.sems["S"], se_)
                t.wait_ge(b.sems["V"], ve_)
                b.inc(t.matmul(psC[0:1, 0:tw], ONES[0:64, 1:2], EXPT[0:64, 0:tw],
                               start=True, stop=True), "T", 1)
                b.inc(t.matmul(psB[0:1, 0:tw], ONES[0:64, 2:3], EXPT2[0:64, 0:tw],
                               start=True, stop=True), "T", 1)
            b.on("tensor", _tm2, bar=False)
            tm2 = b.cnt["T"]

            def _vm(v, oc=oc, tw=tw, tm2=tm2):
                v.wait_ge(b.sems["T"], tm2)
                b.inc(v.tensor_scalar(MWT[0:1, 0:tw], psC[0:1, 0:tw],
                                      HDBE[0:1, 2 * oc:2 * oc + 1], None, ALU.mult),
                      "V", 1)
                b.inc(v.tensor_scalar(MWT[0:1, TW:TW + tw], psB[0:1, 0:tw],
                                      HDBE[0:1, 2 * oc + 1:2 * oc + 2], None, ALU.mult),
                      "V", 1)
            b.on("vector", _vm, bar=True)
            b.dma("gpsimd", [(o_mw[2 * oc:2 * oc + 1, toff:toff + tw],
                              MWT[0:1, 0:tw]),
                             (o_mw[2 * oc + 1:2 * oc + 2, toff:toff + tw],
                              MWT[0:1, TW:TW + tw])], bar=True)
    finL = b.cnt["LD"]
    finG = b.cnt["GD"]

    def _finp(e, finL=finL, finG=finG):
        e.wait_ge(b.sems["LD"], finL)
        e.wait_ge(b.sems["GD"], finG)
    b.on("gpsimd", _finp, bar=False)


def emit_l1post(nc, b, ctx, d_y, d_ident):
    d_g = nc.dram_tensor("g", [D, RPC], F32, kind="ExternalInput").ap()
    d_x2 = nc.dram_tensor("x2", [D, RPC], F32, kind="ExternalInput").ap()
    d_wo = nc.dram_tensor("wo", [D, D], F32, kind="ExternalInput").ap()
    d_gnv = nc.dram_tensor("gnv", [128, 16], F32, kind="ExternalInput").ap()
    d_w1 = nc.dram_tensor("w1", [D, 4 * D], F32, kind="ExternalInput").ap()
    d_w2 = nc.dram_tensor("w2", [4 * D, D], F32, kind="ExternalInput").ap()
    d_fv = nc.dram_tensor("fv", [128, 56], F32, kind="ExternalInput").ap()
    x_out = nc.dram_tensor("x_out", [D, RPC], F32, kind="ExternalOutput").ap()

    NT = 8
    W = RPC // NT
    GN_EPS = 1e-5 * 64.0

    WO = b.take("WO", [128, NCH * D], BF16)
    W1B = b.take("W1B", [128, NCH * 4 * D], BF16)
    W2B = b.take("W2B", [128, 32 * D], BF16)
    Y = b.take("Y", [128, NCH * W], F32)
    G = b.take("G", [128, NCH * W], F32)
    X2 = b.take("X2", [128, NCH * W], F32)
    X3 = b.take("X3", [128, NCH * W], F32)
    YN = b.take("YN", [128, NCH * W], BF16)
    H1 = b.take("H1", [128, 32 * W], BF16)
    YL2 = b.take("YL2", [64, W], F32)
    SQU = b.take("SQU", [128, 2 * W], F32)
    MROW = b.take("MROW", [1, 256], F32)
    ONES = b.take("ONES", [128, 128], F32)
    VEC = b.take("VEC", [128, 16], F32)
    FV = b.take("FV", [128, 56], F32)
    ROWS_ = tuple(b.take(f"ROW{i}", [1, 512], F32) for i in range(2))
    SQ2 = b.take("SQ2", [128, 2 * W], F32)
    MUR = ROWS_[0]
    RSR = ROWS_[1]
    TRW = SQ2

    psA = ctx.enter_context(nc.psum_tensor("psA", [128, 512], F32))
    psB = ctx.enter_context(nc.psum_tensor("psB", [128, 512], F32))
    psC = ctx.enter_context(nc.psum_tensor("psC", [128, 512], F32))
    psD = ctx.enter_context(nc.psum_tensor("psD", [128, 512], F32))
    psSum = ctx.enter_context(nc.psum_tensor("psSum", [128, 512], F32))
    psSq = ctx.enter_context(nc.psum_tensor("psSq", [128, 512], F32))
    psMu = ctx.enter_context(nc.psum_tensor("psMu", [128, 512], F32))
    psRstd = ctx.enter_context(nc.psum_tensor("psRstd", [128, 512], F32))
    LNPS = {"sum": psSum, "sq": psSq, "mu": psMu, "rstd": psRstd}

    b.dma("sync", [(VEC[:], d_gnv[:]), (FV[:], d_fv[:])], bar=True)
    b.dma("gpsimd", [(WO[:, ic * D:(ic + 1) * D], d_wo[ic * 128:(ic + 1) * 128, :])
                     for ic in range(NCH)]
          + [(W1B[:, ic * 4 * D:(ic + 1) * 4 * D],
              d_w1[ic * 128:(ic + 1) * 128, :]) for ic in range(NCH)]
          + [(W2B[:, ic * D:(ic + 1) * D], d_w2[ic * 128:(ic + 1) * 128, :])
             for ic in range(32)], bar=True)

    def _v0(v):
        b.inc(v.memset(ONES[:], 1.0), "V", 1)
        b.inc(v.memset(MROW[0:1, 0:64], 1.0), "V", 1)
        b.inc(v.memset(MROW[0:1, 64:128], 0.0), "V", 1)
        b.inc(v.memset(MROW[0:1, 128:192], 0.0), "V", 1)
        b.inc(v.memset(MROW[0:1, 192:256], 1.0), "V", 1)
    b.on("vector", _v0)

    def GNG(j):
        return VEC[:, j:j + 1]

    def GNB(j):
        return VEC[:, 8 + j:8 + j + 1]

    def FB1(j):
        return FV[:, j:j + 1]

    def FB2(j):
        return FV[:, 32 + j:32 + j + 1]

    def FG(j):
        return FV[:, 40 + j:40 + j + 1]

    def FBt(j):
        return FV[:, 48 + j:48 + j + 1]

    eps_gn = b.ccol(GN_EPS)

    for ti in range(NT):
        toff = ti * W
        b.dma("sync", [(Y[:, ic * W:(ic + 1) * W],
                        d_y[ic * 128:(ic + 1) * 128, toff:toff + W])
                       for ic in range(NCH)]
              + [(G[:, ic * W:(ic + 1) * W],
                  d_g[ic * 128:(ic + 1) * 128, toff:toff + W]) for ic in range(NCH)]
              + [(X2[:, ic * W:(ic + 1) * W],
                  d_x2[ic * 128:(ic + 1) * 128, toff:toff + W]) for ic in range(NCH)],
              bar=True)

        for c in range(NCH):
            yc_ = Y[:, c * W:(c + 1) * W]

            def _vg(v, c=c, yc_=yc_):
                b.inc(v.tensor_copy(YL2[0:64, 0:W], yc_[64:128, 0:W]), "V", 1)
            b.on("vector", _vg)
            vg = b.cnt["V"]

            def _sg(e, c=c, yc_=yc_, vg=vg):
                e.wait_ge(b.sems["V"], vg)
                b.inc(e.activation(SQU[0:64, 0:W], yc_[0:64, 0:W], AF.Square), "S", 1)
                b.inc(e.activation(SQU[0:64, W:2 * W], YL2[0:64, 0:W], AF.Square),
                      "S", 1)
            b.on("scalar", _sg, bar=False)
            sg = b.cnt["S"]

            def _tg(t, c=c, yc_=yc_, sg=sg):
                t.wait_ge(b.sems["S"], sg)
                b.inc(t.matmul(psSum[0:1, 0:W], ONES[0:64, 0:1], yc_[0:64, 0:W],
                               start=True, stop=True), "T", 1)
                b.inc(t.matmul(psSum[0:1, W:2 * W], ONES[0:64, 0:1], YL2[0:64, 0:W],
                               start=True, stop=True), "T", 1)
                b.inc(t.matmul(psSq[0:1, 0:W], ONES[0:64, 0:1], SQU[0:64, 0:W],
                               start=True, stop=True), "T", 1)
                b.inc(t.matmul(psSq[0:1, W:2 * W], ONES[0:64, 0:1],
                               SQU[0:64, W:2 * W], start=True, stop=True), "T", 1)
            b.on("tensor", _tg, bar=False)
            tg = b.cnt["T"]

            def _vs(v, tg=tg):
                v.wait_ge(b.sems["T"], tg)
                b.inc(v.tensor_scalar(MUR[0:1, 0:2 * W], psSum[0:1, 0:2 * W],
                                      1.0 / 64, None, ALU.mult), "V", 1)
                b.inc(v.tensor_scalar(RSR[0:1, 0:2 * W], psSq[0:1, 0:2 * W],
                                      1.0 / 64, None, ALU.mult), "V", 1)
                b.inc(v.tensor_tensor(TRW[0:1, 0:2 * W], MUR[0:1, 0:2 * W],
                                      MUR[0:1, 0:2 * W], ALU.mult), "V", 1)
                b.inc(v.tensor_tensor(RSR[0:1, 0:2 * W], RSR[0:1, 0:2 * W],
                                      TRW[0:1, 0:2 * W], ALU.subtract), "V", 1)
            b.on("vector", _vs, bar=False)
            vs = b.cnt["V"]

            def _ss(e, vs=vs):
                e.wait_ge(b.sems["V"], vs)
                b.inc(e.activation(RSR[0:1, 0:2 * W], RSR[0:1, 0:2 * W], AF.Sqrt,
                                   bias=eps_gn[0:1, 0:1]), "S", 1)
            b.on("scalar", _ss, bar=False)
            ss = b.cnt["S"]

            def _vr(v, ss=ss):
                v.wait_ge(b.sems["S"], ss)
                b.inc(v.reciprocal(RSR[0:1, 0:2 * W], RSR[0:1, 0:2 * W]), "V", 1)
            b.on("vector", _vr, bar=False)
            vr = b.cnt["V"]

            def _tb(t, vr=vr):
                t.wait_ge(b.sems["V"], vr)
                t.matmul(psMu[:, 0:W], MROW[0:1, 0:128], MUR[0:1, 0:W],
                         start=True, stop=False)
                b.inc(t.matmul(psMu[:, 0:W], MROW[0:1, 128:256], MUR[0:1, W:2 * W],
                               start=False, stop=True), "T", 1)
                t.matmul(psRstd[:, 0:W], MROW[0:1, 0:128], RSR[0:1, 0:W],
                         start=True, stop=False)
                b.inc(t.matmul(psRstd[:, 0:W], MROW[0:1, 128:256], RSR[0:1, W:2 * W],
                               start=False, stop=True), "T", 1)
            b.on("tensor", _tb, bar=False)
            tbx = b.cnt["T"]

            def _vn(v, c=c, yc_=yc_, tbx=tbx):
                v.wait_ge(b.sems["T"], tbx)
                b.inc(v.tensor_tensor(yc_, yc_, psMu[:, 0:W], ALU.subtract), "V", 1)
                b.inc(v.tensor_tensor(yc_, yc_, psRstd[:, 0:W], ALU.mult), "V", 1)
                b.inc(v.tensor_scalar(yc_, yc_, GNG(c), GNB(c), ALU.mult, ALU.add),
                      "V", 1)
                b.inc(v.tensor_tensor(YN[:, c * W:(c + 1) * W], yc_,
                                      G[:, c * W:(c + 1) * W], ALU.mult), "V", 1)
            b.on("vector", _vn, bar=False)

        t0 = b.cnt["T"]
        v0 = b.cnt["V"]

        def _tw(t, v0=v0):
            for oc in range(NCH):
                bank = (psA, psB)[oc % 2]
                if oc >= 2:
                    t.wait_ge(b.sems["V"], v0 + (oc - 1))
                for ic in range(NCH):
                    inst = t.matmul(bank[:, 0:W],
                                    WO[:, ic * D + oc * 128: ic * D + (oc + 1) * 128],
                                    YN[:, ic * W:(ic + 1) * W],
                                    start=(ic == 0), stop=(ic == NCH - 1))
                b.inc(inst, "T", 1)
        b.on("tensor", _tw)

        def _vw(v, t0=t0):
            for oc in range(NCH):
                v.wait_ge(b.sems["T"], t0 + oc + 1)
                xap = X2[:, oc * W:(oc + 1) * W]
                b.inc(v.tensor_tensor(xap, (psA, psB)[oc % 2][:, 0:W], xap, ALU.add),
                      "V", 1)
        b.on("vector", _vw, bar=False)

        def x2fn(ic):
            return X2[:, ic * W:(ic + 1) * W]

        def to_x3(ic, v, normed):
            b.inc(v.tensor_copy(X3[:, ic * W:(ic + 1) * W], normed), "V", 1)
        layernorm(b, x2fn, W, LNPS, ROWS_, SQ2, ONES, to_x3)

        def _vb(v):
            for ic in range(NCH):
                b.inc(v.tensor_copy(YN[:, ic * W:(ic + 1) * W],
                                    X3[:, ic * W:(ic + 1) * W]), "V", 1)
        b.on("vector", _vb)

        t1 = b.cnt["T"]
        s1 = b.cnt["S"]

        def _tf(t, s1=s1):
            for oc in range(32):
                bank = (psA, psB)[oc % 2]
                if oc >= 2:
                    t.wait_ge(b.sems["S"], s1 + (oc - 1))
                for ic in range(NCH):
                    inst = t.matmul(bank[:, 0:W],
                                    W1B[:, ic * 4 * D + oc * 128:
                                        ic * 4 * D + (oc + 1) * 128],
                                    YN[:, ic * W:(ic + 1) * W],
                                    start=(ic == 0), stop=(ic == NCH - 1))
                b.inc(inst, "T", 1)
        b.on("tensor", _tf)

        def _sf(e, t1=t1):
            for oc in range(32):
                e.wait_ge(b.sems["T"], t1 + oc + 1)
                b.inc(e.activation(H1[:, oc * W:(oc + 1) * W],
                                   (psA, psB)[oc % 2][:, 0:W], AF.Relu,
                                   bias=FB1(oc)), "S", 1)
        b.on("scalar", _sf, bar=False)

        t2 = b.cnt["T"]
        v2 = b.cnt["V"]

        def _tf2(t, v2=v2):
            for oc in range(NCH):
                bank = (psC, psD)[oc % 2]
                if oc >= 2:
                    t.wait_ge(b.sems["V"], v2 + (oc - 1))
                for ic in range(32):
                    inst = t.matmul(bank[:, 0:W],
                                    W2B[:, ic * D + oc * 128: ic * D + (oc + 1) * 128],
                                    H1[:, ic * W:(ic + 1) * W],
                                    start=(ic == 0), stop=(ic == 31))
                b.inc(inst, "T", 1)
        b.on("tensor", _tf2)

        def _vf2(v, t2=t2):
            for oc in range(NCH):
                v.wait_ge(b.sems["T"], t2 + oc + 1)
                xap = X2[:, oc * W:(oc + 1) * W]
                b.inc(v.scalar_tensor_tensor(xap, (psC, psD)[oc % 2][:, 0:W],
                                             FB2(oc), X3[:, oc * W:(oc + 1) * W],
                                             ALU.add, ALU.add), "V", 1)
        b.on("vector", _vf2, bar=False)

        def x4fn(ic):
            return X2[:, ic * W:(ic + 1) * W]

        def to_v(ic, v, normed):
            b.inc(v.tensor_scalar(Y[:, ic * W:(ic + 1) * W], normed,
                                  FG(ic), FBt(ic), ALU.mult, ALU.add), "V", 1)
        layernorm(b, x4fn, W, LNPS, ROWS_, SQ2, ONES, to_v)

        def _vu(v):
            for ic in range(NCH):
                b.inc(v.tensor_tensor(Y[:, ic * W:(ic + 1) * W],
                                      Y[:, ic * W:(ic + 1) * W],
                                      X3[:, ic * W:(ic + 1) * W], ALU.add), "V", 1)
        b.on("vector", _vu)

        def yfn(ic):
            return Y[:, ic * W:(ic + 1) * W]

        def to_out(ic, v, normed):
            b.inc(v.tensor_copy(X3[:, ic * W:(ic + 1) * W], normed), "V", 1)
        layernorm(b, yfn, W, LNPS, ROWS_, SQ2, ONES, to_out)

        b.dma("gpsimd", [(x_out[ic * 128:(ic + 1) * 128, toff:toff + W],
                          X3[:, ic * W:(ic + 1) * W]) for ic in range(NCH)],
              bar=True)

    fin = b.cnt["GD"]

    def _fin(e, fin=fin):
        e.wait_ge(b.sems["GD"], fin)
    b.on("gpsimd", _fin, bar=False)


def _cm(v):
    return np.ascontiguousarray(np.asarray(v, np.float32).reshape(-1, 128).T)


def _pack_l0_vecs(inp):
    cols = []
    for i in range(L0L):
        abp = np.asarray(inp["L0_attn_attn_b"][i]).reshape(16, 3, 64).transpose(1, 0, 2).reshape(3072)
        cols += [_cm(inp["L0_attn_ln1_w"][i]), _cm(inp["L0_attn_ln1_b"][i]),
                 _cm(abp), _cm(inp["L0_attn_aproj_b"][i]),
                 _cm(inp["L0_attn_fc_b"][i]), _cm(inp["L0_attn_mproj_b"][i]),
                 _cm(inp["L0_attn_ln2_w"][i]), _cm(inp["L0_attn_ln2_b"][i])]
    return np.ascontiguousarray(np.concatenate(cols, axis=1).astype(np.float32))


_PROGS = {}


def run_l0(inp, x_fm_full):
    C = host_consts()
    vecs = _pack_l0_vecs(inp)
    aw_p = [np.ascontiguousarray(
        np.asarray(inp["L0_attn_attn_w"][i]).reshape(D, 16, 3, 64).transpose(0, 2, 1, 3)
        .reshape(D, 3 * D).astype(np.float32)) for i in range(L0L)]
    if "l0" not in _PROGS:
        _PROGS["l0"] = build_l0(WTOK)
    in_maps = []
    for c in range(NCORES):
        m = {"x_in": core_slice_fm(x_fm_full, c), "vecs": vecs,
             "Ar": C["Ar"], "Ai": C["Ai"], "ident": C["ident"],
             "filt_r": core_filt_slice(C["filt_r"], c),
             "filt_i": core_filt_slice(C["filt_i"], c),
             "ffn_W": np.asarray(inp["L0_ffn_W"], np.float32),
             "lora_down": np.asarray(inp["L0_ffn_lora_down"], np.float32),
             "lora_up": np.asarray(inp["L0_ffn_lora_up"], np.float32),
             "down_aux": np.asarray(inp["L0_ffn_down_aux"], np.float32),
             "up_aux": np.asarray(inp["L0_ffn_up_aux"], np.float32),
             "ffn_b": _cm(inp["L0_ffn_b"])}
        for i in range(L0L):
            m[f"aw{i}"] = aw_p[i]
            m[f"pw{i}"] = np.asarray(inp["L0_attn_aproj_w"][i], np.float32)
            m[f"fw{i}"] = np.asarray(inp["L0_attn_fc_w"][i], np.float32)
            m[f"mw{i}"] = np.asarray(inp["L0_attn_mproj_w"][i], np.float32)
        in_maps.append(m)
    res = run_bass_kernel_spmd(_PROGS["l0"], in_maps, core_ids=list(range(NCORES)))
    return [np.asarray(res.results[c]["x_out"]) for c in range(NCORES)]


def run_l1pre(inp, x2s):
    cols = [_cm(np.asarray(inp[f"L1_attn_time_maa_{n}"]).reshape(D))
            for n in ("x", "w", "k", "v", "r", "g")]
    cols.append(_cm(np.asarray(inp["L1_attn_time_decay"]).reshape(D)))
    cols.append(np.zeros((128, 8), np.float32))
    vecs = np.ascontiguousarray(np.concatenate(cols, axis=1))
    hdb = np.ascontiguousarray(np.asarray(inp["L1_attn_head_decay_bias"],
                                          np.float32).reshape(1, 16))
    if "l1pre" not in _PROGS:
        _PROGS["l1pre"] = build_l1pre()
    in_maps = []
    for c in range(NCORES):
        x2 = x2s[c]
        prev = (np.zeros((D, 1), np.float32) if c in (0, 4)
                else np.ascontiguousarray(x2[:, FH - 1:FH]))
        nxt = (np.zeros((D, 1), np.float32) if c in (3, 7)
               else np.ascontiguousarray(x2[:, FH + RPC:FH + RPC + 1]))
        x2p = np.ascontiguousarray(
            np.concatenate([prev, x2[:, FH:FH + RPC], nxt], axis=1))
        m = {"x2p": x2p, "vecs": vecs, "hdb": hdb,
             "W_r": np.asarray(inp["L1_attn_W_r"], np.float32),
             "W_k": np.asarray(inp["L1_attn_W_k"], np.float32),
             "W_v": np.asarray(inp["L1_attn_W_v"], np.float32),
             "W_g": np.asarray(inp["L1_attn_W_g"], np.float32),
             "maa_w1": np.asarray(inp["L1_attn_time_maa_w1"], np.float32),
             "maa_w2": np.ascontiguousarray(
                 np.asarray(inp["L1_attn_time_maa_w2"], np.float32).reshape(160, D)),
             "tdw1": np.asarray(inp["L1_attn_time_decay_w1"], np.float32),
             "tdw2": np.asarray(inp["L1_attn_time_decay_w2"], np.float32)}
        in_maps.append(m)
    res = run_bass_kernel_spmd(_PROGS["l1pre"], in_maps, core_ids=list(range(NCORES)))
    return [res.results[c] for c in range(NCORES)]


def run_l1attn(inp, pre):
    C = host_consts()
    slow_row = np.ascontiguousarray(
        np.asarray(inp["L1_attn_slow_scale"], np.float32).reshape(1, 16))
    alpha2 = np.ascontiguousarray(
        np.asarray(inp["L1_attn_decay_mix"], np.float32).reshape(16, 64).T)
    kb = {0: np.concatenate([np.asarray(pre[c]["k_bf"]) for c in range(4)], axis=1),
          1: np.concatenate([np.asarray(pre[c]["k_bf"]) for c in range(4, 8)], axis=1)}
    vb = {0: np.concatenate([np.asarray(pre[c]["v_bf"]) for c in range(4)], axis=1),
          1: np.concatenate([np.asarray(pre[c]["v_bf"]) for c in range(4, 8)], axis=1)}
    mwb = {0: np.concatenate([np.asarray(pre[c]["meanw"]) for c in range(4)], axis=1),
           1: np.concatenate([np.asarray(pre[c]["meanw"]) for c in range(4, 8)], axis=1)}
    ys = [None] * NCORES
    for g in range(4):
        soff = g * RPC
        key = f"l1attn{g}"
        if key not in _PROGS:
            _PROGS[key] = build_l1attn(soff)
        cids = [g, g + 4]
        in_maps = []
        for c in cids:
            bat = c // 4
            in_maps.append({"r_bf": np.asarray(pre[c]["r_bf"]), "kAll": kb[bat],
                            "vAll": vb[bat], "mwAll": mwb[bat],
                            "slow_row": slow_row, "alpha2": alpha2,
                            "ident": C["ident"]})
        res = run_bass_kernel_spmd(_PROGS[key], in_maps, core_ids=cids)
        for i, c in enumerate(cids):
            ys[c] = np.asarray(res.results[i]["y_attn"])
    return ys


def rot_order(c):
    q = c % 4
    base = 4 * (c // 4)
    return [base + (q + d) % 4 for d in range(4)]


def corr_row(c):
    q = c % 4

    def svec(o):
        d = (o - q) % 4
        v = np.zeros(4)
        v[d] += 1
        if d >= 1:
            v[d - 1] -= 1
        return v

    def base_true(o):
        t = np.zeros(4)
        for o2 in range(o):
            t += svec(o2)
        return t

    M = np.zeros((4, 4), np.float64)
    for d in range(1, 4):
        o = (q + d) % 4
        row = base_true(o) - base_true(q)
        row[d - 1] -= 1.0
        M[d] = row
    return np.ascontiguousarray(M.reshape(1, 16).astype(np.float32))


def slob_row(c):
    q = c % 4
    r = np.zeros((1, 8), np.float32)
    for d in range(1, 4):
        o = (q + d) % 4
        if o < q:
            r[0, d] = 1.0
        else:
            r[0, 4 + d] = 1.0
    return r


def run_l1attn_rot(inp, pre):
    C = host_consts()
    slow_row = np.ascontiguousarray(
        np.asarray(inp["L1_attn_slow_scale"], np.float32).reshape(1, 16))
    alpha2 = np.ascontiguousarray(
        np.asarray(inp["L1_attn_decay_mix"], np.float32).reshape(16, 64).T)
    if "l1attn_rot" not in _PROGS:
        _PROGS["l1attn_rot"] = build_l1attn_rot()
    in_maps = []
    for c in range(NCORES):
        order = rot_order(c)
        in_maps.append({
            "r_bf": np.asarray(pre[c]["r_bf"]),
            "kAll": np.ascontiguousarray(np.concatenate(
                [np.asarray(pre[o]["k_bf"]) for o in order], axis=1)),
            "vAll": np.ascontiguousarray(np.concatenate(
                [np.asarray(pre[o]["v_bf"]) for o in order], axis=1)),
            "mwAll": np.ascontiguousarray(np.concatenate(
                [np.asarray(pre[o]["meanw"]) for o in order], axis=1)),
            "slow_row": slow_row, "alpha2": alpha2, "ident": C["ident"],
            "corr4": corr_row(c), "slob": slob_row(c)})
    res = run_bass_kernel_spmd(_PROGS["l1attn_rot"], in_maps,
                               core_ids=list(range(NCORES)))
    return [np.asarray(res.results[c]["y_attn"]) for c in range(NCORES)]


def run_l1post(inp, x2s, pre, ys):
    C = host_consts()
    gnv = np.ascontiguousarray(np.concatenate(
        [_cm(inp["L1_attn_ln_gamma"]), _cm(inp["L1_attn_ln_beta"])], axis=1))
    fv = np.ascontiguousarray(np.concatenate(
        [_cm(np.asarray(inp["L1_ffn_b1"][0])), _cm(np.asarray(inp["L1_ffn_b2"][0])),
         _cm(inp["L1_ffn_gamma"]), _cm(inp["L1_ffn_beta"])], axis=1))
    if "l1post" not in _PROGS:
        _PROGS["l1post"] = build_l1post()
    in_maps = []
    for c in range(NCORES):
        in_maps.append({
            "y_attn": ys[c], "g": np.asarray(pre[c]["g"]),
            "x2": np.ascontiguousarray(x2s[c][:, FH:FH + RPC]),
            "wo": np.asarray(inp["L1_attn_W_o"], np.float32),
            "gnv": gnv, "fv": fv,
            "w1": np.asarray(inp["L1_ffn_w1"][0], np.float32),
            "w2": np.asarray(inp["L1_ffn_w2"][0], np.float32),
            "ident": C["ident"]})
    res = run_bass_kernel_spmd(_PROGS["l1post"], in_maps, core_ids=list(range(NCORES)))
    return [np.asarray(res.results[c]["x_out"]) for c in range(NCORES)]


def run_p1(inp, x_fm_full):
    C = host_consts()
    vecs = _pack_l0_vecs(inp)
    aw_p = [np.ascontiguousarray(
        np.asarray(inp["L0_attn_attn_w"][i]).reshape(D, 16, 3, 64).transpose(0, 2, 1, 3)
        .reshape(D, 3 * D).astype(np.float32)) for i in range(L0L)]
    pcols = [_cm(np.asarray(inp[f"L1_attn_time_maa_{n}"]).reshape(D))
             for n in ("x", "w", "k", "v", "r", "g")]
    pcols.append(_cm(np.asarray(inp["L1_attn_time_decay"]).reshape(D)))
    pcols.append(np.zeros((128, 8), np.float32))
    pvecs = np.ascontiguousarray(np.concatenate(pcols, axis=1))
    hdb = np.ascontiguousarray(np.asarray(inp["L1_attn_head_decay_bias"],
                                          np.float32).reshape(1, 16))
    if "p1" not in _PROGS:
        _PROGS["p1"] = build_l0(WTOK, with_l1pre=True)
    in_maps = []
    for c in range(NCORES):
        em = np.ones((128, 2), np.float32)
        if c in (0, 4):
            em[:, 0] = 0.0
        if c in (3, 7):
            em[:, 1] = 0.0
        m = {"x_in": core_slice_fm(x_fm_full, c), "vecs": vecs,
             "Ar": C["Ar"], "Ai": C["Ai"], "ident": C["ident"],
             "filt_r": core_filt_slice(C["filt_r"], c),
             "filt_i": core_filt_slice(C["filt_i"], c),
             "ffn_W": np.asarray(inp["L0_ffn_W"], np.float32),
             "lora_down": np.asarray(inp["L0_ffn_lora_down"], np.float32),
             "lora_up": np.asarray(inp["L0_ffn_lora_up"], np.float32),
             "down_aux": np.asarray(inp["L0_ffn_down_aux"], np.float32),
             "up_aux": np.asarray(inp["L0_ffn_up_aux"], np.float32),
             "ffn_b": _cm(inp["L0_ffn_b"]),
             "pvecs": pvecs, "hdb": hdb, "em": em,
             "W_r": np.asarray(inp["L1_attn_W_r"], np.float32),
             "W_k": np.asarray(inp["L1_attn_W_k"], np.float32),
             "W_v": np.asarray(inp["L1_attn_W_v"], np.float32),
             "W_g": np.asarray(inp["L1_attn_W_g"], np.float32),
             "maa_w1": np.asarray(inp["L1_attn_time_maa_w1"], np.float32),
             "maa_w2": np.ascontiguousarray(
                 np.asarray(inp["L1_attn_time_maa_w2"], np.float32).reshape(160, D)),
             "tdw1": np.asarray(inp["L1_attn_time_decay_w1"], np.float32),
             "tdw2": np.asarray(inp["L1_attn_time_decay_w2"], np.float32)}
        for i in range(L0L):
            m[f"aw{i}"] = aw_p[i]
            m[f"pw{i}"] = np.asarray(inp["L0_attn_aproj_w"][i], np.float32)
            m[f"fw{i}"] = np.asarray(inp["L0_attn_fc_w"][i], np.float32)
            m[f"mw{i}"] = np.asarray(inp["L0_attn_mproj_w"][i], np.float32)
        in_maps.append(m)
    res = run_bass_kernel_spmd(_PROGS["p1"], in_maps, core_ids=list(range(NCORES)))
    return [res.results[c] for c in range(NCORES)]


def run_p2(inp, p1):
    C = host_consts()
    slow_row = np.ascontiguousarray(
        np.asarray(inp["L1_attn_slow_scale"], np.float32).reshape(1, 16))
    alpha2 = np.ascontiguousarray(
        np.asarray(inp["L1_attn_decay_mix"], np.float32).reshape(16, 64).T)
    gnv = np.ascontiguousarray(np.concatenate(
        [_cm(inp["L1_attn_ln_gamma"]), _cm(inp["L1_attn_ln_beta"])], axis=1))
    fv = np.ascontiguousarray(np.concatenate(
        [_cm(np.asarray(inp["L1_ffn_b1"][0])), _cm(np.asarray(inp["L1_ffn_b2"][0])),
         _cm(inp["L1_ffn_gamma"]), _cm(inp["L1_ffn_beta"])], axis=1))
    if "p2" not in _PROGS:
        _PROGS["p2"] = build_l1attn_rot(post=True)
    in_maps = []
    for c in range(NCORES):
        order = rot_order(c)
        x2 = np.asarray(p1[c]["x_out"])
        in_maps.append({
            "r_bf": np.asarray(p1[c]["r_bf"]),
            "kAll": np.ascontiguousarray(np.concatenate(
                [np.asarray(p1[o]["k_bf"]) for o in order], axis=1)),
            "vAll": np.ascontiguousarray(np.concatenate(
                [np.asarray(p1[o]["v_bf"]) for o in order], axis=1)),
            "mwAll": np.ascontiguousarray(np.concatenate(
                [np.asarray(p1[o]["meanw"]) for o in order], axis=1)),
            "slow_row": slow_row, "alpha2": alpha2, "ident": C["ident"],
            "corr4": corr_row(c), "slob": slob_row(c),
            "g": np.asarray(p1[c]["g"]),
            "x2": np.ascontiguousarray(x2[:, FH:FH + RPC]),
            "wo": np.asarray(inp["L1_attn_W_o"], np.float32),
            "gnv": gnv, "fv": fv,
            "w1": np.asarray(inp["L1_ffn_w1"][0], np.float32),
            "w2": np.asarray(inp["L1_ffn_w2"][0], np.float32)})
    res = run_bass_kernel_spmd(_PROGS["p2"], in_maps, core_ids=list(range(NCORES)))
    return [np.asarray(res.results[c]["x_out"]) for c in range(NCORES)]


def kernel(**inputs):
    x0 = np.asarray(inputs["L0_attn_x"], np.float32)
    x_fm = np.ascontiguousarray(x0.reshape(16384, D).T)
    p1 = run_p1(inputs, x_fm)
    outs = run_p2(inputs, p1)
    full = np.concatenate([o.T for o in outs], axis=0)
    return np.ascontiguousarray(full.reshape(2, SEQ, D))


def build_l1attn_rot(dbg=False, post=False):
    nc = bass.Bass("TRN2", target_bir_lowering=False, debug=False)
    if dbg:
        d_dbgA = nc.dram_tensor("dbgA", [128, 1024], F32, kind="ExternalOutput").ap()
        d_dbgB = nc.dram_tensor("dbgB", [64, 4096], F32, kind="ExternalOutput").ap()
        d_dbgC = nc.dram_tensor("dbgC", [1, SEQ], F32, kind="ExternalOutput").ap()
    d_r = nc.dram_tensor("r_bf", [D, RPC], BF16, kind="ExternalInput").ap()
    d_k = nc.dram_tensor("kAll", [D, SEQ], BF16, kind="ExternalInput").ap()
    d_v = nc.dram_tensor("vAll", [D, SEQ], BF16, kind="ExternalInput").ap()
    d_mw = nc.dram_tensor("mwAll", [16, SEQ], F32, kind="ExternalInput").ap()
    d_slow = nc.dram_tensor("slow_row", [1, 16], F32, kind="ExternalInput").ap()
    d_alpha = nc.dram_tensor("alpha2", [64, 16], F32, kind="ExternalInput").ap()
    d_ident = nc.dram_tensor("ident", [128, 128], F32, kind="ExternalInput").ap()
    d_corr = nc.dram_tensor("corr4", [1, 16], F32, kind="ExternalInput").ap()
    d_slob = nc.dram_tensor("slob", [1, 8], F32, kind="ExternalInput").ap()
    o_y = nc.dram_tensor("y_attn", [D, RPC], F32,
                         kind=("Internal" if post else "ExternalOutput")).ap()

    NSTRIP, SW, NJ = 4, 512, SEQ // 128

    ctx = ExitStack()
    with ctx:
        b = Bld(nc, ctx)
        markA = b.off
        psctx = ExitStack()
        MWR = b.take("MWR", [1, SEQ], F32)
        CH = b.take("CH", [1, SEQ], F32)
        ZROW = b.take("ZROW", [1, SEQ], F32)
        CROWS = b.take("CROWS", [1, 512], F32)
        CMAT = b.take("CMAT", [128, 64], F32)
        CMATS = b.take("CMATS", [128, 64], F32)
        CT2D = b.take("CT2D", [64, 128], F32)
        KH = b.take("KH", [64, SEQ], BF16)
        VH = b.take("VH", [64, SEQ], BF16)
        VT = b.take("VT", [128, 64 * 64], BF16)
        RH = b.take("RH", [64, RPC], BF16)
        IDF = b.take("IDF", [128, 128], F32)
        IDB = b.take("IDB", [128, 128], BF16)
        ONES = b.take("ONES", [128, 128], F32)
        SIGR = b.take("SIGR", [1, 16], F32)
        SIGB = b.take("SIGB", [128, 16], F32)
        ALPH = b.take("ALPH", [64, 16], F32)
        SCL = b.take("SCL", [128, 4 * 64], F32)
        CQB = b.take("CQB", [128, 2], F32)
        RVB = b.take("RVB", [64, 4 * SW], F32)
        RROW4 = b.take("RROW4", [1, 4 * SW], F32)
        DROW = b.take("DROW", [1, 2 * SW], F32)
        CTBF = b.take("CTBF", [128, SW], F32)
        CTBS = b.take("CTBS", [128, SW], F32)
        PTILE = b.take("PTILE", [128, 4 * SW], BF16)
        DBGPT = b.take("DBGPT", [128, 2 * SW], F32)
        MTILE = b.take("MTILE", [128, 2 * 64], F32)
        MKT = b.take("MKT", [128, 2 * SW], F32)
        T64 = b.take("T64", [64, 2 * SW], F32)
        YT = b.take("YT", [64, SW], F32)
        YACC = b.take("YACC", [64, 4 * SW], F32)
        CORR4 = b.take("CORR4", [1, 16], F32)
        SLOBR = b.take("SLOBR", [1, 8], F32)
        SLOBB = b.take("SLOBB", [64, 8], F32)

        psS1 = psctx.enter_context(nc.psum_tensor("psS1", [128, 512], F32))
        psS2 = psctx.enter_context(nc.psum_tensor("psS2", [128, 512], F32))
        psYA = psctx.enter_context(nc.psum_tensor("psYA", [128, 512], F32))
        psYB = psctx.enter_context(nc.psum_tensor("psYB", [128, 512], F32))
        psYC = psctx.enter_context(nc.psum_tensor("psYC", [128, 512], F32))
        psYD = psctx.enter_context(nc.psum_tensor("psYD", [128, 512], F32))
        psG1 = psctx.enter_context(nc.psum_tensor("psG1", [128, 512], F32))
        psTB = psctx.enter_context(nc.psum_tensor("psTB", [128, 512], BF16))

        b.dma("sync", [(IDF[:], d_ident[:]), (SIGR[:], d_slow[:]),
                       (ALPH[0:64, :], d_alpha[:]), (CORR4[:], d_corr[:]),
                       (SLOBR[:], d_slob[:])])

        def _v0(v):
            b.inc(v.memset(ONES[:], 1.0), "V", 1)
            b.inc(v.memset(ZROW[:], 0.0), "V", 1)
            b.inc(v.tensor_copy(IDB[:], IDF[:]), "V", 1)
        b.on("vector", _v0)

        def _s0(e):
            b.inc(e.activation(SIGR[:], SIGR[:], AF.Sigmoid), "S", 1)
            b.inc(e.activation(ALPH[0:64, :], ALPH[0:64, :], AF.Sigmoid), "S", 1)
        b.on("scalar", _s0)

        def _t0(t):
            b.inc(t.matmul(psG1[:, 0:16], ONES[0:1, 0:128], SIGR[:],
                           start=True, stop=True), "T", 1)
            b.inc(t.matmul(psG1[:, 16:24], ONES[0:1, 0:128], SLOBR[:],
                           start=True, stop=True), "T", 1)
            for ps in (psYA, psYB, psYC, psYD):
                b.inc(t.matmul(ps[0:64, 0:512], ONES[0:1, 0:64], ZROW[0:1, 0:512],
                               start=True, stop=True), "T", 1)
        b.on("tensor", _t0)
        tb0 = b.cnt["T"]

        def _v1(v, tb0=tb0):
            v.wait_ge(b.sems["T"], tb0)
            b.inc(v.tensor_copy(SIGB[:], psG1[:, 0:16]), "V", 1)
            b.inc(v.tensor_copy(SLOBB[0:64, :], psG1[0:64, 16:24]), "V", 1)
        b.on("vector", _v1, bar=False)

        for h in range(16):
            vnow = b.cnt["V"]
            tnow = b.cnt["T"]

            def _ldm(e, vnow=vnow, h=h):
                if vnow:
                    e.wait_ge(b.sems["V"], vnow)
                b.inc(e.dma_start(out=MWR[:], in_=d_mw[h:h + 1, :]), "LD", 16)
            b.on("sync", _ldm, bar=False)

            def _ldk(e, tnow=tnow, h=h):
                if tnow:
                    e.wait_ge(b.sems["T"], tnow)
                b.inc(e.dma_start(out=KH[0:64, :],
                                  in_=d_k[h * 64:(h + 1) * 64, :]), "GD", 16)
                b.inc(e.dma_start(out=VH[0:64, :],
                                  in_=d_v[h * 64:(h + 1) * 64, :]), "GD", 16)
                b.inc(e.dma_start(out=RH[0:64, :],
                                  in_=d_r[h * 64:(h + 1) * 64, :]), "GD", 16)
            b.on("gpsimd", _ldk, bar=False)

            ldnow = b.cnt["LD"]
            snow = b.cnt["S"]

            def _vc(v, h=h, ldnow=ldnow, tnow=tnow, snow=snow):
                v.wait_ge(b.sems["LD"], ldnow)
                if tnow:
                    v.wait_ge(b.sems["T"], tnow)
                if snow:
                    v.wait_ge(b.sems["S"], snow)
                b.inc(v.tensor_tensor_scan(CH[:], MWR[:], ZROW[:], 0.0,
                                           ALU.add, ALU.add), "V", 1)
                v.wait_ge(b.sems["V"], b.cnt["V"])
                for d in range(1, 4):
                    for dp in range(4):
                        e_ = CH[0:1, dp * 2048 + 2047:dp * 2048 + 2048]
                        c_ = CORR4[0:1, d * 4 + dp:d * 4 + dp + 1]
                        t_ = DROW[0:1, d:d + 1]
                        if dp == 0:
                            b.inc(v.tensor_scalar(t_, e_, c_, None, ALU.mult),
                                  "V", 1)
                        else:
                            b.inc(v.scalar_tensor_tensor(t_, e_, c_, t_,
                                                         ALU.mult, ALU.add),
                                  "V", 1)
                        v.wait_ge(b.sems["V"], b.cnt["V"])
                for d in range(1, 4):
                    b.inc(v.tensor_scalar(CH[0:1, d * 2048:(d + 1) * 2048],
                                          CH[0:1, d * 2048:(d + 1) * 2048],
                                          DROW[0:1, d:d + 1], None, ALU.add),
                          "V", 1)
                v.wait_ge(b.sems["V"], b.cnt["V"])
            b.on("vector", _vc, bar=False)

            vscan = b.cnt["V"]

            def _ldc(e, vscan=vscan):
                e.wait_ge(b.sems["V"], vscan)
                b.inc(e.dma_start(out=CT2D[0:64, :], in_=CH[0:1, :]), "LD", 16)
            b.on("sync", _ldc, bar=False)

            def _tc(t):
                b.inc(t.transpose(psG1[:, 0:64], CT2D[0:64, 0:128],
                                  IDF[0:64, 0:64]), "T", 1)
            b.on("tensor", _tc)
            tc = b.cnt["T"]

            def _vc2(v, h=h, tc=tc):
                v.wait_ge(b.sems["T"], tc)
                b.inc(v.tensor_copy(CMAT[:, 0:64], psG1[:, 0:64]), "V", 1)
                v.wait_ge(b.sems["V"], b.cnt["V"])
                b.inc(v.tensor_scalar(CMATS[:, 0:64], CMAT[:, 0:64],
                                      SIGB[:, h:h + 1], None, ALU.mult), "V", 1)
                v.wait_ge(b.sems["V"], b.cnt["V"])
            b.on("vector", _vc2, bar=False)

            for g in range(8):
                def _tv(t, g=g):
                    for jj in range(8):
                        j = g * 8 + jj
                        b.inc(t.transpose(psTB[:, jj * 64:(jj + 1) * 64],
                                          VH[0:64, j * 128:(j + 1) * 128],
                                          IDB[0:64, 0:64]), "T", 1)
                b.on("tensor", _tv)
                tv = b.cnt["T"]

                def _vv(v, g=g, tv=tv):
                    v.wait_ge(b.sems["T"], tv)
                    b.inc(v.tensor_copy(VT[:, g * 512:(g + 1) * 512], psTB[:, 0:512]),
                          "V", 1)
                b.on("vector", _vv, bar=False)

            for si in range(NSTRIP):
                q0 = si * SW
                qm = q0 + SW // 2
                jd = q0 // 128

                def _tq(t, qm=qm):
                    b.inc(t.matmul(psG1[:, 0:1], ONES[0:1, 0:128], CH[0:1, qm:qm + 1],
                                   start=True, stop=True), "T", 1)
                b.on("tensor", _tq)
                tq = b.cnt["T"]

                def _vq(v, h=h, tq=tq, q0=q0, qm=qm):
                    v.wait_ge(b.sems["T"], tq)
                    b.inc(v.tensor_copy(CQB[:, 0:1], psG1[:, 0:1]), "V", 1)
                    v.wait_ge(b.sems["V"], b.cnt["V"])
                    b.inc(v.tensor_scalar(CQB[:, 1:2], CQB[:, 0:1], SIGB[:, h:h + 1],
                                          None, ALU.mult), "V", 1)
                    b.inc(v.tensor_scalar(CROWS[0:1, 0:SW], CH[0:1, q0:q0 + SW],
                                          SIGR[0:1, h:h + 1], None, ALU.mult), "V", 1)
                    for mi, cm in ((0, CMAT), (1, CMATS)):
                        t_ = MTILE[:, mi * 64:mi * 64 + 64]
                        b.inc(v.tensor_scalar(t_, cm[:, 0:64], CQB[:, mi:mi + 1],
                                              None, ALU.subtract), "V", 1)
                        v.wait_ge(b.sems["V"], b.cnt["V"])
                        b.inc(v.tensor_scalar(t_, t_, 80.0, -80.0, ALU.min, ALU.max),
                              "V", 1)
                        v.wait_ge(b.sems["V"], b.cnt["V"])
                    d0 = DROW[0:1, 0:SW]
                    b.inc(v.tensor_scalar(d0, CH[0:1, q0:q0 + SW],
                                          CH[0:1, qm:qm + 1], None, ALU.subtract),
                          "V", 1)
                    v.wait_ge(b.sems["V"], b.cnt["V"])
                    d1 = DROW[0:1, SW:2 * SW]
                    b.inc(v.tensor_scalar(d1, d0, SIGR[0:1, h:h + 1], None, ALU.mult),
                          "V", 1)
                    v.wait_ge(b.sems["V"], b.cnt["V"])
                    b.inc(v.tensor_scalar(d0, d0, 80.0, -80.0, ALU.min, ALU.max), "V", 1)
                    b.inc(v.tensor_scalar(d1, d1, 80.0, -80.0, ALU.min, ALU.max), "V", 1)
                    b.inc(v.memset(YACC[0:64, 0:4 * SW], 0.0), "V", 1)
                    v.wait_ge(b.sems["V"], b.cnt["V"])
                b.on("vector", _vq, bar=True)
                vq = b.cnt["V"]

                def _sq(e, vq=vq):
                    e.wait_ge(b.sems["V"], vq)
                    for mi in range(2):
                        src = MTILE[:, mi * 64:mi * 64 + 64]
                        b.inc(e.activation(src, src, AF.Abs), "S", 1)
                        b.inc(e.activation(SCL[:, (2 * mi) * 64:(2 * mi) * 64 + 64],
                                           src, AF.Exp, scale=-1.0), "S", 1)
                        b.inc(e.activation(SCL[:, (2 * mi + 1) * 64:(2 * mi + 1) * 64 + 64],
                                           src, AF.Exp, scale=-1.0), "S", 1)
                        d_ = DROW[0:1, mi * SW:(mi + 1) * SW]
                        b.inc(e.activation(RROW4[0:1, (2 * mi) * SW:(2 * mi) * SW + SW],
                                           d_, AF.Exp), "S", 1)
                        b.inc(e.activation(
                            RROW4[0:1, (2 * mi + 1) * SW:(2 * mi + 1) * SW + SW],
                            d_, AF.Exp, scale=-1.0), "S", 1)
                b.on("scalar", _sq, bar=False)
                sq_ = b.cnt["S"]

                for bi in range(6):
                    def _tbx(t, bi=bi, q0=q0, sq_=sq_):
                        if bi == 0:
                            t.wait_ge(b.sems["S"], sq_)
                        if bi < 4:
                            b.inc(t.matmul(psG1[0:64, 0:SW], ONES[0:1, 0:64],
                                           RROW4[0:1, bi * SW:(bi + 1) * SW],
                                           start=True, stop=True), "T", 1)
                        else:
                            rhs = (CH[0:1, q0:q0 + SW] if bi == 4
                                   else CROWS[0:1, 0:SW])
                            b.inc(t.matmul(psG1[:, 0:SW], ONES[0:1, 0:128], rhs,
                                           start=True, stop=True), "T", 1)
                    b.on("tensor", _tbx)
                    tbx = b.cnt["T"]

                    def _vbx(v, bi=bi, tbx=tbx):
                        v.wait_ge(b.sems["T"], tbx)
                        if bi < 4:
                            b.inc(v.tensor_copy(RVB[0:64, bi * SW:(bi + 1) * SW],
                                                psG1[0:64, 0:SW]), "V", 1)
                        else:
                            b.inc(v.tensor_copy((CTBF, CTBS)[bi - 4][:, 0:SW],
                                                psG1[:, 0:SW]), "V", 1)
                    b.on("vector", _vbx, bar=False)

                t0c = b.cnt["T"]
                v0c = b.cnt["V"]
                s0c = b.cnt["S"]
                diag = lambda j: jd <= j < jd + 4
                bdiag = jd // 4
                NB = NJ // 4
                t_smm, t_av, v_evict, v_badd = {}, {}, {}, {}
                v_diag1, v_diag15, s_diag1, s_diag2 = {}, {}, {}, {}
                s_evict = {}
                vc, tc_, sc = v0c, t0c, s0c
                for j in range(NJ):
                    if diag(j):
                        vc += 2
                        v_diag1[j] = vc
                        vc += 2
                        v_diag15[j] = vc
                        vc += 2
                        sc += 2
                        s_diag1[j] = sc
                        sc += 2
                        s_diag2[j] = sc
                        s_evict[j] = None
                    elif j % 2 == 0:
                        vc += 2
                        s_evict[j] = None
                    else:
                        sc += 2
                        s_evict[j] = sc
                    v_evict[j] = vc
                    if j % 4 == 3:
                        vc += 2 if j // 4 < 4 else 4
                        v_badd[j // 4] = vc
                tc_ += 1
                t_smm[0] = tc_
                tc_ += 1
                t_smm[1] = tc_
                for j in range(2, NJ):
                    tc_ += 1
                    t_smm[j] = tc_
                    tc_ += 1
                    t_av[j - 2] = tc_
                tc_ += 1
                t_av[NJ - 2] = tc_
                tc_ += 1
                t_av[NJ - 1] = tc_

                def _tj(t, si=si, jd=jd, bdiag=bdiag):
                    def emit_av(j):
                        t.wait_ge(b.sems["V"], v_evict[j])
                        if s_evict[j] is not None:
                            t.wait_ge(b.sems["S"], s_evict[j])
                        bi = j // 4
                        if j % 4 == 0 and bi >= 2:
                            t.wait_ge(b.sems["V"], v_badd[bi - 2])
                        po = (j % 2) * 2 * SW
                        for mi in range(2):
                            ps = ((psYA, psYC) if bi % 2 == 0 else (psYB, psYD))[mi]
                            inst = t.matmul(ps[0:64, 0:SW], VT[:, j * 64:(j + 1) * 64],
                                            PTILE[:, po + mi * SW:po + (mi + 1) * SW],
                                            start=(j % 4 == 0), stop=False,
                                            skip_group_check=True)
                        b.inc(inst, "T", 1)
                        assert b.cnt["T"] == t_av[j]
                    for j in range(NJ):
                        bank = (psS1, psS2, psG1)[j % 3]
                        if j >= 3:
                            if s_evict[j - 3] is not None:
                                t.wait_ge(b.sems["S"], s_evict[j - 3])
                            else:
                                t.wait_ge(b.sems["V"], v_evict[j - 3])
                        b.inc(t.matmul(bank[:, 0:SW], KH[0:64, j * 128:(j + 1) * 128],
                                       RH[0:64, si * SW:si * SW + SW],
                                       start=True, stop=True), "T", 1)
                        assert b.cnt["T"] == t_smm[j]
                        if j >= 2:
                            emit_av(j - 2)
                    emit_av(NJ - 2)
                    emit_av(NJ - 1)
                b.on("tensor", _tj, bar=False)

                def _vj(v, jd=jd, bdiag=bdiag):
                    for j in range(NJ):
                        bank = (psS1, psS2, psG1)[j % 3]
                        po = (j % 2) * 2 * SW
                        if diag(j) or j % 2 == 0:
                            v.wait_ge(b.sems["T"], t_smm[j])
                            if j >= 2:
                                v.wait_ge(b.sems["T"], t_av[j - 2])
                        if not diag(j) and j % 2 == 1:
                            pass
                        elif not diag(j):
                            ci = 0 if j < jd else 1
                            for mi in range(2):
                                b.inc(v.tensor_scalar(
                                    PTILE[:, po + mi * SW:po + (mi + 1) * SW],
                                    bank[:, 0:SW],
                                    SCL[:, (2 * mi + ci) * 64 + j:
                                        (2 * mi + ci) * 64 + j + 1],
                                    None, ALU.mult), "V", 1)
                        else:
                            for mi, (ctb, cm) in enumerate(((CTBF, CMAT), (CTBS, CMATS))):
                                b.inc(v.tensor_scalar(
                                    MKT[:, mi * SW:(mi + 1) * SW], ctb[:, 0:SW],
                                    cm[:, j:j + 1], None, ALU.subtract), "V", 1)
                            v.wait_ge(b.sems["S"], s_diag1[j])
                            for mi, ctb in enumerate((CTBF, CTBS)):
                                b.inc(v.tensor_tensor(MKT[:, mi * SW:(mi + 1) * SW],
                                                      MKT[:, mi * SW:(mi + 1) * SW],
                                                      ctb[:, 0:SW], ALU.add), "V", 1)
                            v.wait_ge(b.sems["S"], s_diag2[j])
                            for mi in range(2):
                                b.inc(v.tensor_tensor(
                                    PTILE[:, po + mi * SW:po + (mi + 1) * SW],
                                    bank[:, 0:SW],
                                    MKT[:, mi * SW:(mi + 1) * SW], ALU.mult), "V", 1)

                        if j % 4 == 3:
                            bi = j // 4
                            v.wait_ge(b.sems["T"], t_av[j])
                            if bi < 4:
                                lo = bi <= bdiag
                                for mi in range(2):
                                    ps = ((psYA, psYC) if bi % 2 == 0
                                          else (psYB, psYD))[mi]
                                    seg = (0 if lo else 2) + mi
                                    dst = YACC[0:64, seg * SW:(seg + 1) * SW]
                                    b.inc(v.tensor_tensor(dst, dst, ps[0:64, 0:SW],
                                                          ALU.add), "V", 1)
                            else:
                                d = bi // 4
                                for mi in range(2):
                                    ps = ((psYA, psYC) if bi % 2 == 0
                                          else (psYB, psYD))[mi]
                                    for side in range(2):
                                        dst = YACC[0:64, (2 * side + mi) * SW:
                                                   (2 * side + mi + 1) * SW]
                                        b.inc(v.scalar_tensor_tensor(
                                            dst, ps[0:64, 0:SW],
                                            SLOBB[0:64, 4 * side + d:
                                                  4 * side + d + 1],
                                            dst, ALU.mult, ALU.add), "V", 1)
                            assert b.cnt["V"] == v_badd[bi]
                            if dbg and h == 0 and si == 0 and j == 11:
                                v.tensor_copy(DBGPT[0:64, 0:SW], psYA[0:64, 0:SW])
                            if dbg and h == 0 and si == 0 and j == 15:
                                v.tensor_copy(DBGPT[0:64, SW:2 * SW],
                                              YACC[0:64, 2 * SW:3 * SW])
                b.on("vector", _vj, bar=False)

                def _sj(e, jd=jd):
                    for j in range(NJ):
                        if not diag(j) and j % 2 == 1:
                            bank = (psS1, psS2, psG1)[j % 3]
                            po = (j % 2) * 2 * SW
                            ci = 0 if j < jd else 1
                            e.wait_ge(b.sems["T"], t_smm[j])
                            if j >= 2:
                                e.wait_ge(b.sems["T"], t_av[j - 2])
                            for mi in range(2):
                                b.inc(e.activation(
                                    PTILE[:, po + mi * SW:po + (mi + 1) * SW],
                                    bank[:, 0:SW], AF.Copy,
                                    scale=SCL[:, (2 * mi + ci) * 64 + j:
                                              (2 * mi + ci) * 64 + j + 1]), "S", 1)
                            assert b.cnt["S"] == s_evict[j]
                            continue
                        if diag(j):
                            e.wait_ge(b.sems["V"], v_diag1[j])
                            for mi in range(2):
                                b.inc(e.activation(MKT[:, mi * SW:(mi + 1) * SW],
                                                   MKT[:, mi * SW:(mi + 1) * SW],
                                                   AF.Abs), "S", 1)
                            e.wait_ge(b.sems["V"], v_diag15[j])
                            for mi in range(2):
                                b.inc(e.activation(MKT[:, mi * SW:(mi + 1) * SW],
                                                   MKT[:, mi * SW:(mi + 1) * SW],
                                                   AF.Exp, scale=-1.0,
                                                   bias=CQB[:, mi:mi + 1]), "S", 1)

                b.on("scalar", _sj, bar=False)

                up_empty = bdiag == NB - 1

                def _vy(v, h=h, si=si, up_empty=up_empty):
                    v.wait_ge(b.sems["V"], b.cnt["V"])
                    for mi in range(2):
                        dst = T64[0:64, mi * SW:(mi + 1) * SW]
                        b.inc(v.tensor_tensor(dst, YACC[0:64, mi * SW:(mi + 1) * SW],
                                              RVB[0:64, (2 * mi) * SW:(2 * mi) * SW + SW],
                                              ALU.mult), "V", 1)
                        if not up_empty:
                            b.inc(v.tensor_tensor(YT[0:64, 0:SW],
                                                  YACC[0:64, (2 + mi) * SW:(3 + mi) * SW],
                                                  RVB[0:64, (2 * mi + 1) * SW:
                                                      (2 * mi + 1) * SW + SW],
                                                  ALU.mult), "V", 1)
                            v.wait_ge(b.sems["V"], b.cnt["V"])
                            b.inc(v.tensor_tensor(dst, dst, YT[0:64, 0:SW], ALU.add),
                                  "V", 1)
                            v.wait_ge(b.sems["V"], b.cnt["V"])
                    v.wait_ge(b.sems["V"], b.cnt["V"])
                    b.inc(v.tensor_tensor(YT[0:64, 0:SW], T64[0:64, 0:SW],
                                          T64[0:64, SW:2 * SW], ALU.subtract), "V", 1)
                    v.wait_ge(b.sems["V"], b.cnt["V"])
                    b.inc(v.scalar_tensor_tensor(YT[0:64, 0:SW], YT[0:64, 0:SW],
                                                 ALPH[0:64, h:h + 1],
                                                 T64[0:64, SW:2 * SW],
                                                 ALU.mult, ALU.add), "V", 1)
                    v.wait_ge(b.sems["V"], b.cnt["V"])
                gnow = b.cnt["GD"]

                def _ywait(v, gnow=gnow):
                    if gnow:
                        v.wait_ge(b.sems["GD"], gnow)
                b.on("vector", _ywait, bar=False)
                b.on("vector", _vy, bar=False)
                vye = b.cnt["V"]

                def _ydma(e, vye=vye, h=h, si=si):
                    e.wait_ge(b.sems["V"], vye)
                    b.inc(e.dma_start(out=o_y[h * 64:(h + 1) * 64,
                                              si * SW:(si + 1) * SW],
                                      in_=YT[0:64, 0:SW]), "GD", 16)
                b.on("gpsimd", _ydma, bar=False)
                if dbg and h == 0 and si == 0:
                    b.dma("gpsimd", [(d_dbgB[0:64, 2048:2048 + 2048],
                                      YACC[0:64, 0:2048])], bar=True)
                if dbg and h == 0 and si == 0:
                    def _vdbg(v):
                        for mi, ps in enumerate((psYA, psYB, psYC, psYD)):
                            b.inc(v.tensor_copy(
                                RVB[0:64, mi * SW:(mi + 1) * SW] if False else
                                T64[0:64, 0:SW], ps[0:64, 0:SW]), "V", 1)
                            b.inc(v.tensor_copy(
                                MKT[0:64, 0:SW] if False else
                                RVB[0:64, mi * SW:(mi + 1) * SW], T64[0:64, 0:SW]),
                                "V", 1)
                    b.on("vector", _vdbg)
                    b.dma("gpsimd", [
                        (d_dbgA[:, 0:64], CMAT[:, 0:64]),
                        (d_dbgA[:, 64:128], CMATS[:, 0:64]),
                        (d_dbgA[:, 128:384], SCL[:, 0:256]),
                        (d_dbgA[:, 384:386], CQB[:, 0:2]),
                        (d_dbgA[:, 386:898], CTBF[:, 0:SW]),
                        (d_dbgB[0:64, 0:2048], RVB[0:64, 0:2048]),
                        (d_dbgB[0:64, 2048:2048 + 512],
                         PTILE[:, 0:SW].bitcast(F32) if False else RVB[0:64, 0:512]),
                        (d_dbgC[:], CH[:]),
                        (d_dbgA[:, 898:1022], DBGPT[:, 0:124]) if False else
                        (d_dbgB[0:64, 0:1024], DBGPT[0:64, 0:1024]),
                        ], bar=True)

        fin = b.cnt["GD"]

        def _fin(e, fin=fin):
            e.wait_ge(b.sems["GD"], fin)
        b.on("gpsimd", _fin, bar=False)

        psctx.close()
        if post:
            b.off = markA
            emit_l1post(nc, b, ctx, o_y, d_ident)
    return nc




# version 14 (final)
# speedup vs baseline: 1.1913x; 1.1913x over previous
import math
from contextlib import ExitStack

import numpy as np

import concourse.bass as bass
import concourse.mybir as mybir
from concourse.bass_utils import run_bass_kernel_spmd

F32 = mybir.dt.float32
BF16 = mybir.dt.bfloat16
I32 = mybir.dt.int32
AF = mybir.ActivationFunctionType
ALU = mybir.AluOpType

NCORES = 8
SEQ = 8192
D = 1024
NCH = 8
PI = math.pi
RPC = 2048
L0L = 4

SBUF_BASE = 16512
SBUF_TOP = 229344


FH, EH = 5, 1
WTOK = RPC + FH + EH


def core_slice_fm(x_fm_full, c):
    s0, s1 = c * RPC, (c + 1) * RPC
    lo = s0 - FH
    hi = s1 + EH
    if c in (0, 4):
        front = np.repeat(x_fm_full[:, s0:s0 + 1], FH, axis=1)
    else:
        front = x_fm_full[:, lo:s0]
    if c in (3, 7):
        back = np.repeat(x_fm_full[:, s1 - 1:s1], EH, axis=1)
    else:
        back = x_fm_full[:, s1:hi]
    return np.ascontiguousarray(np.concatenate([front, x_fm_full[:, s0:s1], back], axis=1))


def core_filt_slice(filt, c):
    s0, s1 = c * RPC, (c + 1) * RPC
    sm = s0 % SEQ
    if c in (0, 4):
        front = np.repeat(filt[:, sm:sm + 1], FH, axis=1)
    else:
        front = filt[:, sm - FH:sm]
    em = (s1 - 1) % SEQ
    if c in (3, 7):
        back = np.repeat(filt[:, em:em + 1], EH, axis=1)
    else:
        back = filt[:, em + 1:em + 1 + EH]
    return np.ascontiguousarray(
        np.concatenate([front, filt[:, sm:sm + RPC], back], axis=1))


def tile_widths(w):
    n = (w + 511) // 512
    base = w // n
    rem = w - base * n
    return [base + (1 if i < rem else 0) for i in range(n)]


_CONSTS = None


def host_consts():
    global _CONSTS
    if _CONSTS is not None:
        return _CONSTS
    ph = np.arange(16)
    pd = np.arange(64)
    D16 = np.exp(-2j * np.pi * np.outer(ph, ph) / 16.0)
    D64 = np.exp(-2j * np.pi * np.outer(pd, pd) / 64.0)
    A = np.kron(D16, D64)
    ks = np.fft.fftfreq(SEQ)[:, None, None]
    kh = np.fft.fftfreq(16)[None, :, None]
    kd = np.fft.fftfreq(64)[None, None, :]
    km = np.sqrt(ks ** 2 + kh ** 2 + kd ** 2 + 1e-10)
    f = np.exp(1j * np.arctan(np.log(km + 1e-10)))
    lam = np.arange(SEQ) / SEQ
    t = np.arange(SEQ) * 0.1
    mod = np.sin(t + lam) * np.cos(t - 2.0 * lam + 0.5 * lam ** 2)
    fp = f.reshape(SEQ, D) * (1.0 + 0.1 * mod)[:, None]
    _CONSTS = dict(
        Ar=np.ascontiguousarray(A.real.astype(np.float32)),
        Ai=np.ascontiguousarray(A.imag.astype(np.float32)),
        filt_r=np.ascontiguousarray(fp.real.astype(np.float32).T),
        filt_i=np.ascontiguousarray(fp.imag.astype(np.float32).T),
        ident=np.eye(128, dtype=np.float32),
    )
    return _CONSTS


ENG_SEM = {"sync": "LD", "gpsimd": "GD", "vector": "V", "scalar": "S", "tensor": "T"}


class Bld:
    def __init__(self, nc, ctx):
        self.nc = nc
        self.ctx = ctx
        self.off = SBUF_BASE
        self._uid = 0
        self.sems = {}
        self.cnt = {}
        self.block = ctx.enter_context(nc.Block())
        self._marks = []
        for s in ("LD", "GD", "V", "S", "T"):
            h = ctx.enter_context(nc.semaphore(s))
            self.sems[s] = h
            self.cnt[s] = 0
        self._consts = self.take("CCOLS", [128, 16], F32)
        self._cvals = {}

    def take(self, name, shape, dtype):
        per = int(np.prod(shape[1:])) * mybir.dt.size(dtype)
        o = (self.off + 31) // 32 * 32
        assert o + per <= SBUF_TOP, f"SBUF overflow at {name}: need {o + per}"
        self.off = o + per
        self._uid += 1
        return self.nc.alloc_sbuf_tensor_at(f"{name}_{self._uid}", list(shape), dtype, offset=o)

    def alias(self, name, shape, dtype, at):
        o = at.manual_sbuf_range[0]
        per = int(np.prod(shape[1:])) * mybir.dt.size(dtype)
        assert o + per <= at.manual_sbuf_range[1] or True
        self._uid += 1
        return self.nc.alloc_sbuf_tensor_at(f"{name}_{self._uid}", list(shape), dtype, offset=o)

    def push(self):
        self._marks.append(self.off)

    def pop(self):
        self.off = self._marks.pop()

    def ccol(self, val):
        val = float(val)
        if val not in self._cvals:
            idx = len(self._cvals)
            assert idx < 16
            self._cvals[val] = idx

            def _m(v, idx=idx, val=val):
                self.inc(v.memset(self._consts[:, idx:idx + 1], val), "V", 1)
            self.on("vector", _m, bar=False)
        return self._consts[:, self._cvals[val]:self._cvals[val] + 1]

    def inc(self, inst, name, k=1):
        inst.then_inc(self.sems[name], k)
        self.cnt[name] += k
        return inst

    def barrier(self, eng, engname):
        for e, s in ENG_SEM.items():
            if e != engname and self.cnt[s] > 0:
                eng.wait_ge(self.sems[s], self.cnt[s])

    def on(self, engname, fn, bar=True):
        def _f(eng):
            if bar:
                self.barrier(eng, engname)
            fn(eng)
        getattr(self.block, engname)(_f)

    def dma(self, queue, pairs, bar=False):
        sem = ENG_SEM[queue]

        def _f(e):
            for out_ap, in_ap in pairs:
                self.inc(e.dma_start(out=out_ap, in_=in_ap), sem, 16)
        self.on(queue, _f, bar=bar)


def proj(b, w_sb, wcols, xfn, ic_list, oc_n, n, ps_pair, ev_eng, ev_fn):
    t0 = b.cnt["T"]
    ev_sem = ENG_SEM[ev_eng]
    s0 = b.cnt[ev_sem]
    ic_n = len(ic_list)

    def _t(t):
        for oc in range(oc_n):
            bank = ps_pair[oc % 2]
            if oc >= 2:
                t.wait_ge(b.sems[ev_sem], s0 + (oc - 1))
            for j, ic in enumerate(ic_list):
                inst = t.matmul(bank[:, 0:n],
                                w_sb[:, ic * wcols + oc * 128: ic * wcols + (oc + 1) * 128],
                                xfn(ic),
                                start=(j == 0), stop=(j == ic_n - 1))
            b.inc(inst, "T", 1)
    b.on("tensor", _t)

    def _e(e):
        for oc in range(oc_n):
            e.wait_ge(b.sems["T"], t0 + oc + 1)
            b.inc(ev_fn(oc, ps_pair[oc % 2][:, 0:n], e), ev_sem, 1)
    b.on(ev_eng, _e, bar=False)


def layernorm(b, xfn, n, ps, rows, sq2, ones, out_fn, eps=1e-5):
    r_mu, r_var = rows[0][:], rows[1][:]
    eps_ap = b.ccol(eps)
    pre = {s: b.cnt[s] for s in ("LD", "GD", "V", "S", "T")}
    t0 = b.cnt["T"]
    s0 = b.cnt["S"]

    def _s(e):
        for nm in ("LD", "GD", "V", "T"):
            if pre[nm] > 0:
                e.wait_ge(b.sems[nm], pre[nm])
        for ic in range(NCH):
            if ic >= 2:
                e.wait_ge(b.sems["T"], t0 + 1 + (ic - 1))
            b.inc(e.activation(sq2[:, (ic % 2) * n:(ic % 2) * n + n], xfn(ic),
                               AF.Square), "S", 1)
    b.on("scalar", _s, bar=False)

    def _t(t):
        for nm in ("LD", "GD", "V", "S"):
            if pre[nm] > 0:
                t.wait_ge(b.sems[nm], pre[nm])
        for ic in range(NCH):
            inst = t.matmul(ps["sum"][0:1, 0:n], ones[:, 0:1], xfn(ic),
                            start=(ic == 0), stop=(ic == NCH - 1))
        b.inc(inst, "T", 1)
        for ic in range(NCH):
            t.wait_ge(b.sems["S"], s0 + ic + 1)
            inst = t.matmul(ps["sq"][0:1, 0:n], ones[:, 0:1],
                            sq2[:, (ic % 2) * n:(ic % 2) * n + n],
                            start=(ic == 0), stop=(ic == NCH - 1))
            b.inc(inst, "T", 1)
    b.on("tensor", _t, bar=False)
    t_sums = b.cnt["T"]

    def _v(v):
        v.wait_ge(b.sems["T"], t_sums)
        b.inc(v.tensor_scalar(r_mu[:, 0:n], ps["sum"][0:1, 0:n], 1.0 / D, None, ALU.mult), "V", 1)
        b.inc(v.tensor_scalar(r_var[:, 0:n], ps["sq"][0:1, 0:n], 1.0 / D, None, ALU.mult), "V", 1)
        b.inc(v.tensor_tensor(ps["sq"][0:1, 0:n], r_mu[:, 0:n], r_mu[:, 0:n], ALU.mult), "V", 1)
        b.inc(v.tensor_tensor(r_var[:, 0:n], r_var[:, 0:n], ps["sq"][0:1, 0:n],
                              ALU.subtract), "V", 1)
    b.on("vector", _v, bar=False)
    v_var = b.cnt["V"]

    def _s2(e):
        e.wait_ge(b.sems["V"], v_var)
        b.inc(e.activation(r_var[:, 0:n], r_var[:, 0:n], AF.Sqrt, bias=eps_ap[0:1, 0:1]), "S", 1)
    b.on("scalar", _s2, bar=False)
    s_sqrt = b.cnt["S"]

    def _v2(v):
        v.wait_ge(b.sems["S"], s_sqrt)
        b.inc(v.reciprocal(r_var[:, 0:n], r_var[:, 0:n]), "V", 1)
    b.on("vector", _v2, bar=False)
    v_rstd = b.cnt["V"]

    def _t2(t):
        t.wait_ge(b.sems["V"], v_rstd)
        b.inc(t.matmul(ps["mu"][:, 0:n], ones[0:1, 0:128], r_mu[:, 0:n],
                       start=True, stop=True), "T", 1)
        b.inc(t.matmul(ps["rstd"][:, 0:n], ones[0:1, 0:128], r_var[:, 0:n],
                       start=True, stop=True), "T", 1)
    b.on("tensor", _t2, bar=False)
    t_bc = b.cnt["T"]

    def _v3(v):
        v.wait_ge(b.sems["T"], t_bc)
        for ic in range(NCH):
            slot = sq2[:, (ic % 2) * n:(ic % 2) * n + n]
            b.inc(v.tensor_tensor(slot, xfn(ic), ps["mu"][:, 0:n], ALU.subtract), "V", 1)
            b.inc(v.tensor_tensor(slot, slot, ps["rstd"][:, 0:n], ALU.mult), "V", 1)
            out_fn(ic, v, slot)
    b.on("vector", _v3, bar=False)


def build_l0(w_tok, layers=L0L, do_outer=True, do_ffn=True, stages=(1, 1, 1), no_pop=False, with_l1pre=False):
    nc = bass.Bass("TRN2", target_bir_lowering=False, debug=False)
    L = L0L
    x_in = nc.dram_tensor("x_in", [D, w_tok], F32, kind="ExternalInput").ap()
    x_out = nc.dram_tensor("x_out", [D, w_tok], F32, kind="ExternalOutput").ap()
    d_aw = [nc.dram_tensor(f"aw{i}", [D, 3 * D], F32, kind="ExternalInput").ap() for i in range(L)]
    d_pw = [nc.dram_tensor(f"pw{i}", [D, D], F32, kind="ExternalInput").ap() for i in range(L)]
    d_fw = [nc.dram_tensor(f"fw{i}", [D, 4 * D], F32, kind="ExternalInput").ap() for i in range(L)]
    d_mw = [nc.dram_tensor(f"mw{i}", [4 * D, D], F32, kind="ExternalInput").ap() for i in range(L)]
    NVEC = 8 + 8 + 24 + 8 + 32 + 8 + 8 + 8
    d_vec = nc.dram_tensor("vecs", [128, L * NVEC], F32, kind="ExternalInput").ap()
    d_Ar = nc.dram_tensor("Ar", [D, D], F32, kind="ExternalInput").ap()
    d_Ai = nc.dram_tensor("Ai", [D, D], F32, kind="ExternalInput").ap()
    d_fr = nc.dram_tensor("filt_r", [D, w_tok], F32, kind="ExternalInput").ap()
    d_fi = nc.dram_tensor("filt_i", [D, w_tok], F32, kind="ExternalInput").ap()
    d_ident = nc.dram_tensor("ident", [128, 128], F32, kind="ExternalInput").ap()
    d_W = nc.dram_tensor("ffn_W", [D, D], F32, kind="ExternalInput").ap()
    d_ld = nc.dram_tensor("lora_down", [4, 100], F32, kind="ExternalInput").ap()
    d_lu = nc.dram_tensor("lora_up", [50, 4], F32, kind="ExternalInput").ap()
    d_da = nc.dram_tensor("down_aux", [100, D], F32, kind="ExternalInput").ap()
    d_ua = nc.dram_tensor("up_aux", [D, 50], F32, kind="ExternalInput").ap()
    d_ffnb = nc.dram_tensor("ffn_b", [128, 8], F32, kind="ExternalInput").ap()
    p0_dram = nc.dram_tensor("p0scr", [D, w_tok], BF16, kind="Internal").ap()

    tiles = []
    off = 0
    for w in tile_widths(w_tok):
        tiles.append((off, w))
        off += w

    ctx = ExitStack()
    with ctx:
        b = Bld(nc, ctx)
        mark0 = b.off
        XB = b.take("XB", [128, NCH * w_tok], F32)
        VEC = b.take("VEC", [128, L * NVEC], F32)
        ONES = b.take("ONES", [128, 128], F32)
        IDF = b.take("IDF", [128, 128], F32)
        ROWS_ = tuple(b.take(f"ROW{i}", [1, 512], F32) for i in range(2))

        psA = ctx.enter_context(nc.psum_tensor("psA", [128, 512], F32))
        psB = ctx.enter_context(nc.psum_tensor("psB", [128, 512], F32))
        psSum = ctx.enter_context(nc.psum_tensor("psSum", [128, 512], F32))
        psSq = ctx.enter_context(nc.psum_tensor("psSq", [128, 512], F32))
        psMu = ctx.enter_context(nc.psum_tensor("psMu", [128, 512], F32))
        psRstd = ctx.enter_context(nc.psum_tensor("psRstd", [128, 512], F32))
        psG = ctx.enter_context(nc.psum_tensor("psG", [128, 512], F32))
        LNPS = {"sum": psSum, "sq": psSq, "mu": psMu, "rstd": psRstd}

        b.dma("sync", [(XB[:, ic * w_tok:(ic + 1) * w_tok],
                        x_in[ic * 128:(ic + 1) * 128, :]) for ic in range(NCH)]
              + [(VEC[:], d_vec[:]), (ONES[:], d_ident[:]), (IDF[:], d_ident[:])])

        def _v0(v):
            b.inc(v.memset(ONES[:], 1.0), "V", 1)
        b.on("vector", _v0)

        voff = [0]

        def vcols(n128):
            o = voff[0]
            voff[0] += n128
            return (lambda j, o=o: VEC[:, o + j:o + j + 1])
        LV = [dict(ln1w=vcols(8), ln1b=vcols(8), ab=vcols(24), pb=vcols(8),
                   fcb=vcols(32), mpb=vcols(8), ln2w=vcols(8), ln2b=vcols(8))
              for _ in range(L)]

        if no_pop:
            b.pop = lambda: None
        for li in range(layers):
            lv = LV[li]
            emit_l0_layer(b, nc, ctx, li, lv, tiles, w_tok, XB, ONES, IDF, ROWS_,
                          LNPS, psA, psB, psG,
                          d_aw[li], d_pw[li], d_fw[li], d_mw[li],
                          d_Ar, d_Ai, d_fr, d_fi, p0_dram, stages=stages)

        TW = max(w for w in tile_widths(w_tok))
        if not do_outer:
            do_ffn_local = False
        b.push()
        X0T = b.take("X0T", [128, NCH * TW], F32)
        SQ2 = b.take("SQ2", [128, 2 * TW], F32)
        for (toff, tw) in (tiles if do_outer else []):
            b.dma("sync", [(X0T[:, ic * tw:(ic + 1) * tw],
                            x_in[ic * 128:(ic + 1) * 128, toff:toff + tw])
                           for ic in range(NCH)], bar=True)

            def _v(v, toff=toff, tw=tw):
                for ic in range(NCH):
                    b.inc(v.tensor_tensor(
                        XB[:, ic * w_tok + toff:ic * w_tok + toff + tw],
                        XB[:, ic * w_tok + toff:ic * w_tok + toff + tw],
                        X0T[:, ic * tw:(ic + 1) * tw], ALU.add), "V", 1)
            b.on("vector", _v)

            def xfn(ic, o=toff, n=tw):
                return XB[:, ic * w_tok + o:ic * w_tok + o + n]

            def out_plain(ic, v, normed, o=toff, n=tw):
                b.inc(v.tensor_copy(XB[:, ic * w_tok + o:ic * w_tok + o + n], normed), "V", 1)
            layernorm(b, xfn, tw, LNPS, ROWS_, SQ2, ONES, out_plain)
        b.pop()

        if do_ffn:
            emit_l0_ffn(b, nc, ctx, tiles, w_tok, XB, ONES, IDF, ROWS_, LNPS,
                        psA, psB, psG, d_W, d_ld, d_lu, d_da, d_ua, d_ffnb)

        st = b.cnt
        b.dma("sync", [(x_out[ic * 128:(ic + 1) * 128, :],
                        XB[:, ic * w_tok:(ic + 1) * w_tok]) for ic in range(NCH)], bar=True)
        final = b.cnt["LD"]

        def _fin(e):
            e.wait_ge(b.sems["LD"], final)
        b.on("sync", _fin, bar=False)

        if with_l1pre:
            b.off = mark0
            emit_l1pre(nc, b, ctx, x_out, psA, psB, psMu, psRstd)
    return nc


def emit_l0_layer(b, nc, ctx, li, lv, tiles, w_tok, XB, ONES, IDF, ROWS_, LNPS,
                  psA, psB, psG, d_aw, d_pw, d_fw, d_mw, d_Ar, d_Ai, d_fr, d_fi,
                  p0_dram, stages=(1, 1, 1)):
    TW = max(w for (_, w) in tiles)

    if not stages[0]:
        return _g2_g3_only(b, nc, ctx, li, lv, tiles, w_tok, XB, ONES, IDF, ROWS_,
                           LNPS, psA, psB, psG, d_pw, d_fw, d_mw, d_Ar, d_Ai,
                           d_fr, d_fi, p0_dram, stages, TW)
    b.push()
    WA = b.take("WA", [128, NCH * 3 * D], BF16)
    QKV = b.take("QKV", [128, 24 * TW], F32)
    LNX = b.take("LNX", [128, NCH * TW], BF16)
    TMP = b.take("TMP", [128, 15 * TW], F32)
    TI = b.take("TI", [128, TW], I32)
    P0T = b.take("P0T", [128, NCH * TW], BF16)
    QPREV = b.take("QPREV", [128, 24], F32)
    SQ2 = TMP

    b.dma("gpsimd", [(WA[:, ic * 3 * D:(ic + 1) * 3 * D],
                      d_aw[ic * 128:(ic + 1) * 128, :]) for ic in range(NCH)], bar=True)

    for tix, (toff, tw) in enumerate(tiles):
        def xfn(ic, o=toff, n=tw):
            return XB[:, ic * w_tok + o:ic * w_tok + o + n]

        def ln1_out(ic, v, normed, n=tw):
            b.inc(v.tensor_scalar(LNX[:, ic * n:(ic + 1) * n], normed,
                                  lv["ln1w"](ic), lv["ln1b"](ic), ALU.mult, ALU.add), "V", 1)
        layernorm(b, xfn, tw, LNPS, ROWS_, SQ2, ONES, ln1_out)

        def qkv_ev(oc, ps_ap, e, n=tw):
            return e.activation(QKV[:, oc * n:(oc + 1) * n], ps_ap, AF.Identity,
                                bias=lv["ab"](oc))
        proj(b, WA, 3 * D, lambda ic, n=tw: LNX[:, ic * n:(ic + 1) * n],
             list(range(NCH)), 24, tw, (psA, psB), "scalar", qkv_ev)

        emit_quat(b, tix, tw, TW, QKV, QPREV, TMP, TI, P0T)

        def _vq(v, n=tw):
            b.inc(v.tensor_copy(QPREV[:],
                                QKV[:].rearrange("p (c t) -> p c t", c=24)[:, :, n - 1:n]),
                  "V", 1)
        b.on("vector", _vq, bar=False)

        b.dma("gpsimd", [(p0_dram[ic * 128:(ic + 1) * 128, toff:toff + tw],
                          P0T[:, ic * tw:(ic + 1) * tw]) for ic in range(NCH)], bar=True)
    b.pop()
    if not (stages[1] or stages[2]):
        return
    return _g2_g3_only(b, nc, ctx, li, lv, tiles, w_tok, XB, ONES, IDF, ROWS_,
                       LNPS, psA, psB, psG, d_pw, d_fw, d_mw, d_Ar, d_Ai,
                       d_fr, d_fi, p0_dram, stages, TW)


def _g2_g3_only(b, nc, ctx, li, lv, tiles, w_tok, XB, ONES, IDF, ROWS_, LNPS,
                psA, psB, psG, d_pw, d_fw, d_mw, d_Ar, d_Ai, d_fr, d_fi,
                p0_dram, stages, TW):
    if not stages[1]:
        return _g3_only(b, nc, ctx, li, lv, tiles, w_tok, XB, ONES, IDF, ROWS_,
                        LNPS, psA, psB, psG, d_fw, d_mw, stages, TW)
    b.push()
    ARB = b.take("ARB", [128, NCH * D], BF16)
    AIB = b.take("AIB", [128, NCH * D], BF16)
    G1B = b.take("G1B", [128, NCH * D], BF16)
    G2B = b.take("G2B", [128, NCH * D], BF16)
    b.push()
    PWB = b.take("PWB", [128, NCH * D], BF16)
    b.pop()
    P0L = b.take("P0L", [128, NCH * TW], BF16)
    UR = b.take("UR", [128, NCH * TW], F32)
    UI = b.take("UI", [128, NCH * TW], F32)
    FR = b.take("FR", [128, NCH * TW], F32)
    FI = b.take("FI", [128, NCH * TW], F32)
    WRB = b.take("WRB", [128, NCH * TW], BF16)
    WIB = b.take("WIB", [128, NCH * TW], BF16)
    TMP2 = b.alias("TMP2", [128, 2 * TW], F32, P0L)

    b.dma("gpsimd", [(ARB[:, ic * D:(ic + 1) * D], d_Ar[ic * 128:(ic + 1) * 128, :])
                     for ic in range(NCH)]
          + [(AIB[:, ic * D:(ic + 1) * D], d_Ai[ic * 128:(ic + 1) * 128, :])
             for ic in range(NCH)]
          + [(PWB[:, ic * D:(ic + 1) * D], d_pw[ic * 128:(ic + 1) * 128, :])
             for ic in range(NCH)], bar=True)

    for (gsrc, gdst) in ((ARB, G1B), (AIB, G2B)):
        for fc in range(NCH):
            for half in range(2):
                def _t(t, gsrc=gsrc, fc=fc, half=half):
                    for kc in range(NCH):
                        inst = t.matmul(psG[:, 0:512],
                                        gsrc[:, kc * D + fc * 128: kc * D + (fc + 1) * 128],
                                        PWB[:, kc * D + half * 512: kc * D + (half + 1) * 512],
                                        start=(kc == 0), stop=(kc == NCH - 1))
                    b.inc(inst, "T", 1)
                b.on("tensor", _t)
                tt = b.cnt["T"]

                def _s(e, gdst=gdst, fc=fc, half=half, tt=tt):
                    e.wait_ge(b.sems["T"], tt)
                    b.inc(e.activation(gdst[:, fc * D + half * 512: fc * D + (half + 1) * 512],
                                       psG[:, 0:512], AF.Copy, scale=1.0 / D), "S", 1)
                b.on("scalar", _s, bar=False)

    for (toff, tw) in tiles:
        b.dma("gpsimd", [(P0L[:, ic * tw:(ic + 1) * tw],
                          p0_dram[ic * 128:(ic + 1) * 128, toff:toff + tw])
                         for ic in range(NCH)], bar=True)
        b.dma("sync", [(FR[:, ic * tw:(ic + 1) * tw],
                        d_fr[ic * 128:(ic + 1) * 128, toff:toff + tw]) for ic in range(NCH)]
              + [(FI[:, ic * tw:(ic + 1) * tw],
                  d_fi[ic * 128:(ic + 1) * 128, toff:toff + tw]) for ic in range(NCH)],
              bar=True)

        def p0fn(ic, n=tw):
            return P0L[:, ic * n:(ic + 1) * n]

        proj(b, ARB, D, p0fn, list(range(NCH)), NCH, tw, (psA, psB), "vector",
             lambda oc, ps_ap, e, n=tw: e.tensor_copy(UR[:, oc * n:(oc + 1) * n], ps_ap))
        proj(b, AIB, D, p0fn, list(range(NCH)), NCH, tw, (psA, psB), "vector",
             lambda oc, ps_ap, e, n=tw: e.tensor_copy(UI[:, oc * n:(oc + 1) * n], ps_ap))

        def _vf(v, tw=tw):
            t1, t2 = TMP2[:, 0:tw], TMP2[:, TW:TW + tw]
            for ic in range(NCH):
                s = slice(ic * tw, (ic + 1) * tw)
                b.inc(v.tensor_tensor(t1, UR[:, s], FR[:, s], ALU.mult), "V", 1)
                b.inc(v.tensor_tensor(t2, UI[:, s], FI[:, s], ALU.mult), "V", 1)
                b.inc(v.tensor_tensor(WRB[:, s], t1, t2, ALU.subtract), "V", 1)
                b.inc(v.tensor_tensor(t1, UR[:, s], FI[:, s], ALU.mult), "V", 1)
                b.inc(v.tensor_tensor(t2, UI[:, s], FR[:, s], ALU.mult), "V", 1)
                b.inc(v.tensor_tensor(WIB[:, s], t1, t2, ALU.add), "V", 1)
        b.on("vector", _vf)

        t0 = b.cnt["T"]
        v0 = b.cnt["V"]

        def _t(t, toff=toff, tw=tw, t0=t0, v0=v0):
            for oc in range(NCH):
                bank = (psA, psB)[oc % 2]
                if oc >= 2:
                    t.wait_ge(b.sems["V"], v0 + (oc - 1))
                for ic in range(NCH):
                    t.matmul(bank[:, 0:tw],
                             G1B[:, ic * D + oc * 128: ic * D + (oc + 1) * 128],
                             WRB[:, ic * tw:(ic + 1) * tw], start=(ic == 0), stop=False)
                for ic in range(NCH):
                    inst = t.matmul(bank[:, 0:tw],
                                    G2B[:, ic * D + oc * 128: ic * D + (oc + 1) * 128],
                                    WIB[:, ic * tw:(ic + 1) * tw],
                                    start=False, stop=(ic == NCH - 1))
                b.inc(inst, "T", 1)
        b.on("tensor", _t)

        def _v(v, toff=toff, tw=tw, t0=t0):
            for oc in range(NCH):
                v.wait_ge(b.sems["T"], t0 + oc + 1)
                xap = XB[:, oc * w_tok + toff:oc * w_tok + toff + tw]
                b.inc(v.scalar_tensor_tensor(xap, (psA, psB)[oc % 2][:, 0:tw],
                                             lv["pb"](oc), xap, ALU.add, ALU.add), "V", 1)
        b.on("vector", _v, bar=False)
    b.pop()
    return _g3_only(b, nc, ctx, li, lv, tiles, w_tok, XB, ONES, IDF, ROWS_,
                    LNPS, psA, psB, psG, d_fw, d_mw, stages, TW)


def _g3_only(b, nc, ctx, li, lv, tiles, w_tok, XB, ONES, IDF, ROWS_, LNPS,
             psA, psB, psG, d_fw, d_mw, stages, TW):
    if not stages[2]:
        return
    w_tot = sum(w for (_, w) in tiles)
    n3 = (w_tot + 255) // 256
    base3 = w_tot // n3
    rem3 = w_tot - base3 * n3
    tiles3 = []
    o3 = 0
    for i in range(n3):
        w = base3 + (1 if i < rem3 else 0)
        tiles3.append((o3, w))
        o3 += w
    TW = max(w for (_, w) in tiles3)
    tiles = tiles3
    b.push()
    WFC = b.take("WFC", [128, NCH * 4 * D], BF16)
    WMP = b.take("WMP", [128, 32 * D], BF16)
    LNX2 = b.take("LNX2", [128, NCH * TW], BF16)
    GEL = b.take("GEL", [128, NCH * TW], BF16)
    SQ3 = b.alias("SQ3", [128, 2 * TW], F32, GEL)

    b.dma("gpsimd", [(WFC[:, ic * 4 * D:(ic + 1) * 4 * D],
                      d_fw[ic * 128:(ic + 1) * 128, :]) for ic in range(NCH)]
          + [(WMP[:, ic * D:(ic + 1) * D], d_mw[ic * 128:(ic + 1) * 128, :])
             for ic in range(32)], bar=True)

    for (toff, tw) in tiles:
        def xfn(ic, o=toff, n=tw):
            return XB[:, ic * w_tok + o:ic * w_tok + o + n]

        def ln2_out(ic, v, normed, n=tw):
            b.inc(v.tensor_scalar(LNX2[:, ic * n:(ic + 1) * n], normed,
                                  lv["ln2w"](ic), lv["ln2b"](ic), ALU.mult, ALU.add), "V", 1)
        layernorm(b, xfn, tw, LNPS, ROWS_, SQ3, ONES, ln2_out)

        for q in range(4):
            t0 = b.cnt["T"]
            s0 = b.cnt["S"]

            def _t(t, q=q, tw=tw, s0=s0):
                for oc in range(NCH):
                    bank = (psA, psB)[oc % 2]
                    if oc >= 2:
                        t.wait_ge(b.sems["S"], s0 + (oc - 1))
                    for ic in range(NCH):
                        inst = t.matmul(bank[:, 0:tw],
                                        WFC[:, ic * 4 * D + (q * 8 + oc) * 128:
                                            ic * 4 * D + (q * 8 + oc + 1) * 128],
                                        LNX2[:, ic * tw:(ic + 1) * tw],
                                        start=(ic == 0), stop=(ic == NCH - 1))
                    b.inc(inst, "T", 1)
            b.on("tensor", _t)

            def _s(e, q=q, tw=tw, t0=t0):
                for oc in range(NCH):
                    e.wait_ge(b.sems["T"], t0 + oc + 1)
                    b.inc(e.activation(GEL[:, oc * tw:(oc + 1) * tw],
                                       (psA, psB)[oc % 2][:, 0:tw], AF.Gelu,
                                       bias=lv["fcb"](q * 8 + oc)), "S", 1)
            b.on("scalar", _s, bar=False)

            t1 = b.cnt["T"]
            v1 = b.cnt["V"]

            def _t2(t, q=q, toff=toff, tw=tw, v1=v1):
                for oc in range(NCH):
                    bank = (LNPS["mu"], LNPS["rstd"])[oc % 2]
                    if oc >= 2:
                        t.wait_ge(b.sems["V"], v1 + (oc - 1))
                    for j in range(NCH):
                        ic = q * 8 + j
                        inst = t.matmul(bank[:, 0:tw],
                                        WMP[:, ic * D + oc * 128: ic * D + (oc + 1) * 128],
                                        GEL[:, j * tw:(j + 1) * tw],
                                        start=(j == 0), stop=(j == NCH - 1))
                    b.inc(inst, "T", 1)
            b.on("tensor", _t2)

            def _v2(v, q=q, toff=toff, tw=tw, t1=t1):
                for oc in range(NCH):
                    v.wait_ge(b.sems["T"], t1 + oc + 1)
                    xap = XB[:, oc * w_tok + toff:oc * w_tok + toff + tw]
                    bank = (LNPS["mu"], LNPS["rstd"])[oc % 2][:, 0:tw]
                    if q == 0:
                        inst = v.scalar_tensor_tensor(xap, bank, lv["mpb"](oc), xap,
                                                      ALU.add, ALU.add)
                    else:
                        inst = v.tensor_tensor(xap, bank, xap, ALU.add)
                    b.inc(inst, "V", 1)
            b.on("vector", _v2, bar=False)
    b.pop()


def emit_quat(b, tix, tw, TW, QKV, QPREV, TMP, TI, P0T):
    n = tw

    def T_(i):
        return TMP[:, i * TW:i * TW + n]

    for l in range(NCH):
        qs, ks, vs = l, 8 + l, 16 + l
        trip = ((0, qs), (1, ks), (2, vs))

        def _vA(v, l=l, trip=trip):
            for (cx, j) in trip:
                x0 = QKV[:, j * n:(j + 1) * n]
                p1, p2 = T_(3 * cx), T_(3 * cx + 1)
                b.inc(v.tensor_tensor(p1[:, 1:n], QKV[:, j * n + 1:(j + 1) * n],
                                      QKV[:, j * n:(j + 1) * n - 1], ALU.subtract), "V", 1)
                if tix == 0:
                    b.inc(v.memset(p1[:, 0:1], 0.0), "V", 1)
                else:
                    b.inc(v.tensor_tensor(p1[:, 0:1], QKV[:, j * n:j * n + 1],
                                          QPREV[:, j:j + 1], ALU.subtract), "V", 1)
                lj = (l - 1) % 8
                jprev = j - l + lj
                b.inc(v.tensor_copy(p2[0:64, 0:n],
                                    QKV[64:128, jprev * n:(jprev + 1) * n]), "V", 1)
                b.inc(v.tensor_copy(p2[64:128, 0:n], QKV[0:64, j * n:(j + 1) * n]), "V", 1)
                b.inc(v.tensor_scalar(T_(12), x0, 1.0 / (2 * PI), 64.25,
                                      ALU.mult, ALU.add), "V", 1)
                b.inc(v.tensor_copy(TI[:, 0:n], T_(12)), "V", 1)
                b.inc(v.tensor_copy(T_(12), TI[:, 0:n]), "V", 1)
                b.inc(v.scalar_tensor_tensor(T_(9 + cx), T_(12), -2 * PI, x0,
                                             ALU.mult, ALU.add), "V", 1)
                b.inc(v.tensor_scalar(T_(9 + cx), T_(9 + cx), float(128 * PI + PI / 2),
                                      None, ALU.add), "V", 1)
                b.inc(v.tensor_scalar(T_(12), T_(9 + cx), float(PI), None, ALU.is_gt),
                      "V", 1)
                b.inc(v.scalar_tensor_tensor(T_(9 + cx), T_(12), -2 * PI, T_(9 + cx),
                                             ALU.mult, ALU.add), "V", 1)
        b.on("vector", _vA)
        vA = b.cnt["V"]

        def _sA(e, l=l):
            e.wait_ge(b.sems["V"], vA)
            for cx in range(3):
                b.inc(e.activation(T_(3 * cx + 2), T_(9 + cx), AF.Sin), "S", 1)
        b.on("scalar", _sA, bar=False)
        sA = b.cnt["S"]

        def _vB(v, l=l):
            v.wait_ge(b.sems["S"], sA)
            qs, ks, vs = l, 8 + l, 16 + l
            qq = (QKV[:, qs * n:(qs + 1) * n], T_(0), T_(1), T_(2))
            kk = (QKV[:, ks * n:(ks + 1) * n], T_(3), T_(4), T_(5))
            vv = (QKV[:, vs * n:(vs + 1) * n], T_(6), T_(7), T_(8))
            qk = [T_(9), T_(10), T_(11), T_(12)]
            acc, tmp = T_(13), T_(14)
            SIGNS = [
                [(0, 0, 1), (1, 1, -1), (2, 2, -1), (3, 3, -1)],
                [(0, 1, 1), (1, 0, 1), (2, 3, 1), (3, 2, -1)],
                [(0, 2, 1), (1, 3, -1), (2, 0, 1), (3, 1, 1)],
                [(0, 3, 1), (1, 2, 1), (2, 1, -1), (3, 0, 1)],
            ]
            for ci, terms in enumerate(SIGNS):
                dst = qk[ci]
                for idx, (ia, ib, sg) in enumerate(terms):
                    if idx == 0:
                        b.inc(v.tensor_tensor(dst, qq[ia], kk[ib], ALU.mult), "V", 1)
                        assert sg > 0
                    else:
                        b.inc(v.tensor_tensor(tmp, qq[ia], kk[ib], ALU.mult), "V", 1)
                        b.inc(v.tensor_tensor(dst, dst, tmp,
                                              ALU.add if sg > 0 else ALU.subtract), "V", 1)
            b.inc(v.tensor_tensor(acc, qk[0], vv[0], ALU.mult), "V", 1)
            for ci in (1, 2, 3):
                b.inc(v.tensor_tensor(tmp, qk[ci], vv[ci], ALU.mult), "V", 1)
                b.inc(v.tensor_tensor(acc, acc, tmp, ALU.subtract), "V", 1)
            b.inc(v.tensor_copy(P0T[:, l * n:(l + 1) * n], acc), "V", 1)
        b.on("vector", _vB, bar=False)


def emit_l0_ffn(b, nc, ctx, tiles, w_tok, XB, ONES, IDF, ROWS_, LNPS,
                psA, psB, psG, d_W, d_ld, d_lu, d_da, d_ua, d_ffnb):
    b.push()
    WF = b.take("WF", [128, NCH * D], F32)
    WTF = b.take("WTF", [128, NCH * D], F32)
    WTB = b.take("WTB", [128, NCH * D], BF16)
    UAS = b.take("UAS", [128, NCH * 64], F32)
    LDS = b.take("LDS", [4, 128], F32)
    LDT = b.take("LDT", [100, 8], F32)
    DAS = b.take("DAS", [100, D], F32)
    M1S = b.take("M1S", [4, D], F32)
    LUS = b.take("LUS", [50, 8], F32)
    LUT_ = b.take("LUT", [4, 64], F32)
    M2S = b.take("M2S", [50, D], F32)
    UAT = b.take("UAT", [50, D], F32)
    FB = b.take("FB", [128, 8], F32)
    TW = max(w for (_, w) in tiles)
    LNX3 = b.take("LNX3", [128, NCH * TW], BF16)
    SQ4 = b.take("SQ4", [128, 2 * TW], F32)

    b.dma("sync", [(WF[:, ic * D:(ic + 1) * D], d_W[ic * 128:(ic + 1) * 128, :])
                   for ic in range(NCH)]
          + [(LDS[0:4, 0:100], d_ld[:]), (DAS[0:100, :], d_da[:]),
             (LUS[0:50, 0:4], d_lu[:]), (FB[:], d_ffnb[:])]
          + [(UAS[:, ic * 64:ic * 64 + 50], d_ua[ic * 128:(ic + 1) * 128, :])
             for ic in range(NCH)], bar=True)

    t0 = b.cnt["T"]
    s0 = b.cnt["S"]

    def _t(t):
        k = 0
        for ic in range(NCH):
            for jc in range(NCH):
                if k >= 2:
                    t.wait_ge(b.sems["S"], s0 + (k - 1))
                b.inc(t.transpose((psA, psB)[k % 2][:, 0:128],
                                  WF[:, ic * D + jc * 128: ic * D + (jc + 1) * 128],
                                  IDF[:]), "T", 1)
                k += 1
    b.on("tensor", _t)

    def _s(e):
        k = 0
        for ic in range(NCH):
            for jc in range(NCH):
                e.wait_ge(b.sems["T"], t0 + k + 1)
                b.inc(e.copy(WTF[:, jc * D + ic * 128: jc * D + (ic + 1) * 128],
                             (psA, psB)[k % 2][:, 0:128]), "S", 1)
                k += 1
    b.on("scalar", _s, bar=False)

    def _t1(t):
        b.inc(t.transpose(psA[0:100, 0:4], LDS[0:4, 0:100], IDF[0:4, 0:4]), "T", 1)
        b.inc(t.transpose(psB[0:4, 0:50], LUS[0:50, 0:4], IDF[0:50, 0:50]), "T", 1)
    b.on("tensor", _t1)
    tS = b.cnt["T"]

    def _s1(e):
        e.wait_ge(b.sems["T"], tS)
        b.inc(e.copy(LDT[0:100, 0:4], psA[0:100, 0:4]), "S", 1)
        b.inc(e.copy(LUT_[0:4, 0:50], psB[0:4, 0:50]), "S", 1)
    b.on("scalar", _s1, bar=False)

    for ic in range(NCH):
        def _tu(t, ic=ic):
            b.inc(t.transpose((psA, psB)[ic % 2][0:50, 0:128],
                              UAS[:, ic * 64:ic * 64 + 50], IDF[:]), "T", 1)
        b.on("tensor", _tu)
        tc = b.cnt["T"]

        def _su(e, ic=ic, tc=tc):
            e.wait_ge(b.sems["T"], tc)
            b.inc(e.copy(UAT[0:50, ic * 128:(ic + 1) * 128],
                         (psA, psB)[ic % 2][0:50, 0:128]), "S", 1)
        b.on("scalar", _su, bar=False)

    def _t3(t):
        b.inc(t.matmul(psA[0:4, 0:512], LDT[0:100, 0:4], DAS[0:100, 0:512],
                       start=True, stop=True), "T", 1)
        b.inc(t.matmul(psB[0:4, 0:512], LDT[0:100, 0:4], DAS[0:100, 512:1024],
                       start=True, stop=True), "T", 1)
    b.on("tensor", _t3)
    tM1 = b.cnt["T"]

    def _sM1(e):
        e.wait_ge(b.sems["T"], tM1)
        b.inc(e.copy(M1S[0:4, 0:512], psA[0:4, 0:512]), "S", 1)
        b.inc(e.copy(M1S[0:4, 512:1024], psB[0:4, 0:512]), "S", 1)
    b.on("scalar", _sM1, bar=False)

    def _t4(t):
        b.inc(t.matmul(psA[0:50, 0:512], LUT_[0:4, 0:50], M1S[0:4, 0:512],
                       start=True, stop=True), "T", 1)
        b.inc(t.matmul(psB[0:50, 0:512], LUT_[0:4, 0:50], M1S[0:4, 512:1024],
                       start=True, stop=True), "T", 1)
    b.on("tensor", _t4)
    tM2 = b.cnt["T"]

    def _sM2(e):
        e.wait_ge(b.sems["T"], tM2)
        b.inc(e.copy(M2S[0:50, 0:512], psA[0:50, 0:512]), "S", 1)
        b.inc(e.copy(M2S[0:50, 512:1024], psB[0:50, 0:512]), "S", 1)
    b.on("scalar", _sM2, bar=False)

    for ic in range(NCH):
        def _t5(t, ic=ic):
            b.inc(t.matmul(psA[:, 0:512], M2S[0:50, ic * 128:(ic + 1) * 128],
                           UAT[0:50, 0:512], start=True, stop=True), "T", 1)
            b.inc(t.matmul(psB[:, 0:512], M2S[0:50, ic * 128:(ic + 1) * 128],
                           UAT[0:50, 512:1024], start=True, stop=True), "T", 1)
        b.on("tensor", _t5)
        tD = b.cnt["T"]

        def _vd(v, ic=ic, tD=tD):
            v.wait_ge(b.sems["T"], tD)
            for half in range(2):
                ap = WTF[:, ic * D + half * 512: ic * D + (half + 1) * 512]
                b.inc(v.tensor_tensor(ap, ap, (psA, psB)[half][:, 0:512], ALU.add), "V", 1)
        b.on("vector", _vd, bar=False)

    def _vc(v):
        for ic in range(NCH):
            b.inc(v.tensor_copy(WTB[:, ic * D:(ic + 1) * D], WTF[:, ic * D:(ic + 1) * D]),
                  "V", 1)
    b.on("vector", _vc)

    for (toff, tw) in tiles:
        def xfn(ic, o=toff, n=tw):
            return XB[:, ic * w_tok + o:ic * w_tok + o + n]

        def _vb(v, toff=toff, tw=tw):
            for ic in range(NCH):
                b.inc(v.tensor_copy(LNX3[:, ic * tw:(ic + 1) * tw],
                                    XB[:, ic * w_tok + toff:ic * w_tok + toff + tw]), "V", 1)
        b.on("vector", _vb)

        t6 = b.cnt["T"]
        v6 = b.cnt["V"]

        def _t6(t, tw=tw, v6=v6):
            for oc in range(NCH):
                bank = (psA, psB)[oc % 2]
                if oc >= 2:
                    t.wait_ge(b.sems["V"], v6 + (oc - 1))
                for ic in range(NCH):
                    inst = t.matmul(bank[:, 0:tw],
                                    WTB[:, ic * D + oc * 128: ic * D + (oc + 1) * 128],
                                    LNX3[:, ic * tw:(ic + 1) * tw],
                                    start=(ic == 0), stop=(ic == NCH - 1))
                b.inc(inst, "T", 1)
        b.on("tensor", _t6)

        def _v6(v, toff=toff, tw=tw, t6=t6):
            for oc in range(NCH):
                v.wait_ge(b.sems["T"], t6 + oc + 1)
                xap = XB[:, oc * w_tok + toff:oc * w_tok + toff + tw]
                b.inc(v.scalar_tensor_tensor(xap, (psA, psB)[oc % 2][:, 0:tw],
                                             FB[:, oc:oc + 1], xap, ALU.add, ALU.add), "V", 1)
        b.on("vector", _v6, bar=False)

        def out_plain(ic, v, normed, o=toff, n=tw):
            b.inc(v.tensor_copy(XB[:, ic * w_tok + o:ic * w_tok + o + n], normed), "V", 1)
        layernorm(b, xfn, tw, LNPS, ROWS_, SQ4, ONES, out_plain)
    b.pop()


def build_l1pre():
    nc = bass.Bass("TRN2", target_bir_lowering=False, debug=False)
    W2 = RPC + 2
    x_in = nc.dram_tensor("x2p", [D, W2], F32, kind="ExternalInput").ap()
    d_wr = nc.dram_tensor("W_r", [D, D], F32, kind="ExternalInput").ap()
    d_wk = nc.dram_tensor("W_k", [D, D], F32, kind="ExternalInput").ap()
    d_wv = nc.dram_tensor("W_v", [D, D], F32, kind="ExternalInput").ap()
    d_wg = nc.dram_tensor("W_g", [D, D], F32, kind="ExternalInput").ap()
    d_w1 = nc.dram_tensor("maa_w1", [D, 160], F32, kind="ExternalInput").ap()
    d_w2 = nc.dram_tensor("maa_w2", [5 * 32, D], F32, kind="ExternalInput").ap()
    d_tdw1 = nc.dram_tensor("tdw1", [D, 64], F32, kind="ExternalInput").ap()
    d_tdw2 = nc.dram_tensor("tdw2", [64, D], F32, kind="ExternalInput").ap()
    d_vec = nc.dram_tensor("vecs", [128, 64], F32, kind="ExternalInput").ap()
    d_hdb = nc.dram_tensor("hdb", [1, 16], F32, kind="ExternalInput").ap()
    o_r = nc.dram_tensor("r_bf", [D, RPC], BF16, kind="ExternalOutput").ap()
    o_k = nc.dram_tensor("k_bf", [D, RPC], BF16, kind="ExternalOutput").ap()
    o_v = nc.dram_tensor("v_bf", [D, RPC], BF16, kind="ExternalOutput").ap()
    o_g = nc.dram_tensor("g", [D, RPC], F32, kind="ExternalOutput").ap()
    o_mw = nc.dram_tensor("meanw", [16, RPC], F32, kind="ExternalOutput").ap()

    tiles = []
    off = 0
    for w in tile_widths(RPC):
        tiles.append((off, w))
        off += w
    TW = max(w for (_, w) in tiles)

    ctx = ExitStack()
    with ctx:
        b = Bld(nc, ctx)
        XT = b.take("XT", [128, NCH * (TW + 2)], F32)
        VEC = b.take("VEC", [128, 64], F32)
        ONES = b.take("ONES", [128, 128], F32)
        ROWS_ = tuple(b.take(f"ROW{i}", [1, 512], F32) for i in range(2))
        WPROJ = b.take("WPROJ", [128, 4 * NCH * D], BF16)
        W1B = b.take("W1B", [128, NCH * 160], BF16)
        W2B = b.take("W2B", [32, 5 * D], BF16)
        TD1 = b.take("TD1", [128, NCH * 64], BF16)
        TD2 = b.take("TD2", [64, D], BF16)
        HDBE = b.take("HDBE", [1, 16], F32)
        DX = b.take("DX", [128, NCH * TW], F32)
        XXX = b.take("XXX", [128, NCH * TW], BF16)
        X5 = b.take("X5", [128, 5 * NCH * TW], BF16)
        M5 = b.take("M5", [32, 5 * TW], BF16)
        MBT = b.take("MBT", [64, TW], F32)
        TT_ = b.take("TT", [128, 2 * TW], F32)
        OUTT = b.take("OUTT", [128, NCH * TW], BF16)
        MWT = b.take("MWT", [1, 2 * TW], F32)
        TWB = b.take("TWB", [64, TW], BF16)
        EXPT = b.take("EXPT", [128, TW], F32)
        EXPT2 = b.take("EXPT2", [64, TW], F32)
        GT = b.alias("GT", [128, NCH * TW], F32, DX)

        psA = ctx.enter_context(nc.psum_tensor("psA", [128, 512], F32))
        psB = ctx.enter_context(nc.psum_tensor("psB", [128, 512], F32))
        psC = ctx.enter_context(nc.psum_tensor("psC", [128, 512], F32))
        psD = ctx.enter_context(nc.psum_tensor("psD", [128, 512], F32))

        b.dma("sync", [(VEC[:], d_vec[:]), (HDBE[:], d_hdb[:])])
        b.dma("gpsimd",
              [(WPROJ[:, (w * NCH + ic) * D:(w * NCH + ic) * D + D],
                dw[ic * 128:(ic + 1) * 128, :])
               for w, dw in enumerate((d_wr, d_wk, d_wv, d_wg)) for ic in range(NCH)]
              + [(W1B[:, ic * 160:(ic + 1) * 160], d_w1[ic * 128:(ic + 1) * 128, :])
                 for ic in range(NCH)]
              + [(W2B[0:32, f * D:(f + 1) * D], d_w2[f * 32:(f + 1) * 32, :])
                 for f in range(5)]
              + [(TD1[:, ic * 64:(ic + 1) * 64], d_tdw1[ic * 128:(ic + 1) * 128, :])
                 for ic in range(NCH)]
              + [(TD2[0:64, :], d_tdw2[:])])

        def _v0(v):
            b.inc(v.memset(ONES[:], 1.0), "V", 1)
        b.on("vector", _v0)

        def _s0(e):
            b.inc(e.activation(HDBE[:], HDBE[:], AF.Exp), "S", 1)
        b.on("scalar", _s0)
        s0_ = b.cnt["S"]

        def _v1(v, s0_=s0_):
            v.wait_ge(b.sems["S"], s0_)
            b.inc(v.tensor_scalar(HDBE[:], HDBE[:], -1.0 / 64.0, None, ALU.mult), "V", 1)
        b.on("vector", _v1, bar=False)

        MAAX, MAAW, MAAK, MAAV, MAAR, MAAG, TDC = [
            (lambda j, o=o: VEC[:, o * 8 + j:o * 8 + j + 1]) for o in range(7)]

        for (toff, tw) in tiles:
            WS = TW + 2
            b.dma("sync", [(XT[:, ic * WS:ic * WS + tw + 2],
                            x_in[ic * 128:(ic + 1) * 128, toff:toff + tw + 2])
                           for ic in range(NCH)], bar=True)

            def _vd(v, tw=tw, WS=WS):
                for ic in range(NCH):
                    xs = XT[:, ic * WS + 1:ic * WS + 1 + tw]
                    xp = XT[:, ic * WS:ic * WS + tw]
                    xn = XT[:, ic * WS + 2:ic * WS + 2 + tw]
                    dx = DX[:, ic * tw:(ic + 1) * tw]
                    b.inc(v.tensor_tensor(dx, xp, xn, ALU.add), "V", 1)
                    b.inc(v.scalar_tensor_tensor(dx, dx, 0.5, xs, ALU.mult, ALU.subtract),
                          "V", 1)
                    t = TT_[:, 0:tw]
                    b.inc(v.tensor_scalar(t, dx, MAAX(ic), None, ALU.mult), "V", 1)
                    b.inc(v.tensor_tensor(XXX[:, ic * tw:(ic + 1) * tw], t, xs, ALU.add),
                          "V", 1)
            b.on("vector", _vd)

            t0 = b.cnt["T"]

            def _tm(t, tw=tw):
                for ic in range(NCH):
                    inst = t.matmul(psA[:, 0:tw], W1B[:, ic * 160:ic * 160 + 128],
                                    XXX[:, ic * tw:(ic + 1) * tw],
                                    start=(ic == 0), stop=(ic == NCH - 1))
                b.inc(inst, "T", 1)
                for ic in range(NCH):
                    inst = t.matmul(psB[0:32, 0:tw], W1B[:, ic * 160 + 128:ic * 160 + 160],
                                    XXX[:, ic * tw:(ic + 1) * tw],
                                    start=(ic == 0), stop=(ic == NCH - 1))
                b.inc(inst, "T", 1)
            b.on("tensor", _tm)
            tm = b.cnt["T"]

            def _sm(e, tw=tw, tm=tm):
                e.wait_ge(b.sems["T"], tm)
                b.inc(e.activation(M5[0:32, 0 * TW:0 * TW + tw], psA[0:32, 0:tw], AF.Tanh), "S", 1)
                b.inc(e.activation(M5[0:32, 1 * TW:1 * TW + tw], psA[32:64, 0:tw], AF.Tanh), "S", 1)
                b.inc(e.activation(MBT[0:64, 0:tw], psA[64:128, 0:tw], AF.Tanh), "S", 1)
                b.inc(e.activation(M5[0:32, 4 * TW:4 * TW + tw], psB[0:32, 0:tw], AF.Tanh), "S", 1)
            b.on("scalar", _sm, bar=False)
            sm_ = b.cnt["S"]

            def _vmx(v, tw=tw, sm_=sm_):
                v.wait_ge(b.sems["S"], sm_)
                b.inc(v.tensor_copy(M5[0:32, 2 * TW:2 * TW + tw], MBT[0:32, 0:tw]), "V", 1)
                b.inc(v.tensor_copy(M5[0:32, 3 * TW:3 * TW + tw], MBT[32:64, 0:tw]), "V", 1)
            b.on("vector", _vmx, bar=False)

            maas = [MAAW, MAAK, MAAV, MAAR, MAAG]
            for f in range(5):
                t1 = b.cnt["T"]
                v1 = b.cnt["V"]

                def _tf(t, f=f, tw=tw, v1=v1):
                    for oc in range(NCH):
                        bank = (psC, psD)[oc % 2]
                        if oc >= 2:
                            t.wait_ge(b.sems["V"], v1 + 3 * (oc - 1))
                        b.inc(t.matmul(bank[:, 0:tw],
                                       W2B[0:32, f * D + oc * 128: f * D + (oc + 1) * 128],
                                       M5[0:32, f * TW:f * TW + tw],
                                       start=True, stop=True), "T", 1)
                b.on("tensor", _tf)

                def _vf(v, f=f, tw=tw, t1=t1):
                    for oc in range(NCH):
                        v.wait_ge(b.sems["T"], t1 + oc + 1)
                        bank = (psC, psD)[oc % 2][:, 0:tw]
                        t_ = TT_[:, 0:tw]
                        b.inc(v.tensor_scalar(t_, bank, maas[f](oc), None, ALU.add),
                              "V", 1)
                        b.inc(v.tensor_tensor(t_, t_, DX[:, oc * tw:(oc + 1) * tw],
                                              ALU.mult), "V", 1)
                        b.inc(v.tensor_tensor(
                            X5[:, (f * NCH + oc) * tw:(f * NCH + oc) * tw + tw],
                            t_, XT[:, oc * (TW + 2) + 1:oc * (TW + 2) + 1 + tw],
                            ALU.add), "V", 1)
                b.on("vector", _vf, bar=False)

            for (name, f, wslot, oap, act) in (
                    ("r", 3, 0, o_r, "scale8"), ("k", 1, 1, o_k, None),
                    ("v", 2, 2, o_v, None), ("g", 4, 3, o_g, "silu")):
                def xf(ic, f=f, tw=tw):
                    return X5[:, (f * NCH + ic) * tw:(f * NCH + ic) * tw + tw]

                if act == "silu":
                    def ev(oc, ps_ap, e, tw=tw):
                        return e.activation(GT[:, oc * tw:(oc + 1) * tw], ps_ap, AF.Silu)
                elif act == "scale8":
                    def ev(oc, ps_ap, e, tw=tw):
                        return e.activation(OUTT[:, oc * tw:(oc + 1) * tw], ps_ap,
                                            AF.Copy, scale=0.125)
                else:
                    def ev(oc, ps_ap, e, tw=tw):
                        return e.copy(OUTT[:, oc * tw:(oc + 1) * tw], ps_ap)
                proj(b, WPROJ[:, wslot * NCH * D:(wslot + 1) * NCH * D] if False else WPROJ,
                     0, xf, list(range(NCH)), 0, 0, (psA, psB), "scalar", ev) if False                     else None
                t2 = b.cnt["T"]
                s2 = b.cnt["S"]
                gd2 = b.cnt["GD"]

                def _tp(t, f=f, wslot=wslot, tw=tw, s2=s2):
                    for oc in range(NCH):
                        bank = (psA, psB)[oc % 2]
                        if oc >= 2:
                            t.wait_ge(b.sems["S"], s2 + (oc - 1))
                        for ic in range(NCH):
                            inst = t.matmul(
                                bank[:, 0:tw],
                                WPROJ[:, (wslot * NCH + ic) * D + oc * 128:
                                      (wslot * NCH + ic) * D + (oc + 1) * 128],
                                X5[:, (f * NCH + ic) * tw:(f * NCH + ic) * tw + tw],
                                start=(ic == 0), stop=(ic == NCH - 1))
                        b.inc(inst, "T", 1)
                b.on("tensor", _tp)

                def _sp(e, ev=ev, tw=tw, t2=t2, gd2=gd2):
                    if gd2 > 0:
                        e.wait_ge(b.sems["GD"], gd2)
                    for oc in range(NCH):
                        e.wait_ge(b.sems["T"], t2 + oc + 1)
                        b.inc(ev(oc, (psA, psB)[oc % 2][:, 0:tw], e), "S", 1)
                b.on("scalar", _sp, bar=False)
                if name == "g":
                    b.dma("gpsimd", [(o_g[ic * 128:(ic + 1) * 128, toff:toff + tw],
                                      GT[:, ic * tw:(ic + 1) * tw]) for ic in range(NCH)],
                          bar=True)
                else:
                    b.dma("gpsimd", [(oap[ic * 128:(ic + 1) * 128, toff:toff + tw],
                                      OUTT[:, ic * tw:(ic + 1) * tw]) for ic in range(NCH)],
                          bar=True)

            t3 = b.cnt["T"]

            def _tw(t, tw=tw):
                for ic in range(NCH):
                    inst = t.matmul(psC[0:64, 0:tw], TD1[:, ic * 64:(ic + 1) * 64],
                                    X5[:, (0 * NCH + ic) * tw:(0 * NCH + ic) * tw + tw],
                                    start=(ic == 0), stop=(ic == NCH - 1))
                b.inc(inst, "T", 1)
            b.on("tensor", _tw)
            tw3 = b.cnt["T"]

            def _sw(e, tw=tw, tw3=tw3):
                e.wait_ge(b.sems["T"], tw3)
                b.inc(e.activation(TWB[0:64, 0:tw], psC[0:64, 0:tw], AF.Tanh), "S", 1)
            b.on("scalar", _sw, bar=False)
            sw3 = b.cnt["S"]

            for oc in range(NCH):
                def _tw2(t, oc=oc, tw=tw, sw3=sw3):
                    t.wait_ge(b.sems["S"], sw3)
                    b.inc(t.matmul(psD[:, 0:tw], TD2[0:64, oc * 128:(oc + 1) * 128],
                                   TWB[0:64, 0:tw], start=True, stop=True), "T", 1)
                b.on("tensor", _tw2)
                tc = b.cnt["T"]

                def _se(e, oc=oc, tw=tw, tc=tc):
                    e.wait_ge(b.sems["T"], tc)
                    b.inc(e.activation(EXPT[:, 0:tw], psD[:, 0:tw], AF.Exp,
                                       bias=TDC(oc)), "S", 1)
                b.on("scalar", _se, bar=False)
                se_ = b.cnt["S"]
                def _ve(v, oc=oc, tw=tw, se_=se_):
                    v.wait_ge(b.sems["S"], se_)
                    b.inc(v.tensor_copy(EXPT2[0:64, 0:tw], EXPT[64:128, 0:tw]), "V", 1)
                b.on("vector", _ve, bar=False)
                ve_ = b.cnt["V"]

                def _tm2(t, oc=oc, tw=tw, se_=se_, ve_=ve_):
                    t.wait_ge(b.sems["S"], se_)
                    t.wait_ge(b.sems["V"], ve_)
                    b.inc(t.matmul(psC[0:1, 0:tw], ONES[0:64, 1:2], EXPT[0:64, 0:tw],
                                   start=True, stop=True), "T", 1)
                    b.inc(t.matmul(psB[0:1, 0:tw], ONES[0:64, 2:3], EXPT2[0:64, 0:tw],
                                   start=True, stop=True), "T", 1)
                b.on("tensor", _tm2, bar=False)
                tm2 = b.cnt["T"]

                def _vm(v, oc=oc, tw=tw, tm2=tm2):
                    v.wait_ge(b.sems["T"], tm2)
                    b.inc(v.tensor_scalar(MWT[0:1, 0:tw], psC[0:1, 0:tw],
                                          HDBE[0:1, 2 * oc:2 * oc + 1], None, ALU.mult),
                          "V", 1)
                    b.inc(v.tensor_scalar(MWT[0:1, TW:TW + tw], psB[0:1, 0:tw],
                                          HDBE[0:1, 2 * oc + 1:2 * oc + 2], None, ALU.mult),
                          "V", 1)
                b.on("vector", _vm, bar=True)
                b.dma("gpsimd", [(o_mw[2 * oc:2 * oc + 1, toff:toff + tw],
                                  MWT[0:1, 0:tw]),
                                 (o_mw[2 * oc + 1:2 * oc + 2, toff:toff + tw],
                                  MWT[0:1, TW:TW + tw])], bar=True)
    return nc


def build_l1attn(soff, dbg=False):
    nc = bass.Bass("TRN2", target_bir_lowering=False, debug=False)
    if dbg:
        d_dbgA = nc.dram_tensor("dbgA", [128, 1024], F32, kind="ExternalOutput").ap()
        d_dbgB = nc.dram_tensor("dbgB", [64, 4096], F32, kind="ExternalOutput").ap()
        d_dbgC = nc.dram_tensor("dbgC", [1, SEQ], F32, kind="ExternalOutput").ap()
    d_r = nc.dram_tensor("r_bf", [D, RPC], BF16, kind="ExternalInput").ap()
    d_k = nc.dram_tensor("kAll", [D, SEQ], BF16, kind="ExternalInput").ap()
    d_v = nc.dram_tensor("vAll", [D, SEQ], BF16, kind="ExternalInput").ap()
    d_mw = nc.dram_tensor("mwAll", [16, SEQ], F32, kind="ExternalInput").ap()
    d_slow = nc.dram_tensor("slow_row", [1, 16], F32, kind="ExternalInput").ap()
    d_alpha = nc.dram_tensor("alpha2", [64, 16], F32, kind="ExternalInput").ap()
    d_ident = nc.dram_tensor("ident", [128, 128], F32, kind="ExternalInput").ap()
    o_y = nc.dram_tensor("y_attn", [D, RPC], F32, kind="ExternalOutput").ap()

    NSTRIP, SW, NJ = 4, 512, SEQ // 128

    ctx = ExitStack()
    with ctx:
        b = Bld(nc, ctx)
        MWR = b.take("MWR", [1, SEQ], F32)
        CH = b.take("CH", [1, SEQ], F32)
        ZROW = b.take("ZROW", [1, SEQ], F32)
        CROWS = b.take("CROWS", [1, 512], F32)
        CMAT = b.take("CMAT", [128, 64], F32)
        CMATS = b.take("CMATS", [128, 64], F32)
        KH = b.take("KH", [64, SEQ], BF16)
        VH = b.take("VH", [64, SEQ], BF16)
        VT = b.take("VT", [128, 64 * 64], BF16)
        RH = b.take("RH", [64, RPC], BF16)
        IDF = b.take("IDF", [128, 128], F32)
        IDB = b.take("IDB", [128, 128], BF16)
        ONES = b.take("ONES", [128, 128], F32)
        SIGR = b.take("SIGR", [1, 16], F32)
        SIGB = b.take("SIGB", [128, 16], F32)
        ALPH = b.take("ALPH", [64, 16], F32)
        SCL = b.take("SCL", [128, 4 * 64], F32)
        CQB = b.take("CQB", [128, 2], F32)
        RVB = b.take("RVB", [64, 4 * SW], F32)
        RROW4 = b.take("RROW4", [1, 4 * SW], F32)
        DROW = b.take("DROW", [1, 2 * SW], F32)
        CTBF = b.take("CTBF", [128, SW], F32)
        CTBS = b.take("CTBS", [128, SW], F32)
        PTILE = b.take("PTILE", [128, 2 * SW], BF16)
        DBGPT = b.take("DBGPT", [128, 2 * SW], F32)
        MTILE = b.take("MTILE", [128, 2 * 64], F32)
        MKT = b.take("MKT", [128, 2 * SW], F32)
        T64 = b.take("T64", [64, 2 * SW], F32)
        YT = b.take("YT", [64, SW], F32)
        YACC = b.take("YACC", [64, 4 * SW], F32)

        psS1 = ctx.enter_context(nc.psum_tensor("psS1", [128, 512], F32))
        psS2 = ctx.enter_context(nc.psum_tensor("psS2", [128, 512], F32))
        psYA = ctx.enter_context(nc.psum_tensor("psYA", [128, 512], F32))
        psYB = ctx.enter_context(nc.psum_tensor("psYB", [128, 512], F32))
        psYC = ctx.enter_context(nc.psum_tensor("psYC", [128, 512], F32))
        psYD = ctx.enter_context(nc.psum_tensor("psYD", [128, 512], F32))
        psG1 = ctx.enter_context(nc.psum_tensor("psG1", [128, 512], F32))
        psTB = ctx.enter_context(nc.psum_tensor("psTB", [128, 512], BF16))

        b.dma("sync", [(IDF[:], d_ident[:]), (SIGR[:], d_slow[:]),
                       (ALPH[0:64, :], d_alpha[:])])

        def _v0(v):
            b.inc(v.memset(ONES[:], 1.0), "V", 1)
            b.inc(v.memset(ZROW[:], 0.0), "V", 1)
            b.inc(v.tensor_copy(IDB[:], IDF[:]), "V", 1)
        b.on("vector", _v0)

        def _s0(e):
            b.inc(e.activation(SIGR[:], SIGR[:], AF.Sigmoid), "S", 1)
            b.inc(e.activation(ALPH[0:64, :], ALPH[0:64, :], AF.Sigmoid), "S", 1)
        b.on("scalar", _s0)

        def _t0(t):
            b.inc(t.matmul(psG1[:, 0:16], ONES[0:1, 0:128], SIGR[:],
                           start=True, stop=True), "T", 1)
            for ps in (psYA, psYB, psYC, psYD):
                b.inc(t.matmul(ps[0:64, 0:512], ONES[0:1, 0:64], ZROW[0:1, 0:512],
                               start=True, stop=True), "T", 1)
        b.on("tensor", _t0)
        tb0 = b.cnt["T"]

        def _v1(v, tb0=tb0):
            v.wait_ge(b.sems["T"], tb0)
            b.inc(v.tensor_copy(SIGB[:], psG1[:, 0:16]), "V", 1)
        b.on("vector", _v1, bar=False)

        for h in range(16):
            b.dma("sync", [(MWR[:], d_mw[h:h + 1, :])], bar=True)
            b.dma("gpsimd", [(KH[0:64, :], d_k[h * 64:(h + 1) * 64, :]),
                             (VH[0:64, :], d_v[h * 64:(h + 1) * 64, :]),
                             (RH[0:64, :], d_r[h * 64:(h + 1) * 64, :])], bar=True)

            def _vc(v, h=h):
                b.inc(v.tensor_tensor_scan(CH[:], MWR[:], ZROW[:], 0.0,
                                           ALU.add, ALU.add), "V", 1)
                v.wait_ge(b.sems["V"], b.cnt["V"])
            b.on("vector", _vc)

            def _tc(t):
                for j in range(64):
                    b.inc(t.transpose(psG1[:, j:j + 1], CH[0:1, j * 128:(j + 1) * 128],
                                      IDF[0:1, 0:1]), "T", 1)
            b.on("tensor", _tc)
            tc = b.cnt["T"]

            def _vc2(v, h=h, tc=tc):
                v.wait_ge(b.sems["T"], tc)
                b.inc(v.tensor_copy(CMAT[:, 0:64], psG1[:, 0:64]), "V", 1)
                v.wait_ge(b.sems["V"], b.cnt["V"])
                b.inc(v.tensor_scalar(CMATS[:, 0:64], CMAT[:, 0:64],
                                      SIGB[:, h:h + 1], None, ALU.mult), "V", 1)
                v.wait_ge(b.sems["V"], b.cnt["V"])
            b.on("vector", _vc2, bar=False)

            for g in range(8):
                def _tv(t, g=g):
                    for jj in range(8):
                        j = g * 8 + jj
                        b.inc(t.transpose(psTB[:, jj * 64:(jj + 1) * 64],
                                          VH[0:64, j * 128:(j + 1) * 128],
                                          IDB[0:64, 0:64]), "T", 1)
                b.on("tensor", _tv)
                tv = b.cnt["T"]

                def _vv(v, g=g, tv=tv):
                    v.wait_ge(b.sems["T"], tv)
                    b.inc(v.tensor_copy(VT[:, g * 512:(g + 1) * 512], psTB[:, 0:512]),
                          "V", 1)
                b.on("vector", _vv, bar=False)

            for si in range(NSTRIP):
                q0 = soff + si * SW
                qm = q0 + SW // 2
                jd = q0 // 128

                def _tq(t, qm=qm):
                    b.inc(t.matmul(psG1[:, 0:1], ONES[0:1, 0:128], CH[0:1, qm:qm + 1],
                                   start=True, stop=True), "T", 1)
                b.on("tensor", _tq)
                tq = b.cnt["T"]

                def _vq(v, h=h, tq=tq, q0=q0, qm=qm):
                    v.wait_ge(b.sems["T"], tq)
                    b.inc(v.tensor_copy(CQB[:, 0:1], psG1[:, 0:1]), "V", 1)
                    v.wait_ge(b.sems["V"], b.cnt["V"])
                    b.inc(v.tensor_scalar(CQB[:, 1:2], CQB[:, 0:1], SIGB[:, h:h + 1],
                                          None, ALU.mult), "V", 1)
                    b.inc(v.tensor_scalar(CROWS[0:1, 0:SW], CH[0:1, q0:q0 + SW],
                                          SIGR[0:1, h:h + 1], None, ALU.mult), "V", 1)
                    for mi, cm in ((0, CMAT), (1, CMATS)):
                        t_ = MTILE[:, mi * 64:mi * 64 + 64]
                        b.inc(v.tensor_scalar(t_, cm[:, 0:64], CQB[:, mi:mi + 1],
                                              None, ALU.subtract), "V", 1)
                        v.wait_ge(b.sems["V"], b.cnt["V"])
                        b.inc(v.tensor_scalar(t_, t_, 80.0, -80.0, ALU.min, ALU.max),
                              "V", 1)
                        v.wait_ge(b.sems["V"], b.cnt["V"])
                    d0 = DROW[0:1, 0:SW]
                    b.inc(v.tensor_scalar(d0, CH[0:1, q0:q0 + SW],
                                          CH[0:1, qm:qm + 1], None, ALU.subtract),
                          "V", 1)
                    v.wait_ge(b.sems["V"], b.cnt["V"])
                    d1 = DROW[0:1, SW:2 * SW]
                    b.inc(v.tensor_scalar(d1, d0, SIGR[0:1, h:h + 1], None, ALU.mult),
                          "V", 1)
                    v.wait_ge(b.sems["V"], b.cnt["V"])
                    b.inc(v.tensor_scalar(d0, d0, 80.0, -80.0, ALU.min, ALU.max), "V", 1)
                    b.inc(v.tensor_scalar(d1, d1, 80.0, -80.0, ALU.min, ALU.max), "V", 1)
                    v.wait_ge(b.sems["V"], b.cnt["V"])
                b.on("vector", _vq, bar=True)
                vq = b.cnt["V"]

                def _sq(e, vq=vq):
                    e.wait_ge(b.sems["V"], vq)
                    for mi in range(2):
                        src = MTILE[:, mi * 64:mi * 64 + 64]
                        b.inc(e.activation(SCL[:, (2 * mi) * 64:(2 * mi) * 64 + 64],
                                           src, AF.Exp, scale=-1.0), "S", 1)
                        b.inc(e.activation(SCL[:, (2 * mi + 1) * 64:(2 * mi + 1) * 64 + 64],
                                           src, AF.Exp), "S", 1)
                        d_ = DROW[0:1, mi * SW:(mi + 1) * SW]
                        b.inc(e.activation(RROW4[0:1, (2 * mi) * SW:(2 * mi) * SW + SW],
                                           d_, AF.Exp), "S", 1)
                        b.inc(e.activation(
                            RROW4[0:1, (2 * mi + 1) * SW:(2 * mi + 1) * SW + SW],
                            d_, AF.Exp, scale=-1.0), "S", 1)
                b.on("scalar", _sq, bar=False)
                sq_ = b.cnt["S"]

                for bi in range(6):
                    def _tbx(t, bi=bi, q0=q0, sq_=sq_):
                        if bi == 0:
                            t.wait_ge(b.sems["S"], sq_)
                        if bi < 4:
                            b.inc(t.matmul(psG1[0:64, 0:SW], ONES[0:1, 0:64],
                                           RROW4[0:1, bi * SW:(bi + 1) * SW],
                                           start=True, stop=True), "T", 1)
                        else:
                            rhs = (CH[0:1, q0:q0 + SW] if bi == 4
                                   else CROWS[0:1, 0:SW])
                            b.inc(t.matmul(psG1[:, 0:SW], ONES[0:1, 0:128], rhs,
                                           start=True, stop=True), "T", 1)
                    b.on("tensor", _tbx)
                    tbx = b.cnt["T"]

                    def _vbx(v, bi=bi, tbx=tbx):
                        v.wait_ge(b.sems["T"], tbx)
                        if bi < 4:
                            b.inc(v.tensor_copy(RVB[0:64, bi * SW:(bi + 1) * SW],
                                                psG1[0:64, 0:SW]), "V", 1)
                        else:
                            b.inc(v.tensor_copy((CTBF, CTBS)[bi - 4][:, 0:SW],
                                                psG1[:, 0:SW]), "V", 1)
                    b.on("vector", _vbx, bar=False)

                t0c = b.cnt["T"]
                v0c = b.cnt["V"]
                s0c = b.cnt["S"]
                diag = lambda j: jd <= j < jd + 4
                bdiag = jd // 4
                NB = NJ // 4
                t_smm, t_av, v_evict, v_badd = {}, {}, {}, {}
                v_diag1, v_diag15, s_diag1, s_diag2 = {}, {}, {}, {}
                vc, tc_, sc = v0c, t0c, s0c
                for j in range(NJ):
                    if diag(j):
                        vc += 2
                        v_diag1[j] = vc
                        vc += 2
                        v_diag15[j] = vc
                        vc += 2
                        sc += 2
                        s_diag1[j] = sc
                        sc += 2
                        s_diag2[j] = sc
                    else:
                        vc += 2
                    v_evict[j] = vc
                    if j % 4 == 3:
                        vc += 4
                        v_badd[j // 4] = vc
                tc_ += 1
                t_smm[0] = tc_
                for j in range(1, NJ):
                    tc_ += 1
                    t_smm[j] = tc_
                    tc_ += 1
                    t_av[j - 1] = tc_
                tc_ += 1
                t_av[NJ - 1] = tc_

                def _tj(t, si=si, jd=jd, bdiag=bdiag):
                    def emit_av(j):
                        t.wait_ge(b.sems["V"], v_evict[j])
                        bi = j // 4
                        if j % 4 == 0 and bi >= 2:
                            t.wait_ge(b.sems["V"], v_badd[bi - 2])
                        lo = j < jd + 4
                        for mi in range(2):
                            ps = ((psYA, psYC) if bi % 2 == 0 else (psYB, psYD))[mi]
                            inst = t.matmul(ps[0:64, 0:SW], VT[:, j * 64:(j + 1) * 64],
                                            PTILE[:, mi * SW:(mi + 1) * SW],
                                            start=False, stop=False,
                                            skip_group_check=True)
                        b.inc(inst, "T", 1)
                        assert b.cnt["T"] == t_av[j]
                    for j in range(NJ):
                        bank = (psS1, psS2)[j % 2]
                        if j >= 1:
                            t.wait_ge(b.sems["V"], v_evict[j - 1])
                        b.inc(t.matmul(bank[:, 0:SW], KH[0:64, j * 128:(j + 1) * 128],
                                       RH[0:64, si * SW:si * SW + SW],
                                       start=True, stop=True), "T", 1)
                        assert b.cnt["T"] == t_smm[j]
                        if j >= 1:
                            emit_av(j - 1)
                    emit_av(NJ - 1)
                b.on("tensor", _tj, bar=False)

                def _vj(v, jd=jd, bdiag=bdiag):
                    for j in range(NJ):
                        bank = (psS1, psS2)[j % 2]
                        v.wait_ge(b.sems["T"], t_smm[j])
                        if j >= 1:
                            v.wait_ge(b.sems["T"], t_av[j - 1])
                        if not diag(j):
                            ci = 0 if j < jd else 1
                            for mi in range(2):
                                b.inc(v.tensor_scalar(
                                    PTILE[:, mi * SW:(mi + 1) * SW], bank[:, 0:SW],
                                    SCL[:, (2 * mi + ci) * 64 + j:
                                        (2 * mi + ci) * 64 + j + 1],
                                    None, ALU.mult), "V", 1)
                        else:
                            for mi, (ctb, cm) in enumerate(((CTBF, CMAT), (CTBS, CMATS))):
                                b.inc(v.tensor_scalar(
                                    MKT[:, mi * SW:(mi + 1) * SW], ctb[:, 0:SW],
                                    cm[:, j:j + 1], None, ALU.subtract), "V", 1)
                            v.wait_ge(b.sems["S"], s_diag1[j])
                            for mi, ctb in enumerate((CTBF, CTBS)):
                                b.inc(v.tensor_tensor(MKT[:, mi * SW:(mi + 1) * SW],
                                                      MKT[:, mi * SW:(mi + 1) * SW],
                                                      ctb[:, 0:SW], ALU.add), "V", 1)
                            v.wait_ge(b.sems["S"], s_diag2[j])
                            for mi in range(2):
                                b.inc(v.tensor_tensor(
                                    PTILE[:, mi * SW:(mi + 1) * SW], bank[:, 0:SW],
                                    MKT[:, mi * SW:(mi + 1) * SW], ALU.mult), "V", 1)

                        if j % 4 == 3:
                            bi = j // 4
                            v.wait_ge(b.sems["T"], t_av[j])
                            lo = bi <= bdiag
                            for mi in range(2):
                                ps = ((psYA, psYC) if bi % 2 == 0 else (psYB, psYD))[mi]
                                seg = (0 if lo else 2) + mi
                                dst = YACC[0:64, seg * SW:(seg + 1) * SW]
                                first = (bi == 0) if lo else (bi == bdiag + 1)
                                if first:
                                    inst = v.tensor_copy(dst, ps[0:64, 0:SW])
                                else:
                                    inst = v.tensor_tensor(dst, dst, ps[0:64, 0:SW],
                                                           ALU.add)
                                b.inc(inst, "V", 1)
                            for mi in range(2):
                                ps = ((psYA, psYC) if bi % 2 == 0 else (psYB, psYD))[mi]
                                b.inc(v.memset(ps[0:64, 0:SW], 0.0), "V", 1)
                            assert b.cnt["V"] == v_badd[bi]
                            if dbg and h == 0 and si == 0 and j == 11:
                                v.tensor_copy(DBGPT[0:64, 0:SW], psYA[0:64, 0:SW])
                            if dbg and h == 0 and si == 0 and j == 15:
                                v.tensor_copy(DBGPT[0:64, SW:2 * SW],
                                              YACC[0:64, 2 * SW:3 * SW])
                b.on("vector", _vj, bar=False)

                def _sj(e, jd=jd):
                    for j in range(jd, jd + 4):
                        e.wait_ge(b.sems["V"], v_diag1[j])
                        for mi in range(2):
                            b.inc(e.activation(MKT[:, mi * SW:(mi + 1) * SW],
                                               MKT[:, mi * SW:(mi + 1) * SW], AF.Abs),
                                  "S", 1)
                        e.wait_ge(b.sems["V"], v_diag15[j])
                        for mi in range(2):
                            b.inc(e.activation(MKT[:, mi * SW:(mi + 1) * SW],
                                               MKT[:, mi * SW:(mi + 1) * SW], AF.Exp,
                                               scale=-1.0, bias=CQB[:, mi:mi + 1]),
                                  "S", 1)
                b.on("scalar", _sj, bar=False)

                up_empty = bdiag == NB - 1

                def _vy(v, h=h, si=si, up_empty=up_empty):
                    v.wait_ge(b.sems["V"], b.cnt["V"])
                    for mi in range(2):
                        dst = T64[0:64, mi * SW:(mi + 1) * SW]
                        b.inc(v.tensor_tensor(dst, YACC[0:64, mi * SW:(mi + 1) * SW],
                                              RVB[0:64, (2 * mi) * SW:(2 * mi) * SW + SW],
                                              ALU.mult), "V", 1)
                        if not up_empty:
                            b.inc(v.tensor_tensor(YT[0:64, 0:SW],
                                                  YACC[0:64, (2 + mi) * SW:(3 + mi) * SW],
                                                  RVB[0:64, (2 * mi + 1) * SW:
                                                      (2 * mi + 1) * SW + SW],
                                                  ALU.mult), "V", 1)
                            v.wait_ge(b.sems["V"], b.cnt["V"])
                            b.inc(v.tensor_tensor(dst, dst, YT[0:64, 0:SW], ALU.add),
                                  "V", 1)
                            v.wait_ge(b.sems["V"], b.cnt["V"])
                    v.wait_ge(b.sems["V"], b.cnt["V"])
                    b.inc(v.tensor_tensor(YT[0:64, 0:SW], T64[0:64, 0:SW],
                                          T64[0:64, SW:2 * SW], ALU.subtract), "V", 1)
                    v.wait_ge(b.sems["V"], b.cnt["V"])
                    b.inc(v.scalar_tensor_tensor(YT[0:64, 0:SW], YT[0:64, 0:SW],
                                                 ALPH[0:64, h:h + 1],
                                                 T64[0:64, SW:2 * SW],
                                                 ALU.mult, ALU.add), "V", 1)
                    v.wait_ge(b.sems["V"], b.cnt["V"])
                b.on("vector", _vy, bar=True)
                b.dma("gpsimd", [(o_y[h * 64:(h + 1) * 64, si * SW:(si + 1) * SW],
                                  YT[0:64, 0:SW])], bar=True)
                if dbg and h == 0 and si == 0:
                    b.dma("gpsimd", [(d_dbgB[0:64, 2048:2048 + 2048],
                                      YACC[0:64, 0:2048])], bar=True)
                if dbg and h == 0 and si == 0:
                    def _vdbg(v):
                        for mi, ps in enumerate((psYA, psYB, psYC, psYD)):
                            b.inc(v.tensor_copy(
                                RVB[0:64, mi * SW:(mi + 1) * SW] if False else
                                T64[0:64, 0:SW], ps[0:64, 0:SW]), "V", 1)
                            b.inc(v.tensor_copy(
                                MKT[0:64, 0:SW] if False else
                                RVB[0:64, mi * SW:(mi + 1) * SW], T64[0:64, 0:SW]),
                                "V", 1)
                    b.on("vector", _vdbg)
                    b.dma("gpsimd", [
                        (d_dbgA[:, 0:64], CMAT[:, 0:64]),
                        (d_dbgA[:, 64:128], CMATS[:, 0:64]),
                        (d_dbgA[:, 128:384], SCL[:, 0:256]),
                        (d_dbgA[:, 384:386], CQB[:, 0:2]),
                        (d_dbgA[:, 386:898], CTBF[:, 0:SW]),
                        (d_dbgB[0:64, 0:2048], RVB[0:64, 0:2048]),
                        (d_dbgB[0:64, 2048:2048 + 512],
                         PTILE[:, 0:SW].bitcast(F32) if False else RVB[0:64, 0:512]),
                        (d_dbgC[:], CH[:]),
                        (d_dbgA[:, 898:1022], DBGPT[:, 0:124]) if False else
                        (d_dbgB[0:64, 0:1024], DBGPT[0:64, 0:1024]),
                        ], bar=True)

        fin = b.cnt["GD"]

        def _fin(e, fin=fin):
            e.wait_ge(b.sems["GD"], fin)
        b.on("gpsimd", _fin, bar=False)
    return nc


def build_l1post():
    nc = bass.Bass("TRN2", target_bir_lowering=False, debug=False)
    d_y = nc.dram_tensor("y_attn", [D, RPC], F32, kind="ExternalInput").ap()
    d_g = nc.dram_tensor("g", [D, RPC], F32, kind="ExternalInput").ap()
    d_x2 = nc.dram_tensor("x2", [D, RPC], F32, kind="ExternalInput").ap()
    d_wo = nc.dram_tensor("wo", [D, D], F32, kind="ExternalInput").ap()
    d_gnv = nc.dram_tensor("gnv", [128, 16], F32, kind="ExternalInput").ap()
    d_w1 = nc.dram_tensor("w1", [D, 4 * D], F32, kind="ExternalInput").ap()
    d_w2 = nc.dram_tensor("w2", [4 * D, D], F32, kind="ExternalInput").ap()
    d_fv = nc.dram_tensor("fv", [128, 56], F32, kind="ExternalInput").ap()
    d_ident = nc.dram_tensor("ident", [128, 128], F32, kind="ExternalInput").ap()
    x_out = nc.dram_tensor("x_out", [D, RPC], F32, kind="ExternalOutput").ap()

    NT = 8
    W = RPC // NT
    GN_EPS = 1e-5 * 64.0

    ctx = ExitStack()
    with ctx:
        b = Bld(nc, ctx)
        WO = b.take("WO", [128, NCH * D], BF16)
        W1B = b.take("W1B", [128, NCH * 4 * D], BF16)
        W2B = b.take("W2B", [128, 32 * D], BF16)
        Y = b.take("Y", [128, NCH * W], F32)
        G = b.take("G", [128, NCH * W], F32)
        X2 = b.take("X2", [128, NCH * W], F32)
        X3 = b.take("X3", [128, NCH * W], F32)
        YN = b.take("YN", [128, NCH * W], BF16)
        H1 = b.take("H1", [128, 32 * W], BF16)
        YL2 = b.take("YL2", [64, W], F32)
        SQU = b.take("SQU", [128, 2 * W], F32)
        MROW = b.take("MROW", [1, 256], F32)
        ONES = b.take("ONES", [128, 128], F32)
        VEC = b.take("VEC", [128, 16], F32)
        FV = b.take("FV", [128, 56], F32)
        ROWS_ = tuple(b.take(f"ROW{i}", [1, 512], F32) for i in range(2))
        SQ2 = b.take("SQ2", [128, 2 * W], F32)
        MUR = ROWS_[0]
        RSR = ROWS_[1]
        TRW = SQ2

        psA = ctx.enter_context(nc.psum_tensor("psA", [128, 512], F32))
        psB = ctx.enter_context(nc.psum_tensor("psB", [128, 512], F32))
        psC = ctx.enter_context(nc.psum_tensor("psC", [128, 512], F32))
        psD = ctx.enter_context(nc.psum_tensor("psD", [128, 512], F32))
        psSum = ctx.enter_context(nc.psum_tensor("psSum", [128, 512], F32))
        psSq = ctx.enter_context(nc.psum_tensor("psSq", [128, 512], F32))
        psMu = ctx.enter_context(nc.psum_tensor("psMu", [128, 512], F32))
        psRstd = ctx.enter_context(nc.psum_tensor("psRstd", [128, 512], F32))
        LNPS = {"sum": psSum, "sq": psSq, "mu": psMu, "rstd": psRstd}

        b.dma("sync", [(VEC[:], d_gnv[:]), (FV[:], d_fv[:])])
        b.dma("gpsimd", [(WO[:, ic * D:(ic + 1) * D], d_wo[ic * 128:(ic + 1) * 128, :])
                         for ic in range(NCH)]
              + [(W1B[:, ic * 4 * D:(ic + 1) * 4 * D],
                  d_w1[ic * 128:(ic + 1) * 128, :]) for ic in range(NCH)]
              + [(W2B[:, ic * D:(ic + 1) * D], d_w2[ic * 128:(ic + 1) * 128, :])
                 for ic in range(32)])

        def _v0(v):
            b.inc(v.memset(ONES[:], 1.0), "V", 1)
            b.inc(v.memset(MROW[0:1, 0:64], 1.0), "V", 1)
            b.inc(v.memset(MROW[0:1, 64:128], 0.0), "V", 1)
            b.inc(v.memset(MROW[0:1, 128:192], 0.0), "V", 1)
            b.inc(v.memset(MROW[0:1, 192:256], 1.0), "V", 1)
        b.on("vector", _v0)

        def GNG(j):
            return VEC[:, j:j + 1]

        def GNB(j):
            return VEC[:, 8 + j:8 + j + 1]

        def FB1(j):
            return FV[:, j:j + 1]

        def FB2(j):
            return FV[:, 32 + j:32 + j + 1]

        def FG(j):
            return FV[:, 40 + j:40 + j + 1]

        def FBt(j):
            return FV[:, 48 + j:48 + j + 1]

        eps_gn = b.ccol(GN_EPS)

        for ti in range(NT):
            toff = ti * W
            b.dma("sync", [(Y[:, ic * W:(ic + 1) * W],
                            d_y[ic * 128:(ic + 1) * 128, toff:toff + W])
                           for ic in range(NCH)]
                  + [(G[:, ic * W:(ic + 1) * W],
                      d_g[ic * 128:(ic + 1) * 128, toff:toff + W]) for ic in range(NCH)]
                  + [(X2[:, ic * W:(ic + 1) * W],
                      d_x2[ic * 128:(ic + 1) * 128, toff:toff + W]) for ic in range(NCH)],
                  bar=True)

            for c in range(NCH):
                yc_ = Y[:, c * W:(c + 1) * W]

                def _vg(v, c=c, yc_=yc_):
                    b.inc(v.tensor_copy(YL2[0:64, 0:W], yc_[64:128, 0:W]), "V", 1)
                b.on("vector", _vg)
                vg = b.cnt["V"]

                def _sg(e, c=c, yc_=yc_, vg=vg):
                    e.wait_ge(b.sems["V"], vg)
                    b.inc(e.activation(SQU[0:64, 0:W], yc_[0:64, 0:W], AF.Square), "S", 1)
                    b.inc(e.activation(SQU[0:64, W:2 * W], YL2[0:64, 0:W], AF.Square),
                          "S", 1)
                b.on("scalar", _sg, bar=False)
                sg = b.cnt["S"]

                def _tg(t, c=c, yc_=yc_, sg=sg):
                    t.wait_ge(b.sems["S"], sg)
                    b.inc(t.matmul(psSum[0:1, 0:W], ONES[0:64, 0:1], yc_[0:64, 0:W],
                                   start=True, stop=True), "T", 1)
                    b.inc(t.matmul(psSum[0:1, W:2 * W], ONES[0:64, 0:1], YL2[0:64, 0:W],
                                   start=True, stop=True), "T", 1)
                    b.inc(t.matmul(psSq[0:1, 0:W], ONES[0:64, 0:1], SQU[0:64, 0:W],
                                   start=True, stop=True), "T", 1)
                    b.inc(t.matmul(psSq[0:1, W:2 * W], ONES[0:64, 0:1],
                                   SQU[0:64, W:2 * W], start=True, stop=True), "T", 1)
                b.on("tensor", _tg, bar=False)
                tg = b.cnt["T"]

                def _vs(v, tg=tg):
                    v.wait_ge(b.sems["T"], tg)
                    b.inc(v.tensor_scalar(MUR[0:1, 0:2 * W], psSum[0:1, 0:2 * W],
                                          1.0 / 64, None, ALU.mult), "V", 1)
                    b.inc(v.tensor_scalar(RSR[0:1, 0:2 * W], psSq[0:1, 0:2 * W],
                                          1.0 / 64, None, ALU.mult), "V", 1)
                    b.inc(v.tensor_tensor(TRW[0:1, 0:2 * W], MUR[0:1, 0:2 * W],
                                          MUR[0:1, 0:2 * W], ALU.mult), "V", 1)
                    b.inc(v.tensor_tensor(RSR[0:1, 0:2 * W], RSR[0:1, 0:2 * W],
                                          TRW[0:1, 0:2 * W], ALU.subtract), "V", 1)
                b.on("vector", _vs, bar=False)
                vs = b.cnt["V"]

                def _ss(e, vs=vs):
                    e.wait_ge(b.sems["V"], vs)
                    b.inc(e.activation(RSR[0:1, 0:2 * W], RSR[0:1, 0:2 * W], AF.Sqrt,
                                       bias=eps_gn[0:1, 0:1]), "S", 1)
                b.on("scalar", _ss, bar=False)
                ss = b.cnt["S"]

                def _vr(v, ss=ss):
                    v.wait_ge(b.sems["S"], ss)
                    b.inc(v.reciprocal(RSR[0:1, 0:2 * W], RSR[0:1, 0:2 * W]), "V", 1)
                b.on("vector", _vr, bar=False)
                vr = b.cnt["V"]

                def _tb(t, vr=vr):
                    t.wait_ge(b.sems["V"], vr)
                    t.matmul(psMu[:, 0:W], MROW[0:1, 0:128], MUR[0:1, 0:W],
                             start=True, stop=False)
                    b.inc(t.matmul(psMu[:, 0:W], MROW[0:1, 128:256], MUR[0:1, W:2 * W],
                                   start=False, stop=True), "T", 1)
                    t.matmul(psRstd[:, 0:W], MROW[0:1, 0:128], RSR[0:1, 0:W],
                             start=True, stop=False)
                    b.inc(t.matmul(psRstd[:, 0:W], MROW[0:1, 128:256], RSR[0:1, W:2 * W],
                                   start=False, stop=True), "T", 1)
                b.on("tensor", _tb, bar=False)
                tbx = b.cnt["T"]

                def _vn(v, c=c, yc_=yc_, tbx=tbx):
                    v.wait_ge(b.sems["T"], tbx)
                    b.inc(v.tensor_tensor(yc_, yc_, psMu[:, 0:W], ALU.subtract), "V", 1)
                    b.inc(v.tensor_tensor(yc_, yc_, psRstd[:, 0:W], ALU.mult), "V", 1)
                    b.inc(v.tensor_scalar(yc_, yc_, GNG(c), GNB(c), ALU.mult, ALU.add),
                          "V", 1)
                    b.inc(v.tensor_tensor(YN[:, c * W:(c + 1) * W], yc_,
                                          G[:, c * W:(c + 1) * W], ALU.mult), "V", 1)
                b.on("vector", _vn, bar=False)

            t0 = b.cnt["T"]
            v0 = b.cnt["V"]

            def _tw(t, v0=v0):
                for oc in range(NCH):
                    bank = (psA, psB)[oc % 2]
                    if oc >= 2:
                        t.wait_ge(b.sems["V"], v0 + (oc - 1))
                    for ic in range(NCH):
                        inst = t.matmul(bank[:, 0:W],
                                        WO[:, ic * D + oc * 128: ic * D + (oc + 1) * 128],
                                        YN[:, ic * W:(ic + 1) * W],
                                        start=(ic == 0), stop=(ic == NCH - 1))
                    b.inc(inst, "T", 1)
            b.on("tensor", _tw)

            def _vw(v, t0=t0):
                for oc in range(NCH):
                    v.wait_ge(b.sems["T"], t0 + oc + 1)
                    xap = X2[:, oc * W:(oc + 1) * W]
                    b.inc(v.tensor_tensor(xap, (psA, psB)[oc % 2][:, 0:W], xap, ALU.add),
                          "V", 1)
            b.on("vector", _vw, bar=False)

            def x2fn(ic):
                return X2[:, ic * W:(ic + 1) * W]

            def to_x3(ic, v, normed):
                b.inc(v.tensor_copy(X3[:, ic * W:(ic + 1) * W], normed), "V", 1)
            layernorm(b, x2fn, W, LNPS, ROWS_, SQ2, ONES, to_x3)

            def _vb(v):
                for ic in range(NCH):
                    b.inc(v.tensor_copy(YN[:, ic * W:(ic + 1) * W],
                                        X3[:, ic * W:(ic + 1) * W]), "V", 1)
            b.on("vector", _vb)

            t1 = b.cnt["T"]
            s1 = b.cnt["S"]

            def _tf(t, s1=s1):
                for oc in range(32):
                    bank = (psA, psB)[oc % 2]
                    if oc >= 2:
                        t.wait_ge(b.sems["S"], s1 + (oc - 1))
                    for ic in range(NCH):
                        inst = t.matmul(bank[:, 0:W],
                                        W1B[:, ic * 4 * D + oc * 128:
                                            ic * 4 * D + (oc + 1) * 128],
                                        YN[:, ic * W:(ic + 1) * W],
                                        start=(ic == 0), stop=(ic == NCH - 1))
                    b.inc(inst, "T", 1)
            b.on("tensor", _tf)

            def _sf(e, t1=t1):
                for oc in range(32):
                    e.wait_ge(b.sems["T"], t1 + oc + 1)
                    b.inc(e.activation(H1[:, oc * W:(oc + 1) * W],
                                       (psA, psB)[oc % 2][:, 0:W], AF.Relu,
                                       bias=FB1(oc)), "S", 1)
            b.on("scalar", _sf, bar=False)

            t2 = b.cnt["T"]
            v2 = b.cnt["V"]

            def _tf2(t, v2=v2):
                for oc in range(NCH):
                    bank = (psC, psD)[oc % 2]
                    if oc >= 2:
                        t.wait_ge(b.sems["V"], v2 + (oc - 1))
                    for ic in range(32):
                        inst = t.matmul(bank[:, 0:W],
                                        W2B[:, ic * D + oc * 128: ic * D + (oc + 1) * 128],
                                        H1[:, ic * W:(ic + 1) * W],
                                        start=(ic == 0), stop=(ic == 31))
                    b.inc(inst, "T", 1)
            b.on("tensor", _tf2)

            def _vf2(v, t2=t2):
                for oc in range(NCH):
                    v.wait_ge(b.sems["T"], t2 + oc + 1)
                    xap = X2[:, oc * W:(oc + 1) * W]
                    b.inc(v.scalar_tensor_tensor(xap, (psC, psD)[oc % 2][:, 0:W],
                                                 FB2(oc), X3[:, oc * W:(oc + 1) * W],
                                                 ALU.add, ALU.add), "V", 1)
            b.on("vector", _vf2, bar=False)

            def x4fn(ic):
                return X2[:, ic * W:(ic + 1) * W]

            def to_v(ic, v, normed):
                b.inc(v.tensor_scalar(Y[:, ic * W:(ic + 1) * W], normed,
                                      FG(ic), FBt(ic), ALU.mult, ALU.add), "V", 1)
            layernorm(b, x4fn, W, LNPS, ROWS_, SQ2, ONES, to_v)

            def _vu(v):
                for ic in range(NCH):
                    b.inc(v.tensor_tensor(Y[:, ic * W:(ic + 1) * W],
                                          Y[:, ic * W:(ic + 1) * W],
                                          X3[:, ic * W:(ic + 1) * W], ALU.add), "V", 1)
            b.on("vector", _vu)

            def yfn(ic):
                return Y[:, ic * W:(ic + 1) * W]

            def to_out(ic, v, normed):
                b.inc(v.tensor_copy(X3[:, ic * W:(ic + 1) * W], normed), "V", 1)
            layernorm(b, yfn, W, LNPS, ROWS_, SQ2, ONES, to_out)

            b.dma("gpsimd", [(x_out[ic * 128:(ic + 1) * 128, toff:toff + W],
                              X3[:, ic * W:(ic + 1) * W]) for ic in range(NCH)],
                  bar=True)

        fin = b.cnt["GD"]

        def _fin(e, fin=fin):
            e.wait_ge(b.sems["GD"], fin)
        b.on("gpsimd", _fin, bar=False)
    return nc


def emit_l1pre(nc, b, ctx, x2src, psA, psB, psC, psD):
    FHM1 = FH - 1
    d_wr = nc.dram_tensor("W_r", [D, D], F32, kind="ExternalInput").ap()
    d_wk = nc.dram_tensor("W_k", [D, D], F32, kind="ExternalInput").ap()
    d_wv = nc.dram_tensor("W_v", [D, D], F32, kind="ExternalInput").ap()
    d_wg = nc.dram_tensor("W_g", [D, D], F32, kind="ExternalInput").ap()
    d_w1 = nc.dram_tensor("maa_w1", [D, 160], F32, kind="ExternalInput").ap()
    d_w2 = nc.dram_tensor("maa_w2", [5 * 32, D], F32, kind="ExternalInput").ap()
    d_tdw1 = nc.dram_tensor("tdw1", [D, 64], F32, kind="ExternalInput").ap()
    d_tdw2 = nc.dram_tensor("tdw2", [64, D], F32, kind="ExternalInput").ap()
    d_vec = nc.dram_tensor("pvecs", [128, 64], F32, kind="ExternalInput").ap()
    d_hdb = nc.dram_tensor("hdb", [1, 16], F32, kind="ExternalInput").ap()
    o_r = nc.dram_tensor("r_bf", [D, RPC], BF16, kind="ExternalOutput").ap()
    o_k = nc.dram_tensor("k_bf", [D, RPC], BF16, kind="ExternalOutput").ap()
    o_v = nc.dram_tensor("v_bf", [D, RPC], BF16, kind="ExternalOutput").ap()
    o_g = nc.dram_tensor("g", [D, RPC], F32, kind="ExternalOutput").ap()
    o_mw = nc.dram_tensor("meanw", [16, RPC], F32, kind="ExternalOutput").ap()

    tiles = []
    off = 0
    for w in tile_widths(RPC):
        tiles.append((off, w))
        off += w
    TW = max(w for (_, w) in tiles)

    d_em = nc.dram_tensor("em", [128, 2], F32, kind="ExternalInput").ap()
    XT = b.take("XT", [128, NCH * (TW + 2)], F32)
    EMS = b.take("EMS", [128, 2], F32)
    VEC = b.take("VEC", [128, 64], F32)
    ONES = b.take("ONES", [128, 128], F32)
    ROWS_ = tuple(b.take(f"ROW{i}", [1, 512], F32) for i in range(2))
    WPROJ = b.take("WPROJ", [128, 4 * NCH * D], BF16)
    W1B = b.take("W1B", [128, NCH * 160], BF16)
    W2B = b.take("W2B", [32, 5 * D], BF16)
    TD1 = b.take("TD1", [128, NCH * 64], BF16)
    TD2 = b.take("TD2", [64, D], BF16)
    HDBE = b.take("HDBE", [1, 16], F32)
    DX = b.take("DX", [128, NCH * TW], F32)
    XXX = b.take("XXX", [128, NCH * TW], BF16)
    X5 = b.take("X5", [128, 5 * NCH * TW], BF16)
    M5 = b.take("M5", [32, 5 * TW], BF16)
    MBT = b.take("MBT", [64, TW], F32)
    TT_ = b.take("TT", [128, 2 * TW], F32)
    OUTT = b.take("OUTT", [128, NCH * TW], BF16)
    MWT = b.take("MWT", [1, 2 * TW], F32)
    TWB = b.take("TWB", [64, TW], BF16)
    EXPT = b.take("EXPT", [128, TW], F32)
    EXPT2 = b.take("EXPT2", [64, TW], F32)
    GT = b.alias("GT", [128, NCH * TW], F32, DX)


    b.dma("sync", [(VEC[:], d_vec[:]), (HDBE[:], d_hdb[:]), (EMS[:], d_em[:])],
          bar=True)
    b.dma("gpsimd",
          [(WPROJ[:, (w * NCH + ic) * D:(w * NCH + ic) * D + D],
            dw[ic * 128:(ic + 1) * 128, :])
           for w, dw in enumerate((d_wr, d_wk, d_wv, d_wg)) for ic in range(NCH)]
          + [(W1B[:, ic * 160:(ic + 1) * 160], d_w1[ic * 128:(ic + 1) * 128, :])
             for ic in range(NCH)]
          + [(W2B[0:32, f * D:(f + 1) * D], d_w2[f * 32:(f + 1) * 32, :])
             for f in range(5)]
          + [(TD1[:, ic * 64:(ic + 1) * 64], d_tdw1[ic * 128:(ic + 1) * 128, :])
             for ic in range(NCH)]
          + [(TD2[0:64, :], d_tdw2[:])], bar=True)

    def _v0(v):
        b.inc(v.memset(ONES[:], 1.0), "V", 1)
    b.on("vector", _v0)

    def _s0(e):
        b.inc(e.activation(HDBE[:], HDBE[:], AF.Exp), "S", 1)
    b.on("scalar", _s0)
    s0_ = b.cnt["S"]

    def _v1(v, s0_=s0_):
        v.wait_ge(b.sems["S"], s0_)
        b.inc(v.tensor_scalar(HDBE[:], HDBE[:], -1.0 / 64.0, None, ALU.mult), "V", 1)
    b.on("vector", _v1, bar=False)

    MAAX, MAAW, MAAK, MAAV, MAAR, MAAG, TDC = [
        (lambda j, o=o: VEC[:, o * 8 + j:o * 8 + j + 1]) for o in range(7)]

    for (toff, tw) in tiles:
        WS = TW + 2
        b.dma("sync", [(XT[:, ic * WS:ic * WS + tw + 2],
                        x2src[ic * 128:(ic + 1) * 128, FHM1 + toff:FHM1 + toff + tw + 2])
                       for ic in range(NCH)], bar=True)

        def _vd(v, tw=tw, WS=WS, toff=toff):
            if toff == 0:
                for ic in range(NCH):
                    b.inc(v.tensor_scalar(XT[:, ic * WS:ic * WS + 1],
                                          XT[:, ic * WS:ic * WS + 1],
                                          EMS[:, 0:1], None, ALU.mult), "V", 1)
            if toff + tw == RPC:
                for ic in range(NCH):
                    b.inc(v.tensor_scalar(XT[:, ic * WS + tw + 1:ic * WS + tw + 2],
                                          XT[:, ic * WS + tw + 1:ic * WS + tw + 2],
                                          EMS[:, 1:2], None, ALU.mult), "V", 1)
            for ic in range(NCH):
                xs = XT[:, ic * WS + 1:ic * WS + 1 + tw]
                xp = XT[:, ic * WS:ic * WS + tw]
                xn = XT[:, ic * WS + 2:ic * WS + 2 + tw]
                dx = DX[:, ic * tw:(ic + 1) * tw]
                b.inc(v.tensor_tensor(dx, xp, xn, ALU.add), "V", 1)
                b.inc(v.scalar_tensor_tensor(dx, dx, 0.5, xs, ALU.mult, ALU.subtract),
                      "V", 1)
                t = TT_[:, 0:tw]
                b.inc(v.tensor_scalar(t, dx, MAAX(ic), None, ALU.mult), "V", 1)
                b.inc(v.tensor_tensor(XXX[:, ic * tw:(ic + 1) * tw], t, xs, ALU.add),
                      "V", 1)
        b.on("vector", _vd)

        t0 = b.cnt["T"]

        def _tm(t, tw=tw):
            for ic in range(NCH):
                inst = t.matmul(psA[:, 0:tw], W1B[:, ic * 160:ic * 160 + 128],
                                XXX[:, ic * tw:(ic + 1) * tw],
                                start=(ic == 0), stop=(ic == NCH - 1))
            b.inc(inst, "T", 1)
            for ic in range(NCH):
                inst = t.matmul(psB[0:32, 0:tw], W1B[:, ic * 160 + 128:ic * 160 + 160],
                                XXX[:, ic * tw:(ic + 1) * tw],
                                start=(ic == 0), stop=(ic == NCH - 1))
            b.inc(inst, "T", 1)
        b.on("tensor", _tm)
        tm = b.cnt["T"]

        def _sm(e, tw=tw, tm=tm):
            e.wait_ge(b.sems["T"], tm)
            b.inc(e.activation(M5[0:32, 0 * TW:0 * TW + tw], psA[0:32, 0:tw], AF.Tanh), "S", 1)
            b.inc(e.activation(M5[0:32, 1 * TW:1 * TW + tw], psA[32:64, 0:tw], AF.Tanh), "S", 1)
            b.inc(e.activation(MBT[0:64, 0:tw], psA[64:128, 0:tw], AF.Tanh), "S", 1)
            b.inc(e.activation(M5[0:32, 4 * TW:4 * TW + tw], psB[0:32, 0:tw], AF.Tanh), "S", 1)
        b.on("scalar", _sm, bar=False)
        sm_ = b.cnt["S"]

        def _vmx(v, tw=tw, sm_=sm_):
            v.wait_ge(b.sems["S"], sm_)
            b.inc(v.tensor_copy(M5[0:32, 2 * TW:2 * TW + tw], MBT[0:32, 0:tw]), "V", 1)
            b.inc(v.tensor_copy(M5[0:32, 3 * TW:3 * TW + tw], MBT[32:64, 0:tw]), "V", 1)
        b.on("vector", _vmx, bar=False)

        maas = [MAAW, MAAK, MAAV, MAAR, MAAG]
        for f in range(5):
            t1 = b.cnt["T"]
            v1 = b.cnt["V"]

            def _tf(t, f=f, tw=tw, v1=v1):
                for oc in range(NCH):
                    bank = (psC, psD)[oc % 2]
                    if oc >= 2:
                        t.wait_ge(b.sems["V"], v1 + 3 * (oc - 1))
                    b.inc(t.matmul(bank[:, 0:tw],
                                   W2B[0:32, f * D + oc * 128: f * D + (oc + 1) * 128],
                                   M5[0:32, f * TW:f * TW + tw],
                                   start=True, stop=True), "T", 1)
            b.on("tensor", _tf)

            def _vf(v, f=f, tw=tw, t1=t1):
                for oc in range(NCH):
                    v.wait_ge(b.sems["T"], t1 + oc + 1)
                    bank = (psC, psD)[oc % 2][:, 0:tw]
                    t_ = TT_[:, 0:tw]
                    b.inc(v.tensor_scalar(t_, bank, maas[f](oc), None, ALU.add),
                          "V", 1)
                    b.inc(v.tensor_tensor(t_, t_, DX[:, oc * tw:(oc + 1) * tw],
                                          ALU.mult), "V", 1)
                    b.inc(v.tensor_tensor(
                        X5[:, (f * NCH + oc) * tw:(f * NCH + oc) * tw + tw],
                        t_, XT[:, oc * (TW + 2) + 1:oc * (TW + 2) + 1 + tw],
                        ALU.add), "V", 1)
            b.on("vector", _vf, bar=False)

        for (name, f, wslot, oap, act) in (
                ("r", 3, 0, o_r, "scale8"), ("k", 1, 1, o_k, None),
                ("v", 2, 2, o_v, None), ("g", 4, 3, o_g, "silu")):
            def xf(ic, f=f, tw=tw):
                return X5[:, (f * NCH + ic) * tw:(f * NCH + ic) * tw + tw]

            if act == "silu":
                def ev(oc, ps_ap, e, tw=tw):
                    return e.activation(GT[:, oc * tw:(oc + 1) * tw], ps_ap, AF.Silu)
            elif act == "scale8":
                def ev(oc, ps_ap, e, tw=tw):
                    return e.activation(OUTT[:, oc * tw:(oc + 1) * tw], ps_ap,
                                        AF.Copy, scale=0.125)
            else:
                def ev(oc, ps_ap, e, tw=tw):
                    return e.copy(OUTT[:, oc * tw:(oc + 1) * tw], ps_ap)
            proj(b, WPROJ[:, wslot * NCH * D:(wslot + 1) * NCH * D] if False else WPROJ,
                 0, xf, list(range(NCH)), 0, 0, (psA, psB), "scalar", ev) if False                     else None
            t2 = b.cnt["T"]
            s2 = b.cnt["S"]
            gd2 = b.cnt["GD"]

            def _tp(t, f=f, wslot=wslot, tw=tw, s2=s2):
                for oc in range(NCH):
                    bank = (psA, psB)[oc % 2]
                    if oc >= 2:
                        t.wait_ge(b.sems["S"], s2 + (oc - 1))
                    for ic in range(NCH):
                        inst = t.matmul(
                            bank[:, 0:tw],
                            WPROJ[:, (wslot * NCH + ic) * D + oc * 128:
                                  (wslot * NCH + ic) * D + (oc + 1) * 128],
                            X5[:, (f * NCH + ic) * tw:(f * NCH + ic) * tw + tw],
                            start=(ic == 0), stop=(ic == NCH - 1))
                    b.inc(inst, "T", 1)
            b.on("tensor", _tp)

            def _sp(e, ev=ev, tw=tw, t2=t2, gd2=gd2):
                if gd2 > 0:
                    e.wait_ge(b.sems["GD"], gd2)
                for oc in range(NCH):
                    e.wait_ge(b.sems["T"], t2 + oc + 1)
                    b.inc(ev(oc, (psA, psB)[oc % 2][:, 0:tw], e), "S", 1)
            b.on("scalar", _sp, bar=False)
            if name == "g":
                b.dma("gpsimd", [(o_g[ic * 128:(ic + 1) * 128, toff:toff + tw],
                                  GT[:, ic * tw:(ic + 1) * tw]) for ic in range(NCH)],
                      bar=True)
            else:
                b.dma("gpsimd", [(oap[ic * 128:(ic + 1) * 128, toff:toff + tw],
                                  OUTT[:, ic * tw:(ic + 1) * tw]) for ic in range(NCH)],
                      bar=True)

        t3 = b.cnt["T"]

        def _tw(t, tw=tw):
            for ic in range(NCH):
                inst = t.matmul(psC[0:64, 0:tw], TD1[:, ic * 64:(ic + 1) * 64],
                                X5[:, (0 * NCH + ic) * tw:(0 * NCH + ic) * tw + tw],
                                start=(ic == 0), stop=(ic == NCH - 1))
            b.inc(inst, "T", 1)
        b.on("tensor", _tw)
        tw3 = b.cnt["T"]

        def _sw(e, tw=tw, tw3=tw3):
            e.wait_ge(b.sems["T"], tw3)
            b.inc(e.activation(TWB[0:64, 0:tw], psC[0:64, 0:tw], AF.Tanh), "S", 1)
        b.on("scalar", _sw, bar=False)
        sw3 = b.cnt["S"]

        for oc in range(NCH):
            def _tw2(t, oc=oc, tw=tw, sw3=sw3):
                t.wait_ge(b.sems["S"], sw3)
                b.inc(t.matmul(psD[:, 0:tw], TD2[0:64, oc * 128:(oc + 1) * 128],
                               TWB[0:64, 0:tw], start=True, stop=True), "T", 1)
            b.on("tensor", _tw2)
            tc = b.cnt["T"]

            def _se(e, oc=oc, tw=tw, tc=tc):
                e.wait_ge(b.sems["T"], tc)
                b.inc(e.activation(EXPT[:, 0:tw], psD[:, 0:tw], AF.Exp,
                                   bias=TDC(oc)), "S", 1)
            b.on("scalar", _se, bar=False)
            se_ = b.cnt["S"]
            def _ve(v, oc=oc, tw=tw, se_=se_):
                v.wait_ge(b.sems["S"], se_)
                b.inc(v.tensor_copy(EXPT2[0:64, 0:tw], EXPT[64:128, 0:tw]), "V", 1)
            b.on("vector", _ve, bar=False)
            ve_ = b.cnt["V"]

            def _tm2(t, oc=oc, tw=tw, se_=se_, ve_=ve_):
                t.wait_ge(b.sems["S"], se_)
                t.wait_ge(b.sems["V"], ve_)
                b.inc(t.matmul(psC[0:1, 0:tw], ONES[0:64, 1:2], EXPT[0:64, 0:tw],
                               start=True, stop=True), "T", 1)
                b.inc(t.matmul(psB[0:1, 0:tw], ONES[0:64, 2:3], EXPT2[0:64, 0:tw],
                               start=True, stop=True), "T", 1)
            b.on("tensor", _tm2, bar=False)
            tm2 = b.cnt["T"]

            def _vm(v, oc=oc, tw=tw, tm2=tm2):
                v.wait_ge(b.sems["T"], tm2)
                b.inc(v.tensor_scalar(MWT[0:1, 0:tw], psC[0:1, 0:tw],
                                      HDBE[0:1, 2 * oc:2 * oc + 1], None, ALU.mult),
                      "V", 1)
                b.inc(v.tensor_scalar(MWT[0:1, TW:TW + tw], psB[0:1, 0:tw],
                                      HDBE[0:1, 2 * oc + 1:2 * oc + 2], None, ALU.mult),
                      "V", 1)
            b.on("vector", _vm, bar=True)
            b.dma("gpsimd", [(o_mw[2 * oc:2 * oc + 1, toff:toff + tw],
                              MWT[0:1, 0:tw]),
                             (o_mw[2 * oc + 1:2 * oc + 2, toff:toff + tw],
                              MWT[0:1, TW:TW + tw])], bar=True)
    finL = b.cnt["LD"]
    finG = b.cnt["GD"]

    def _finp(e, finL=finL, finG=finG):
        e.wait_ge(b.sems["LD"], finL)
        e.wait_ge(b.sems["GD"], finG)
    b.on("gpsimd", _finp, bar=False)


def emit_l1post(nc, b, ctx, d_y, d_ident):
    d_g = nc.dram_tensor("g", [D, RPC], F32, kind="ExternalInput").ap()
    d_x2 = nc.dram_tensor("x2", [D, RPC], F32, kind="ExternalInput").ap()
    d_wo = nc.dram_tensor("wo", [D, D], F32, kind="ExternalInput").ap()
    d_gnv = nc.dram_tensor("gnv", [128, 16], F32, kind="ExternalInput").ap()
    d_w1 = nc.dram_tensor("w1", [D, 4 * D], F32, kind="ExternalInput").ap()
    d_w2 = nc.dram_tensor("w2", [4 * D, D], F32, kind="ExternalInput").ap()
    d_fv = nc.dram_tensor("fv", [128, 56], F32, kind="ExternalInput").ap()
    x_out = nc.dram_tensor("x_out", [D, RPC], F32, kind="ExternalOutput").ap()

    NT = 8
    W = RPC // NT
    GN_EPS = 1e-5 * 64.0

    WO = b.take("WO", [128, NCH * D], BF16)
    W1B = b.take("W1B", [128, NCH * 4 * D], BF16)
    W2B = b.take("W2B", [128, 32 * D], BF16)
    Y = b.take("Y", [128, NCH * W], F32)
    G = b.take("G", [128, NCH * W], F32)
    X2 = b.take("X2", [128, NCH * W], F32)
    X3 = b.take("X3", [128, NCH * W], F32)
    YN = b.take("YN", [128, NCH * W], BF16)
    H1 = b.take("H1", [128, 32 * W], BF16)
    YL2 = b.take("YL2", [64, W], F32)
    SQU = b.take("SQU", [128, 2 * W], F32)
    MROW = b.take("MROW", [1, 256], F32)
    ONES = b.take("ONES", [128, 128], F32)
    VEC = b.take("VEC", [128, 16], F32)
    FV = b.take("FV", [128, 56], F32)
    ROWS_ = tuple(b.take(f"ROW{i}", [1, 512], F32) for i in range(2))
    SQ2 = b.take("SQ2", [128, 2 * W], F32)
    MUR = ROWS_[0]
    RSR = ROWS_[1]
    TRW = SQ2

    psA = ctx.enter_context(nc.psum_tensor("psA", [128, 512], F32))
    psB = ctx.enter_context(nc.psum_tensor("psB", [128, 512], F32))
    psC = ctx.enter_context(nc.psum_tensor("psC", [128, 512], F32))
    psD = ctx.enter_context(nc.psum_tensor("psD", [128, 512], F32))
    psSum = ctx.enter_context(nc.psum_tensor("psSum", [128, 512], F32))
    psSq = ctx.enter_context(nc.psum_tensor("psSq", [128, 512], F32))
    psMu = ctx.enter_context(nc.psum_tensor("psMu", [128, 512], F32))
    psRstd = ctx.enter_context(nc.psum_tensor("psRstd", [128, 512], F32))
    LNPS = {"sum": psSum, "sq": psSq, "mu": psMu, "rstd": psRstd}

    b.dma("sync", [(VEC[:], d_gnv[:]), (FV[:], d_fv[:])], bar=True)
    b.dma("gpsimd", [(WO[:, ic * D:(ic + 1) * D], d_wo[ic * 128:(ic + 1) * 128, :])
                     for ic in range(NCH)]
          + [(W1B[:, ic * 4 * D:(ic + 1) * 4 * D],
              d_w1[ic * 128:(ic + 1) * 128, :]) for ic in range(NCH)]
          + [(W2B[:, ic * D:(ic + 1) * D], d_w2[ic * 128:(ic + 1) * 128, :])
             for ic in range(32)], bar=True)

    def _v0(v):
        b.inc(v.memset(ONES[:], 1.0), "V", 1)
        b.inc(v.memset(MROW[0:1, 0:64], 1.0), "V", 1)
        b.inc(v.memset(MROW[0:1, 64:128], 0.0), "V", 1)
        b.inc(v.memset(MROW[0:1, 128:192], 0.0), "V", 1)
        b.inc(v.memset(MROW[0:1, 192:256], 1.0), "V", 1)
    b.on("vector", _v0)

    def GNG(j):
        return VEC[:, j:j + 1]

    def GNB(j):
        return VEC[:, 8 + j:8 + j + 1]

    def FB1(j):
        return FV[:, j:j + 1]

    def FB2(j):
        return FV[:, 32 + j:32 + j + 1]

    def FG(j):
        return FV[:, 40 + j:40 + j + 1]

    def FBt(j):
        return FV[:, 48 + j:48 + j + 1]

    eps_gn = b.ccol(GN_EPS)

    for ti in range(NT):
        toff = ti * W
        b.dma("sync", [(Y[:, ic * W:(ic + 1) * W],
                        d_y[ic * 128:(ic + 1) * 128, toff:toff + W])
                       for ic in range(NCH)]
              + [(G[:, ic * W:(ic + 1) * W],
                  d_g[ic * 128:(ic + 1) * 128, toff:toff + W]) for ic in range(NCH)]
              + [(X2[:, ic * W:(ic + 1) * W],
                  d_x2[ic * 128:(ic + 1) * 128, toff:toff + W]) for ic in range(NCH)],
              bar=True)

        for c in range(NCH):
            yc_ = Y[:, c * W:(c + 1) * W]

            def _vg(v, c=c, yc_=yc_):
                b.inc(v.tensor_copy(YL2[0:64, 0:W], yc_[64:128, 0:W]), "V", 1)
            b.on("vector", _vg)
            vg = b.cnt["V"]

            def _sg(e, c=c, yc_=yc_, vg=vg):
                e.wait_ge(b.sems["V"], vg)
                b.inc(e.activation(SQU[0:64, 0:W], yc_[0:64, 0:W], AF.Square), "S", 1)
                b.inc(e.activation(SQU[0:64, W:2 * W], YL2[0:64, 0:W], AF.Square),
                      "S", 1)
            b.on("scalar", _sg, bar=False)
            sg = b.cnt["S"]

            def _tg(t, c=c, yc_=yc_, sg=sg):
                t.wait_ge(b.sems["S"], sg)
                b.inc(t.matmul(psSum[0:1, 0:W], ONES[0:64, 0:1], yc_[0:64, 0:W],
                               start=True, stop=True), "T", 1)
                b.inc(t.matmul(psSum[0:1, W:2 * W], ONES[0:64, 0:1], YL2[0:64, 0:W],
                               start=True, stop=True), "T", 1)
                b.inc(t.matmul(psSq[0:1, 0:W], ONES[0:64, 0:1], SQU[0:64, 0:W],
                               start=True, stop=True), "T", 1)
                b.inc(t.matmul(psSq[0:1, W:2 * W], ONES[0:64, 0:1],
                               SQU[0:64, W:2 * W], start=True, stop=True), "T", 1)
            b.on("tensor", _tg, bar=False)
            tg = b.cnt["T"]

            def _vs(v, tg=tg):
                v.wait_ge(b.sems["T"], tg)
                b.inc(v.tensor_scalar(MUR[0:1, 0:2 * W], psSum[0:1, 0:2 * W],
                                      1.0 / 64, None, ALU.mult), "V", 1)
                b.inc(v.tensor_scalar(RSR[0:1, 0:2 * W], psSq[0:1, 0:2 * W],
                                      1.0 / 64, None, ALU.mult), "V", 1)
                b.inc(v.tensor_tensor(TRW[0:1, 0:2 * W], MUR[0:1, 0:2 * W],
                                      MUR[0:1, 0:2 * W], ALU.mult), "V", 1)
                b.inc(v.tensor_tensor(RSR[0:1, 0:2 * W], RSR[0:1, 0:2 * W],
                                      TRW[0:1, 0:2 * W], ALU.subtract), "V", 1)
            b.on("vector", _vs, bar=False)
            vs = b.cnt["V"]

            def _ss(e, vs=vs):
                e.wait_ge(b.sems["V"], vs)
                b.inc(e.activation(RSR[0:1, 0:2 * W], RSR[0:1, 0:2 * W], AF.Sqrt,
                                   bias=eps_gn[0:1, 0:1]), "S", 1)
            b.on("scalar", _ss, bar=False)
            ss = b.cnt["S"]

            def _vr(v, ss=ss):
                v.wait_ge(b.sems["S"], ss)
                b.inc(v.reciprocal(RSR[0:1, 0:2 * W], RSR[0:1, 0:2 * W]), "V", 1)
            b.on("vector", _vr, bar=False)
            vr = b.cnt["V"]

            def _tb(t, vr=vr):
                t.wait_ge(b.sems["V"], vr)
                t.matmul(psMu[:, 0:W], MROW[0:1, 0:128], MUR[0:1, 0:W],
                         start=True, stop=False)
                b.inc(t.matmul(psMu[:, 0:W], MROW[0:1, 128:256], MUR[0:1, W:2 * W],
                               start=False, stop=True), "T", 1)
                t.matmul(psRstd[:, 0:W], MROW[0:1, 0:128], RSR[0:1, 0:W],
                         start=True, stop=False)
                b.inc(t.matmul(psRstd[:, 0:W], MROW[0:1, 128:256], RSR[0:1, W:2 * W],
                               start=False, stop=True), "T", 1)
            b.on("tensor", _tb, bar=False)
            tbx = b.cnt["T"]

            def _vn(v, c=c, yc_=yc_, tbx=tbx):
                v.wait_ge(b.sems["T"], tbx)
                b.inc(v.tensor_tensor(yc_, yc_, psMu[:, 0:W], ALU.subtract), "V", 1)
                b.inc(v.tensor_tensor(yc_, yc_, psRstd[:, 0:W], ALU.mult), "V", 1)
                b.inc(v.tensor_scalar(yc_, yc_, GNG(c), GNB(c), ALU.mult, ALU.add),
                      "V", 1)
                b.inc(v.tensor_tensor(YN[:, c * W:(c + 1) * W], yc_,
                                      G[:, c * W:(c + 1) * W], ALU.mult), "V", 1)
            b.on("vector", _vn, bar=False)

        t0 = b.cnt["T"]
        v0 = b.cnt["V"]

        def _tw(t, v0=v0):
            for oc in range(NCH):
                bank = (psA, psB)[oc % 2]
                if oc >= 2:
                    t.wait_ge(b.sems["V"], v0 + (oc - 1))
                for ic in range(NCH):
                    inst = t.matmul(bank[:, 0:W],
                                    WO[:, ic * D + oc * 128: ic * D + (oc + 1) * 128],
                                    YN[:, ic * W:(ic + 1) * W],
                                    start=(ic == 0), stop=(ic == NCH - 1))
                b.inc(inst, "T", 1)
        b.on("tensor", _tw)

        def _vw(v, t0=t0):
            for oc in range(NCH):
                v.wait_ge(b.sems["T"], t0 + oc + 1)
                xap = X2[:, oc * W:(oc + 1) * W]
                b.inc(v.tensor_tensor(xap, (psA, psB)[oc % 2][:, 0:W], xap, ALU.add),
                      "V", 1)
        b.on("vector", _vw, bar=False)

        def x2fn(ic):
            return X2[:, ic * W:(ic + 1) * W]

        def to_x3(ic, v, normed):
            b.inc(v.tensor_copy(X3[:, ic * W:(ic + 1) * W], normed), "V", 1)
        layernorm(b, x2fn, W, LNPS, ROWS_, SQ2, ONES, to_x3)

        def _vb(v):
            for ic in range(NCH):
                b.inc(v.tensor_copy(YN[:, ic * W:(ic + 1) * W],
                                    X3[:, ic * W:(ic + 1) * W]), "V", 1)
        b.on("vector", _vb)

        t1 = b.cnt["T"]
        s1 = b.cnt["S"]

        def _tf(t, s1=s1):
            for oc in range(32):
                bank = (psA, psB)[oc % 2]
                if oc >= 2:
                    t.wait_ge(b.sems["S"], s1 + (oc - 1))
                for ic in range(NCH):
                    inst = t.matmul(bank[:, 0:W],
                                    W1B[:, ic * 4 * D + oc * 128:
                                        ic * 4 * D + (oc + 1) * 128],
                                    YN[:, ic * W:(ic + 1) * W],
                                    start=(ic == 0), stop=(ic == NCH - 1))
                b.inc(inst, "T", 1)
        b.on("tensor", _tf)

        def _sf(e, t1=t1):
            for oc in range(32):
                e.wait_ge(b.sems["T"], t1 + oc + 1)
                b.inc(e.activation(H1[:, oc * W:(oc + 1) * W],
                                   (psA, psB)[oc % 2][:, 0:W], AF.Relu,
                                   bias=FB1(oc)), "S", 1)
        b.on("scalar", _sf, bar=False)

        t2 = b.cnt["T"]
        v2 = b.cnt["V"]

        def _tf2(t, v2=v2):
            for oc in range(NCH):
                bank = (psC, psD)[oc % 2]
                if oc >= 2:
                    t.wait_ge(b.sems["V"], v2 + (oc - 1))
                for ic in range(32):
                    inst = t.matmul(bank[:, 0:W],
                                    W2B[:, ic * D + oc * 128: ic * D + (oc + 1) * 128],
                                    H1[:, ic * W:(ic + 1) * W],
                                    start=(ic == 0), stop=(ic == 31))
                b.inc(inst, "T", 1)
        b.on("tensor", _tf2)

        def _vf2(v, t2=t2):
            for oc in range(NCH):
                v.wait_ge(b.sems["T"], t2 + oc + 1)
                xap = X2[:, oc * W:(oc + 1) * W]
                b.inc(v.scalar_tensor_tensor(xap, (psC, psD)[oc % 2][:, 0:W],
                                             FB2(oc), X3[:, oc * W:(oc + 1) * W],
                                             ALU.add, ALU.add), "V", 1)
        b.on("vector", _vf2, bar=False)

        def x4fn(ic):
            return X2[:, ic * W:(ic + 1) * W]

        def to_v(ic, v, normed):
            b.inc(v.tensor_scalar(Y[:, ic * W:(ic + 1) * W], normed,
                                  FG(ic), FBt(ic), ALU.mult, ALU.add), "V", 1)
        layernorm(b, x4fn, W, LNPS, ROWS_, SQ2, ONES, to_v)

        def _vu(v):
            for ic in range(NCH):
                b.inc(v.tensor_tensor(Y[:, ic * W:(ic + 1) * W],
                                      Y[:, ic * W:(ic + 1) * W],
                                      X3[:, ic * W:(ic + 1) * W], ALU.add), "V", 1)
        b.on("vector", _vu)

        def yfn(ic):
            return Y[:, ic * W:(ic + 1) * W]

        def to_out(ic, v, normed):
            b.inc(v.tensor_copy(X3[:, ic * W:(ic + 1) * W], normed), "V", 1)
        layernorm(b, yfn, W, LNPS, ROWS_, SQ2, ONES, to_out)

        b.dma("gpsimd", [(x_out[ic * 128:(ic + 1) * 128, toff:toff + W],
                          X3[:, ic * W:(ic + 1) * W]) for ic in range(NCH)],
              bar=True)

    fin = b.cnt["GD"]

    def _fin(e, fin=fin):
        e.wait_ge(b.sems["GD"], fin)
    b.on("gpsimd", _fin, bar=False)


def _cm(v):
    return np.ascontiguousarray(np.asarray(v, np.float32).reshape(-1, 128).T)


def _pack_l0_vecs(inp):
    cols = []
    for i in range(L0L):
        abp = np.asarray(inp["L0_attn_attn_b"][i]).reshape(16, 3, 64).transpose(1, 0, 2).reshape(3072)
        cols += [_cm(inp["L0_attn_ln1_w"][i]), _cm(inp["L0_attn_ln1_b"][i]),
                 _cm(abp), _cm(inp["L0_attn_aproj_b"][i]),
                 _cm(inp["L0_attn_fc_b"][i]), _cm(inp["L0_attn_mproj_b"][i]),
                 _cm(inp["L0_attn_ln2_w"][i]), _cm(inp["L0_attn_ln2_b"][i])]
    return np.ascontiguousarray(np.concatenate(cols, axis=1).astype(np.float32))


_PROGS = {}


def run_l0(inp, x_fm_full):
    C = host_consts()
    vecs = _pack_l0_vecs(inp)
    aw_p = [np.ascontiguousarray(
        np.asarray(inp["L0_attn_attn_w"][i]).reshape(D, 16, 3, 64).transpose(0, 2, 1, 3)
        .reshape(D, 3 * D).astype(np.float32)) for i in range(L0L)]
    if "l0" not in _PROGS:
        _PROGS["l0"] = build_l0(WTOK)
    in_maps = []
    for c in range(NCORES):
        m = {"x_in": core_slice_fm(x_fm_full, c), "vecs": vecs,
             "Ar": C["Ar"], "Ai": C["Ai"], "ident": C["ident"],
             "filt_r": core_filt_slice(C["filt_r"], c),
             "filt_i": core_filt_slice(C["filt_i"], c),
             "ffn_W": np.asarray(inp["L0_ffn_W"], np.float32),
             "lora_down": np.asarray(inp["L0_ffn_lora_down"], np.float32),
             "lora_up": np.asarray(inp["L0_ffn_lora_up"], np.float32),
             "down_aux": np.asarray(inp["L0_ffn_down_aux"], np.float32),
             "up_aux": np.asarray(inp["L0_ffn_up_aux"], np.float32),
             "ffn_b": _cm(inp["L0_ffn_b"])}
        for i in range(L0L):
            m[f"aw{i}"] = aw_p[i]
            m[f"pw{i}"] = np.asarray(inp["L0_attn_aproj_w"][i], np.float32)
            m[f"fw{i}"] = np.asarray(inp["L0_attn_fc_w"][i], np.float32)
            m[f"mw{i}"] = np.asarray(inp["L0_attn_mproj_w"][i], np.float32)
        in_maps.append(m)
    res = run_bass_kernel_spmd(_PROGS["l0"], in_maps, core_ids=list(range(NCORES)))
    return [np.asarray(res.results[c]["x_out"]) for c in range(NCORES)]


def run_l1pre(inp, x2s):
    cols = [_cm(np.asarray(inp[f"L1_attn_time_maa_{n}"]).reshape(D))
            for n in ("x", "w", "k", "v", "r", "g")]
    cols.append(_cm(np.asarray(inp["L1_attn_time_decay"]).reshape(D)))
    cols.append(np.zeros((128, 8), np.float32))
    vecs = np.ascontiguousarray(np.concatenate(cols, axis=1))
    hdb = np.ascontiguousarray(np.asarray(inp["L1_attn_head_decay_bias"],
                                          np.float32).reshape(1, 16))
    if "l1pre" not in _PROGS:
        _PROGS["l1pre"] = build_l1pre()
    in_maps = []
    for c in range(NCORES):
        x2 = x2s[c]
        prev = (np.zeros((D, 1), np.float32) if c in (0, 4)
                else np.ascontiguousarray(x2[:, FH - 1:FH]))
        nxt = (np.zeros((D, 1), np.float32) if c in (3, 7)
               else np.ascontiguousarray(x2[:, FH + RPC:FH + RPC + 1]))
        x2p = np.ascontiguousarray(
            np.concatenate([prev, x2[:, FH:FH + RPC], nxt], axis=1))
        m = {"x2p": x2p, "vecs": vecs, "hdb": hdb,
             "W_r": np.asarray(inp["L1_attn_W_r"], np.float32),
             "W_k": np.asarray(inp["L1_attn_W_k"], np.float32),
             "W_v": np.asarray(inp["L1_attn_W_v"], np.float32),
             "W_g": np.asarray(inp["L1_attn_W_g"], np.float32),
             "maa_w1": np.asarray(inp["L1_attn_time_maa_w1"], np.float32),
             "maa_w2": np.ascontiguousarray(
                 np.asarray(inp["L1_attn_time_maa_w2"], np.float32).reshape(160, D)),
             "tdw1": np.asarray(inp["L1_attn_time_decay_w1"], np.float32),
             "tdw2": np.asarray(inp["L1_attn_time_decay_w2"], np.float32)}
        in_maps.append(m)
    res = run_bass_kernel_spmd(_PROGS["l1pre"], in_maps, core_ids=list(range(NCORES)))
    return [res.results[c] for c in range(NCORES)]


def run_l1attn(inp, pre):
    C = host_consts()
    slow_row = np.ascontiguousarray(
        np.asarray(inp["L1_attn_slow_scale"], np.float32).reshape(1, 16))
    alpha2 = np.ascontiguousarray(
        np.asarray(inp["L1_attn_decay_mix"], np.float32).reshape(16, 64).T)
    kb = {0: np.concatenate([np.asarray(pre[c]["k_bf"]) for c in range(4)], axis=1),
          1: np.concatenate([np.asarray(pre[c]["k_bf"]) for c in range(4, 8)], axis=1)}
    vb = {0: np.concatenate([np.asarray(pre[c]["v_bf"]) for c in range(4)], axis=1),
          1: np.concatenate([np.asarray(pre[c]["v_bf"]) for c in range(4, 8)], axis=1)}
    mwb = {0: np.concatenate([np.asarray(pre[c]["meanw"]) for c in range(4)], axis=1),
           1: np.concatenate([np.asarray(pre[c]["meanw"]) for c in range(4, 8)], axis=1)}
    ys = [None] * NCORES
    for g in range(4):
        soff = g * RPC
        key = f"l1attn{g}"
        if key not in _PROGS:
            _PROGS[key] = build_l1attn(soff)
        cids = [g, g + 4]
        in_maps = []
        for c in cids:
            bat = c // 4
            in_maps.append({"r_bf": np.asarray(pre[c]["r_bf"]), "kAll": kb[bat],
                            "vAll": vb[bat], "mwAll": mwb[bat],
                            "slow_row": slow_row, "alpha2": alpha2,
                            "ident": C["ident"]})
        res = run_bass_kernel_spmd(_PROGS[key], in_maps, core_ids=cids)
        for i, c in enumerate(cids):
            ys[c] = np.asarray(res.results[i]["y_attn"])
    return ys


def rot_order(c):
    q = c % 4
    base = 4 * (c // 4)
    return [base + (q + d) % 4 for d in range(4)]


def corr_row(c):
    q = c % 4

    def svec(o):
        d = (o - q) % 4
        v = np.zeros(4)
        v[d] += 1
        if d >= 1:
            v[d - 1] -= 1
        return v

    def base_true(o):
        t = np.zeros(4)
        for o2 in range(o):
            t += svec(o2)
        return t

    M = np.zeros((4, 4), np.float64)
    for d in range(1, 4):
        o = (q + d) % 4
        row = base_true(o) - base_true(q)
        row[d - 1] -= 1.0
        M[d] = row
    return np.ascontiguousarray(M.reshape(1, 16).astype(np.float32))


def slob_row(c):
    q = c % 4
    r = np.zeros((1, 8), np.float32)
    for d in range(1, 4):
        o = (q + d) % 4
        if o < q:
            r[0, d] = 1.0
        else:
            r[0, 4 + d] = 1.0
    return r


def run_l1attn_rot(inp, pre):
    C = host_consts()
    slow_row = np.ascontiguousarray(
        np.asarray(inp["L1_attn_slow_scale"], np.float32).reshape(1, 16))
    alpha2 = np.ascontiguousarray(
        np.asarray(inp["L1_attn_decay_mix"], np.float32).reshape(16, 64).T)
    if "l1attn_rot" not in _PROGS:
        _PROGS["l1attn_rot"] = build_l1attn_rot()
    in_maps = []
    for c in range(NCORES):
        order = rot_order(c)
        in_maps.append({
            "r_bf": np.asarray(pre[c]["r_bf"]),
            "kAll": np.ascontiguousarray(np.concatenate(
                [np.asarray(pre[o]["k_bf"]) for o in order], axis=1)),
            "vAll": np.ascontiguousarray(np.concatenate(
                [np.asarray(pre[o]["v_bf"]) for o in order], axis=1)),
            "mwAll": np.ascontiguousarray(np.concatenate(
                [np.asarray(pre[o]["meanw"]) for o in order], axis=1)),
            "slow_row": slow_row, "alpha2": alpha2, "ident": C["ident"],
            "corr4": corr_row(c), "slob": slob_row(c)})
    res = run_bass_kernel_spmd(_PROGS["l1attn_rot"], in_maps,
                               core_ids=list(range(NCORES)))
    return [np.asarray(res.results[c]["y_attn"]) for c in range(NCORES)]


def run_l1post(inp, x2s, pre, ys):
    C = host_consts()
    gnv = np.ascontiguousarray(np.concatenate(
        [_cm(inp["L1_attn_ln_gamma"]), _cm(inp["L1_attn_ln_beta"])], axis=1))
    fv = np.ascontiguousarray(np.concatenate(
        [_cm(np.asarray(inp["L1_ffn_b1"][0])), _cm(np.asarray(inp["L1_ffn_b2"][0])),
         _cm(inp["L1_ffn_gamma"]), _cm(inp["L1_ffn_beta"])], axis=1))
    if "l1post" not in _PROGS:
        _PROGS["l1post"] = build_l1post()
    in_maps = []
    for c in range(NCORES):
        in_maps.append({
            "y_attn": ys[c], "g": np.asarray(pre[c]["g"]),
            "x2": np.ascontiguousarray(x2s[c][:, FH:FH + RPC]),
            "wo": np.asarray(inp["L1_attn_W_o"], np.float32),
            "gnv": gnv, "fv": fv,
            "w1": np.asarray(inp["L1_ffn_w1"][0], np.float32),
            "w2": np.asarray(inp["L1_ffn_w2"][0], np.float32),
            "ident": C["ident"]})
    res = run_bass_kernel_spmd(_PROGS["l1post"], in_maps, core_ids=list(range(NCORES)))
    return [np.asarray(res.results[c]["x_out"]) for c in range(NCORES)]


def run_p1(inp, x_fm_full):
    C = host_consts()
    vecs = _pack_l0_vecs(inp)
    aw_p = [np.ascontiguousarray(
        np.asarray(inp["L0_attn_attn_w"][i]).reshape(D, 16, 3, 64).transpose(0, 2, 1, 3)
        .reshape(D, 3 * D).astype(np.float32)) for i in range(L0L)]
    pcols = [_cm(np.asarray(inp[f"L1_attn_time_maa_{n}"]).reshape(D))
             for n in ("x", "w", "k", "v", "r", "g")]
    pcols.append(_cm(np.asarray(inp["L1_attn_time_decay"]).reshape(D)))
    pcols.append(np.zeros((128, 8), np.float32))
    pvecs = np.ascontiguousarray(np.concatenate(pcols, axis=1))
    hdb = np.ascontiguousarray(np.asarray(inp["L1_attn_head_decay_bias"],
                                          np.float32).reshape(1, 16))
    if "p1" not in _PROGS:
        _PROGS["p1"] = build_l0(WTOK, with_l1pre=True)
    in_maps = []
    for c in range(NCORES):
        em = np.ones((128, 2), np.float32)
        if c in (0, 4):
            em[:, 0] = 0.0
        if c in (3, 7):
            em[:, 1] = 0.0
        m = {"x_in": core_slice_fm(x_fm_full, c), "vecs": vecs,
             "Ar": C["Ar"], "Ai": C["Ai"], "ident": C["ident"],
             "filt_r": core_filt_slice(C["filt_r"], c),
             "filt_i": core_filt_slice(C["filt_i"], c),
             "ffn_W": np.asarray(inp["L0_ffn_W"], np.float32),
             "lora_down": np.asarray(inp["L0_ffn_lora_down"], np.float32),
             "lora_up": np.asarray(inp["L0_ffn_lora_up"], np.float32),
             "down_aux": np.asarray(inp["L0_ffn_down_aux"], np.float32),
             "up_aux": np.asarray(inp["L0_ffn_up_aux"], np.float32),
             "ffn_b": _cm(inp["L0_ffn_b"]),
             "pvecs": pvecs, "hdb": hdb, "em": em,
             "W_r": np.asarray(inp["L1_attn_W_r"], np.float32),
             "W_k": np.asarray(inp["L1_attn_W_k"], np.float32),
             "W_v": np.asarray(inp["L1_attn_W_v"], np.float32),
             "W_g": np.asarray(inp["L1_attn_W_g"], np.float32),
             "maa_w1": np.asarray(inp["L1_attn_time_maa_w1"], np.float32),
             "maa_w2": np.ascontiguousarray(
                 np.asarray(inp["L1_attn_time_maa_w2"], np.float32).reshape(160, D)),
             "tdw1": np.asarray(inp["L1_attn_time_decay_w1"], np.float32),
             "tdw2": np.asarray(inp["L1_attn_time_decay_w2"], np.float32)}
        for i in range(L0L):
            m[f"aw{i}"] = aw_p[i]
            m[f"pw{i}"] = np.asarray(inp["L0_attn_aproj_w"][i], np.float32)
            m[f"fw{i}"] = np.asarray(inp["L0_attn_fc_w"][i], np.float32)
            m[f"mw{i}"] = np.asarray(inp["L0_attn_mproj_w"][i], np.float32)
        in_maps.append(m)
    res = run_bass_kernel_spmd(_PROGS["p1"], in_maps, core_ids=list(range(NCORES)))
    return [res.results[c] for c in range(NCORES)]


def run_p2(inp, p1):
    C = host_consts()
    slow_row = np.ascontiguousarray(
        np.asarray(inp["L1_attn_slow_scale"], np.float32).reshape(1, 16))
    alpha2 = np.ascontiguousarray(
        np.asarray(inp["L1_attn_decay_mix"], np.float32).reshape(16, 64).T)
    gnv = np.ascontiguousarray(np.concatenate(
        [_cm(inp["L1_attn_ln_gamma"]), _cm(inp["L1_attn_ln_beta"])], axis=1))
    fv = np.ascontiguousarray(np.concatenate(
        [_cm(np.asarray(inp["L1_ffn_b1"][0])), _cm(np.asarray(inp["L1_ffn_b2"][0])),
         _cm(inp["L1_ffn_gamma"]), _cm(inp["L1_ffn_beta"])], axis=1))
    if "p2" not in _PROGS:
        _PROGS["p2"] = build_l1attn_rot(post=True)
    in_maps = []
    for c in range(NCORES):
        order = rot_order(c)
        x2 = np.asarray(p1[c]["x_out"])
        in_maps.append({
            "r_bf": np.asarray(p1[c]["r_bf"]),
            "kAll": np.ascontiguousarray(np.concatenate(
                [np.asarray(p1[o]["k_bf"]) for o in order], axis=1)),
            "vAll": np.ascontiguousarray(np.concatenate(
                [np.asarray(p1[o]["v_bf"]) for o in order], axis=1)),
            "mwAll": np.ascontiguousarray(np.concatenate(
                [np.asarray(p1[o]["meanw"]) for o in order], axis=1)),
            "slow_row": slow_row, "alpha2": alpha2, "ident": C["ident"],
            "corr4": corr_row(c), "slob": slob_row(c),
            "g": np.asarray(p1[c]["g"]),
            "x2": np.ascontiguousarray(x2[:, FH:FH + RPC]),
            "wo": np.asarray(inp["L1_attn_W_o"], np.float32),
            "gnv": gnv, "fv": fv,
            "w1": np.asarray(inp["L1_ffn_w1"][0], np.float32),
            "w2": np.asarray(inp["L1_ffn_w2"][0], np.float32)})
    res = run_bass_kernel_spmd(_PROGS["p2"], in_maps, core_ids=list(range(NCORES)))
    return [np.asarray(res.results[c]["x_out"]) for c in range(NCORES)]


def kernel(**inputs):
    x0 = np.asarray(inputs["L0_attn_x"], np.float32)
    x_fm = np.ascontiguousarray(x0.reshape(16384, D).T)
    p1 = run_p1(inputs, x_fm)
    outs = run_p2(inputs, p1)
    full = np.concatenate([o.T for o in outs], axis=0)
    return np.ascontiguousarray(full.reshape(2, SEQ, D))


def build_l1attn_rot(dbg=False, post=False):
    nc = bass.Bass("TRN2", target_bir_lowering=False, debug=False)
    if dbg:
        d_dbgA = nc.dram_tensor("dbgA", [128, 1024], F32, kind="ExternalOutput").ap()
        d_dbgB = nc.dram_tensor("dbgB", [64, 4096], F32, kind="ExternalOutput").ap()
        d_dbgC = nc.dram_tensor("dbgC", [1, SEQ], F32, kind="ExternalOutput").ap()
    d_r = nc.dram_tensor("r_bf", [D, RPC], BF16, kind="ExternalInput").ap()
    d_k = nc.dram_tensor("kAll", [D, SEQ], BF16, kind="ExternalInput").ap()
    d_v = nc.dram_tensor("vAll", [D, SEQ], BF16, kind="ExternalInput").ap()
    d_mw = nc.dram_tensor("mwAll", [16, SEQ], F32, kind="ExternalInput").ap()
    d_slow = nc.dram_tensor("slow_row", [1, 16], F32, kind="ExternalInput").ap()
    d_alpha = nc.dram_tensor("alpha2", [64, 16], F32, kind="ExternalInput").ap()
    d_ident = nc.dram_tensor("ident", [128, 128], F32, kind="ExternalInput").ap()
    d_corr = nc.dram_tensor("corr4", [1, 16], F32, kind="ExternalInput").ap()
    d_slob = nc.dram_tensor("slob", [1, 8], F32, kind="ExternalInput").ap()
    o_y = nc.dram_tensor("y_attn", [D, RPC], F32,
                         kind=("Internal" if post else "ExternalOutput")).ap()

    NSTRIP, SW, NJ = 4, 512, SEQ // 128

    ctx = ExitStack()
    with ctx:
        b = Bld(nc, ctx)
        markA = b.off
        psctx = ExitStack()
        MWR = b.take("MWR", [1, SEQ], F32)
        CH = b.take("CH", [1, SEQ], F32)
        ZROW = b.take("ZROW", [1, SEQ], F32)
        CROWS = b.take("CROWS", [1, 512], F32)
        CMAT = b.take("CMAT", [128, 64], F32)
        CMATS = b.take("CMATS", [128, 64], F32)
        CT2D = b.take("CT2D", [64, 128], F32)
        KH = b.take("KH", [64, SEQ], BF16)
        VH = b.take("VH", [64, SEQ], BF16)
        VT = b.take("VT", [128, 64 * 64], BF16)
        RH = b.take("RH", [64, RPC], BF16)
        IDF = b.take("IDF", [128, 128], F32)
        IDB = b.take("IDB", [128, 128], BF16)
        ONES = b.take("ONES", [128, 128], F32)
        SIGR = b.take("SIGR", [1, 16], F32)
        SIGB = b.take("SIGB", [128, 16], F32)
        ALPH = b.take("ALPH", [64, 16], F32)
        SCL = b.take("SCL", [128, 4 * 64], F32)
        CQB = b.take("CQB", [128, 2], F32)
        RVB = b.take("RVB", [64, 4 * SW], F32)
        RROW4 = b.take("RROW4", [1, 4 * SW], F32)
        DROW = b.take("DROW", [1, 2 * SW], F32)
        CTBF = b.take("CTBF", [128, SW], F32)
        CTBS = b.take("CTBS", [128, SW], F32)
        PTILE = b.take("PTILE", [128, 4 * SW], BF16)
        DBGPT = b.take("DBGPT", [128, 2 * SW], F32)
        MTILE = b.take("MTILE", [128, 2 * 64], F32)
        MKT = b.take("MKT", [128, 2 * SW], F32)
        T64 = b.take("T64", [64, 2 * SW], F32)
        YT = b.take("YT", [64, SW], F32)
        YACC = b.take("YACC", [64, 4 * SW], F32)
        CORR4 = b.take("CORR4", [1, 16], F32)
        SLOBR = b.take("SLOBR", [1, 8], F32)
        SLOBB = b.take("SLOBB", [64, 8], F32)

        psS1 = psctx.enter_context(nc.psum_tensor("psS1", [128, 512], F32))
        psS2 = psctx.enter_context(nc.psum_tensor("psS2", [128, 512], F32))
        psYA = psctx.enter_context(nc.psum_tensor("psYA", [128, 512], F32))
        psYB = psctx.enter_context(nc.psum_tensor("psYB", [128, 512], F32))
        psYC = psctx.enter_context(nc.psum_tensor("psYC", [128, 512], F32))
        psYD = psctx.enter_context(nc.psum_tensor("psYD", [128, 512], F32))
        psG1 = psctx.enter_context(nc.psum_tensor("psG1", [128, 512], F32))
        psTB = psctx.enter_context(nc.psum_tensor("psTB", [128, 512], BF16))

        b.dma("sync", [(IDF[:], d_ident[:]), (SIGR[:], d_slow[:]),
                       (ALPH[0:64, :], d_alpha[:]), (CORR4[:], d_corr[:]),
                       (SLOBR[:], d_slob[:])])

        def _v0(v):
            b.inc(v.memset(ONES[:], 1.0), "V", 1)
            b.inc(v.memset(ZROW[:], 0.0), "V", 1)
            b.inc(v.tensor_copy(IDB[:], IDF[:]), "V", 1)
        b.on("vector", _v0)

        def _s0(e):
            b.inc(e.activation(SIGR[:], SIGR[:], AF.Sigmoid), "S", 1)
            b.inc(e.activation(ALPH[0:64, :], ALPH[0:64, :], AF.Sigmoid), "S", 1)
        b.on("scalar", _s0)

        def _t0(t):
            b.inc(t.matmul(psG1[:, 0:16], ONES[0:1, 0:128], SIGR[:],
                           start=True, stop=True), "T", 1)
            b.inc(t.matmul(psG1[:, 16:24], ONES[0:1, 0:128], SLOBR[:],
                           start=True, stop=True), "T", 1)
            for ps in (psYA, psYB, psYC, psYD):
                b.inc(t.matmul(ps[0:64, 0:512], ONES[0:1, 0:64], ZROW[0:1, 0:512],
                               start=True, stop=True), "T", 1)
        b.on("tensor", _t0)
        tb0 = b.cnt["T"]

        def _v1(v, tb0=tb0):
            v.wait_ge(b.sems["T"], tb0)
            b.inc(v.tensor_copy(SIGB[:], psG1[:, 0:16]), "V", 1)
            b.inc(v.tensor_copy(SLOBB[0:64, :], psG1[0:64, 16:24]), "V", 1)
        b.on("vector", _v1, bar=False)

        for h in range(16):
            vnow = b.cnt["V"]
            tnow = b.cnt["T"]

            def _ldm(e, vnow=vnow, h=h):
                if vnow:
                    e.wait_ge(b.sems["V"], vnow)
                b.inc(e.dma_start(out=MWR[:], in_=d_mw[h:h + 1, :]), "LD", 16)
            b.on("sync", _ldm, bar=False)

            def _ldk(e, tnow=tnow, h=h):
                if tnow:
                    e.wait_ge(b.sems["T"], tnow)
                b.inc(e.dma_start(out=KH[0:64, :],
                                  in_=d_k[h * 64:(h + 1) * 64, :]), "GD", 16)
                b.inc(e.dma_start(out=VH[0:64, :],
                                  in_=d_v[h * 64:(h + 1) * 64, :]), "GD", 16)
                b.inc(e.dma_start(out=RH[0:64, :],
                                  in_=d_r[h * 64:(h + 1) * 64, :]), "GD", 16)
            b.on("gpsimd", _ldk, bar=False)

            ldnow = b.cnt["LD"]
            snow = b.cnt["S"]

            def _vc(v, h=h, ldnow=ldnow, tnow=tnow, snow=snow):
                v.wait_ge(b.sems["LD"], ldnow)
                if tnow:
                    v.wait_ge(b.sems["T"], tnow)
                if snow:
                    v.wait_ge(b.sems["S"], snow)
                b.inc(v.tensor_tensor_scan(CH[:], MWR[:], ZROW[:], 0.0,
                                           ALU.add, ALU.add), "V", 1)
                v.wait_ge(b.sems["V"], b.cnt["V"])
                for d in range(1, 4):
                    for dp in range(4):
                        e_ = CH[0:1, dp * 2048 + 2047:dp * 2048 + 2048]
                        c_ = CORR4[0:1, d * 4 + dp:d * 4 + dp + 1]
                        t_ = DROW[0:1, d:d + 1]
                        if dp == 0:
                            b.inc(v.tensor_scalar(t_, e_, c_, None, ALU.mult),
                                  "V", 1)
                        else:
                            b.inc(v.scalar_tensor_tensor(t_, e_, c_, t_,
                                                         ALU.mult, ALU.add),
                                  "V", 1)
                        v.wait_ge(b.sems["V"], b.cnt["V"])
                for d in range(1, 4):
                    b.inc(v.tensor_scalar(CH[0:1, d * 2048:(d + 1) * 2048],
                                          CH[0:1, d * 2048:(d + 1) * 2048],
                                          DROW[0:1, d:d + 1], None, ALU.add),
                          "V", 1)
                v.wait_ge(b.sems["V"], b.cnt["V"])
            b.on("vector", _vc, bar=False)

            vscan = b.cnt["V"]

            def _ldc(e, vscan=vscan):
                e.wait_ge(b.sems["V"], vscan)
                b.inc(e.dma_start(out=CT2D[0:64, :], in_=CH[0:1, :]), "LD", 16)
            b.on("sync", _ldc, bar=False)

            def _tc(t):
                b.inc(t.transpose(psG1[:, 0:64], CT2D[0:64, 0:128],
                                  IDF[0:64, 0:64]), "T", 1)
            b.on("tensor", _tc)
            tc = b.cnt["T"]

            def _vc2(v, h=h, tc=tc):
                v.wait_ge(b.sems["T"], tc)
                b.inc(v.tensor_copy(CMAT[:, 0:64], psG1[:, 0:64]), "V", 1)
                v.wait_ge(b.sems["V"], b.cnt["V"])
                b.inc(v.tensor_scalar(CMATS[:, 0:64], CMAT[:, 0:64],
                                      SIGB[:, h:h + 1], None, ALU.mult), "V", 1)
                v.wait_ge(b.sems["V"], b.cnt["V"])
            b.on("vector", _vc2, bar=False)

            for g in range(8):
                def _tv(t, g=g):
                    for jj in range(8):
                        j = g * 8 + jj
                        b.inc(t.transpose(psTB[:, jj * 64:(jj + 1) * 64],
                                          VH[0:64, j * 128:(j + 1) * 128],
                                          IDB[0:64, 0:64]), "T", 1)
                b.on("tensor", _tv)
                tv = b.cnt["T"]

                def _vv(v, g=g, tv=tv):
                    v.wait_ge(b.sems["T"], tv)
                    b.inc(v.tensor_copy(VT[:, g * 512:(g + 1) * 512], psTB[:, 0:512]),
                          "V", 1)
                b.on("vector", _vv, bar=False)

            for si in range(NSTRIP):
                q0 = si * SW
                qm = q0 + SW // 2
                jd = q0 // 128

                def _tq(t, qm=qm):
                    b.inc(t.matmul(psG1[:, 0:1], ONES[0:1, 0:128], CH[0:1, qm:qm + 1],
                                   start=True, stop=True), "T", 1)
                b.on("tensor", _tq)
                tq = b.cnt["T"]

                def _vq(v, h=h, tq=tq, q0=q0, qm=qm):
                    v.wait_ge(b.sems["T"], tq)
                    b.inc(v.tensor_copy(CQB[:, 0:1], psG1[:, 0:1]), "V", 1)
                    v.wait_ge(b.sems["V"], b.cnt["V"])
                    b.inc(v.tensor_scalar(CQB[:, 1:2], CQB[:, 0:1], SIGB[:, h:h + 1],
                                          None, ALU.mult), "V", 1)
                    b.inc(v.tensor_scalar(CROWS[0:1, 0:SW], CH[0:1, q0:q0 + SW],
                                          SIGR[0:1, h:h + 1], None, ALU.mult), "V", 1)
                    for mi, cm in ((0, CMAT), (1, CMATS)):
                        t_ = MTILE[:, mi * 64:mi * 64 + 64]
                        b.inc(v.tensor_scalar(t_, cm[:, 0:64], CQB[:, mi:mi + 1],
                                              None, ALU.subtract), "V", 1)
                        v.wait_ge(b.sems["V"], b.cnt["V"])
                        b.inc(v.tensor_scalar(t_, t_, 80.0, -80.0, ALU.min, ALU.max),
                              "V", 1)
                        v.wait_ge(b.sems["V"], b.cnt["V"])
                    d0 = DROW[0:1, 0:SW]
                    b.inc(v.tensor_scalar(d0, CH[0:1, q0:q0 + SW],
                                          CH[0:1, qm:qm + 1], None, ALU.subtract),
                          "V", 1)
                    v.wait_ge(b.sems["V"], b.cnt["V"])
                    d1 = DROW[0:1, SW:2 * SW]
                    b.inc(v.tensor_scalar(d1, d0, SIGR[0:1, h:h + 1], None, ALU.mult),
                          "V", 1)
                    v.wait_ge(b.sems["V"], b.cnt["V"])
                    b.inc(v.tensor_scalar(d0, d0, 80.0, -80.0, ALU.min, ALU.max), "V", 1)
                    b.inc(v.tensor_scalar(d1, d1, 80.0, -80.0, ALU.min, ALU.max), "V", 1)
                    b.inc(v.memset(YACC[0:64, 0:4 * SW], 0.0), "V", 1)
                    v.wait_ge(b.sems["V"], b.cnt["V"])
                b.on("vector", _vq, bar=True)
                vq = b.cnt["V"]

                def _sq(e, vq=vq):
                    e.wait_ge(b.sems["V"], vq)
                    for mi in range(2):
                        src = MTILE[:, mi * 64:mi * 64 + 64]
                        b.inc(e.activation(src, src, AF.Abs), "S", 1)
                        b.inc(e.activation(SCL[:, (2 * mi) * 64:(2 * mi) * 64 + 64],
                                           src, AF.Exp, scale=-1.0), "S", 1)
                        b.inc(e.activation(SCL[:, (2 * mi + 1) * 64:(2 * mi + 1) * 64 + 64],
                                           src, AF.Exp, scale=-1.0), "S", 1)
                        d_ = DROW[0:1, mi * SW:(mi + 1) * SW]
                        b.inc(e.activation(RROW4[0:1, (2 * mi) * SW:(2 * mi) * SW + SW],
                                           d_, AF.Exp), "S", 1)
                        b.inc(e.activation(
                            RROW4[0:1, (2 * mi + 1) * SW:(2 * mi + 1) * SW + SW],
                            d_, AF.Exp, scale=-1.0), "S", 1)
                b.on("scalar", _sq, bar=False)
                sq_ = b.cnt["S"]

                for bi in range(6):
                    def _tbx(t, bi=bi, q0=q0, sq_=sq_):
                        if bi == 0:
                            t.wait_ge(b.sems["S"], sq_)
                        if bi < 4:
                            b.inc(t.matmul(psG1[0:64, 0:SW], ONES[0:1, 0:64],
                                           RROW4[0:1, bi * SW:(bi + 1) * SW],
                                           start=True, stop=True), "T", 1)
                        else:
                            rhs = (CH[0:1, q0:q0 + SW] if bi == 4
                                   else CROWS[0:1, 0:SW])
                            b.inc(t.matmul(psG1[:, 0:SW], ONES[0:1, 0:128], rhs,
                                           start=True, stop=True), "T", 1)
                    b.on("tensor", _tbx)
                    tbx = b.cnt["T"]

                    def _vbx(v, bi=bi, tbx=tbx):
                        v.wait_ge(b.sems["T"], tbx)
                        if bi < 4:
                            b.inc(v.tensor_copy(RVB[0:64, bi * SW:(bi + 1) * SW],
                                                psG1[0:64, 0:SW]), "V", 1)
                        else:
                            b.inc(v.tensor_copy((CTBF, CTBS)[bi - 4][:, 0:SW],
                                                psG1[:, 0:SW]), "V", 1)
                    b.on("vector", _vbx, bar=False)

                t0c = b.cnt["T"]
                v0c = b.cnt["V"]
                s0c = b.cnt["S"]
                diag = lambda j: jd <= j < jd + 4
                bdiag = jd // 4
                NB = NJ // 4
                t_smm, t_av, v_evict, v_badd = {}, {}, {}, {}
                v_diag1, v_diag15, s_diag1, s_diag2 = {}, {}, {}, {}
                s_evict = {}
                vc, tc_, sc = v0c, t0c, s0c
                for j in range(NJ):
                    if diag(j):
                        vc += 2
                        v_diag1[j] = vc
                        vc += 2
                        v_diag15[j] = vc
                        vc += 2
                        sc += 2
                        s_diag1[j] = sc
                        sc += 2
                        s_diag2[j] = sc
                        s_evict[j] = None
                    elif j % 2 == 0:
                        vc += 2
                        s_evict[j] = None
                    else:
                        sc += 2
                        s_evict[j] = sc
                    v_evict[j] = vc
                    if j % 4 == 3:
                        vc += 2 if j // 4 < 4 else 4
                        v_badd[j // 4] = vc
                tc_ += 1
                t_smm[0] = tc_
                tc_ += 1
                t_smm[1] = tc_
                for j in range(2, NJ):
                    tc_ += 1
                    t_smm[j] = tc_
                    tc_ += 1
                    t_av[j - 2] = tc_
                tc_ += 1
                t_av[NJ - 2] = tc_
                tc_ += 1
                t_av[NJ - 1] = tc_

                def _tj(t, si=si, jd=jd, bdiag=bdiag):
                    def emit_av(j):
                        t.wait_ge(b.sems["V"], v_evict[j])
                        if s_evict[j] is not None:
                            t.wait_ge(b.sems["S"], s_evict[j])
                        bi = j // 4
                        if j % 4 == 0 and bi >= 2:
                            t.wait_ge(b.sems["V"], v_badd[bi - 2])
                        po = (j % 2) * 2 * SW
                        for mi in range(2):
                            ps = ((psYA, psYC) if bi % 2 == 0 else (psYB, psYD))[mi]
                            inst = t.matmul(ps[0:64, 0:SW], VT[:, j * 64:(j + 1) * 64],
                                            PTILE[:, po + mi * SW:po + (mi + 1) * SW],
                                            start=(j % 4 == 0), stop=False,
                                            skip_group_check=True)
                        b.inc(inst, "T", 1)
                        assert b.cnt["T"] == t_av[j]
                    for j in range(NJ):
                        bank = (psS1, psS2, psG1)[j % 3]
                        if j == 2:
                            t.wait_ge(b.sems["V"], v0c)
                        if j >= 3:
                            if s_evict[j - 3] is not None:
                                t.wait_ge(b.sems["S"], s_evict[j - 3])
                            else:
                                t.wait_ge(b.sems["V"], v_evict[j - 3])
                        b.inc(t.matmul(bank[:, 0:SW], KH[0:64, j * 128:(j + 1) * 128],
                                       RH[0:64, si * SW:si * SW + SW],
                                       start=True, stop=True), "T", 1)
                        assert b.cnt["T"] == t_smm[j]
                        if j >= 2:
                            emit_av(j - 2)
                    emit_av(NJ - 2)
                    emit_av(NJ - 1)
                b.on("tensor", _tj, bar=False)

                def _vj(v, jd=jd, bdiag=bdiag):
                    for j in range(NJ):
                        bank = (psS1, psS2, psG1)[j % 3]
                        po = (j % 2) * 2 * SW
                        if diag(j) or j % 2 == 0:
                            v.wait_ge(b.sems["T"], t_smm[j])
                            if j >= 2:
                                v.wait_ge(b.sems["T"], t_av[j - 2])
                        if not diag(j) and j % 2 == 1:
                            pass
                        elif not diag(j):
                            ci = 0 if j < jd else 1
                            for mi in range(2):
                                b.inc(v.tensor_scalar(
                                    PTILE[:, po + mi * SW:po + (mi + 1) * SW],
                                    bank[:, 0:SW],
                                    SCL[:, (2 * mi + ci) * 64 + j:
                                        (2 * mi + ci) * 64 + j + 1],
                                    None, ALU.mult), "V", 1)
                        else:
                            for mi, (ctb, cm) in enumerate(((CTBF, CMAT), (CTBS, CMATS))):
                                b.inc(v.tensor_scalar(
                                    MKT[:, mi * SW:(mi + 1) * SW], ctb[:, 0:SW],
                                    cm[:, j:j + 1], None, ALU.subtract), "V", 1)
                            v.wait_ge(b.sems["S"], s_diag1[j])
                            for mi, ctb in enumerate((CTBF, CTBS)):
                                b.inc(v.tensor_tensor(MKT[:, mi * SW:(mi + 1) * SW],
                                                      MKT[:, mi * SW:(mi + 1) * SW],
                                                      ctb[:, 0:SW], ALU.add), "V", 1)
                            v.wait_ge(b.sems["S"], s_diag2[j])
                            for mi in range(2):
                                b.inc(v.tensor_tensor(
                                    PTILE[:, po + mi * SW:po + (mi + 1) * SW],
                                    bank[:, 0:SW],
                                    MKT[:, mi * SW:(mi + 1) * SW], ALU.mult), "V", 1)

                        if j % 4 == 3:
                            bi = j // 4
                            v.wait_ge(b.sems["T"], t_av[j])
                            if bi < 4:
                                lo = bi <= bdiag
                                for mi in range(2):
                                    ps = ((psYA, psYC) if bi % 2 == 0
                                          else (psYB, psYD))[mi]
                                    seg = (0 if lo else 2) + mi
                                    dst = YACC[0:64, seg * SW:(seg + 1) * SW]
                                    b.inc(v.tensor_tensor(dst, dst, ps[0:64, 0:SW],
                                                          ALU.add), "V", 1)
                            else:
                                d = bi // 4
                                for mi in range(2):
                                    ps = ((psYA, psYC) if bi % 2 == 0
                                          else (psYB, psYD))[mi]
                                    for side in range(2):
                                        dst = YACC[0:64, (2 * side + mi) * SW:
                                                   (2 * side + mi + 1) * SW]
                                        b.inc(v.scalar_tensor_tensor(
                                            dst, ps[0:64, 0:SW],
                                            SLOBB[0:64, 4 * side + d:
                                                  4 * side + d + 1],
                                            dst, ALU.mult, ALU.add), "V", 1)
                            assert b.cnt["V"] == v_badd[bi]
                            if dbg and h == 0 and si == 0 and j == 11:
                                v.tensor_copy(DBGPT[0:64, 0:SW], psYA[0:64, 0:SW])
                            if dbg and h == 0 and si == 0 and j == 15:
                                v.tensor_copy(DBGPT[0:64, SW:2 * SW],
                                              YACC[0:64, 2 * SW:3 * SW])
                b.on("vector", _vj, bar=False)

                def _sj(e, jd=jd):
                    for j in range(NJ):
                        if not diag(j) and j % 2 == 1:
                            bank = (psS1, psS2, psG1)[j % 3]
                            po = (j % 2) * 2 * SW
                            ci = 0 if j < jd else 1
                            e.wait_ge(b.sems["T"], t_smm[j])
                            if j >= 2:
                                e.wait_ge(b.sems["T"], t_av[j - 2])
                            for mi in range(2):
                                b.inc(e.activation(
                                    PTILE[:, po + mi * SW:po + (mi + 1) * SW],
                                    bank[:, 0:SW], AF.Copy,
                                    scale=SCL[:, (2 * mi + ci) * 64 + j:
                                              (2 * mi + ci) * 64 + j + 1]), "S", 1)
                            assert b.cnt["S"] == s_evict[j]
                            continue
                        if diag(j):
                            e.wait_ge(b.sems["V"], v_diag1[j])
                            for mi in range(2):
                                b.inc(e.activation(MKT[:, mi * SW:(mi + 1) * SW],
                                                   MKT[:, mi * SW:(mi + 1) * SW],
                                                   AF.Abs), "S", 1)
                            e.wait_ge(b.sems["V"], v_diag15[j])
                            for mi in range(2):
                                b.inc(e.activation(MKT[:, mi * SW:(mi + 1) * SW],
                                                   MKT[:, mi * SW:(mi + 1) * SW],
                                                   AF.Exp, scale=-1.0,
                                                   bias=CQB[:, mi:mi + 1]), "S", 1)

                b.on("scalar", _sj, bar=False)

                up_empty = bdiag == NB - 1

                def _vy(v, h=h, si=si, up_empty=up_empty):
                    v.wait_ge(b.sems["V"], b.cnt["V"])
                    for mi in range(2):
                        dst = T64[0:64, mi * SW:(mi + 1) * SW]
                        b.inc(v.tensor_tensor(dst, YACC[0:64, mi * SW:(mi + 1) * SW],
                                              RVB[0:64, (2 * mi) * SW:(2 * mi) * SW + SW],
                                              ALU.mult), "V", 1)
                        if not up_empty:
                            b.inc(v.tensor_tensor(YT[0:64, 0:SW],
                                                  YACC[0:64, (2 + mi) * SW:(3 + mi) * SW],
                                                  RVB[0:64, (2 * mi + 1) * SW:
                                                      (2 * mi + 1) * SW + SW],
                                                  ALU.mult), "V", 1)
                            v.wait_ge(b.sems["V"], b.cnt["V"])
                            b.inc(v.tensor_tensor(dst, dst, YT[0:64, 0:SW], ALU.add),
                                  "V", 1)
                            v.wait_ge(b.sems["V"], b.cnt["V"])
                    v.wait_ge(b.sems["V"], b.cnt["V"])
                    b.inc(v.tensor_tensor(YT[0:64, 0:SW], T64[0:64, 0:SW],
                                          T64[0:64, SW:2 * SW], ALU.subtract), "V", 1)
                    v.wait_ge(b.sems["V"], b.cnt["V"])
                    b.inc(v.scalar_tensor_tensor(YT[0:64, 0:SW], YT[0:64, 0:SW],
                                                 ALPH[0:64, h:h + 1],
                                                 T64[0:64, SW:2 * SW],
                                                 ALU.mult, ALU.add), "V", 1)
                    v.wait_ge(b.sems["V"], b.cnt["V"])
                gnow = b.cnt["GD"]

                def _ywait(v, gnow=gnow):
                    if gnow:
                        v.wait_ge(b.sems["GD"], gnow)
                b.on("vector", _ywait, bar=False)
                b.on("vector", _vy, bar=False)
                vye = b.cnt["V"]

                def _ydma(e, vye=vye, h=h, si=si):
                    e.wait_ge(b.sems["V"], vye)
                    b.inc(e.dma_start(out=o_y[h * 64:(h + 1) * 64,
                                              si * SW:(si + 1) * SW],
                                      in_=YT[0:64, 0:SW]), "GD", 16)
                b.on("gpsimd", _ydma, bar=False)
                if dbg and h == 0 and si == 0:
                    b.dma("gpsimd", [(d_dbgB[0:64, 2048:2048 + 2048],
                                      YACC[0:64, 0:2048])], bar=True)
                if dbg and h == 0 and si == 0:
                    def _vdbg(v):
                        for mi, ps in enumerate((psYA, psYB, psYC, psYD)):
                            b.inc(v.tensor_copy(
                                RVB[0:64, mi * SW:(mi + 1) * SW] if False else
                                T64[0:64, 0:SW], ps[0:64, 0:SW]), "V", 1)
                            b.inc(v.tensor_copy(
                                MKT[0:64, 0:SW] if False else
                                RVB[0:64, mi * SW:(mi + 1) * SW], T64[0:64, 0:SW]),
                                "V", 1)
                    b.on("vector", _vdbg)
                    b.dma("gpsimd", [
                        (d_dbgA[:, 0:64], CMAT[:, 0:64]),
                        (d_dbgA[:, 64:128], CMATS[:, 0:64]),
                        (d_dbgA[:, 128:384], SCL[:, 0:256]),
                        (d_dbgA[:, 384:386], CQB[:, 0:2]),
                        (d_dbgA[:, 386:898], CTBF[:, 0:SW]),
                        (d_dbgB[0:64, 0:2048], RVB[0:64, 0:2048]),
                        (d_dbgB[0:64, 2048:2048 + 512],
                         PTILE[:, 0:SW].bitcast(F32) if False else RVB[0:64, 0:512]),
                        (d_dbgC[:], CH[:]),
                        (d_dbgA[:, 898:1022], DBGPT[:, 0:124]) if False else
                        (d_dbgB[0:64, 0:1024], DBGPT[0:64, 0:1024]),
                        ], bar=True)

        fin = b.cnt["GD"]

        def _fin(e, fin=fin):
            e.wait_ge(b.sems["GD"], fin)
        b.on("gpsimd", _fin, bar=False)

        psctx.close()
        if post:
            b.off = markA
            emit_l1post(nc, b, ctx, o_y, d_ident)
    return nc


